# version 05
# baseline (speedup 1.0000x reference)
import os
import numpy as np
import ml_dtypes
from contextlib import ExitStack
import concourse.bass as bass
import concourse.mybir as mybir
from concourse.bass_utils import run_bass_kernel_spmd

F32 = mybir.dt.float32
BF16 = mybir.dt.bfloat16
AX = mybir.AxisListType.X
OP = mybir.AluOpType
AF = mybir.ActivationFunctionType

DM = 1024
NC_K = DM // 128
NE = 6
SUB = 512


class Plan:

    ENGINES = ("t", "v", "s", "y", "g")
    NSLOT = 8

    def __init__(self):
        self.steps = {e: [] for e in self.ENGINES}
        self.last_write = {}
        self.readers = {}
        self.dma_count = {e: 0 for e in self.ENGINES}
        self.pin_count = 0
        self.seq = 0

    def emit(self, eng, fn, reads=(), writes=(), dma=False, tag="", extra_deps=(),
             pin=False):
        idx = len(self.steps[eng])
        deps = set(extra_deps)
        for k in reads:
            w = self.last_write.get(k)
            if w is not None:
                deps.add(w)
        for k in writes:
            w = self.last_write.get(k)
            if w is not None:
                deps.add(w)
            for r in self.readers.get(k, ()):
                deps.add(r)
        deps = {(de, di) for (de, di) in deps if not (de == eng and di >= idx)}
        if eng == "t":
            deps = {(de, di) for (de, di) in deps if de != "t"}
        st = {"fn": fn, "deps": sorted(deps), "dma": dma, "tag": tag,
              "seq": self.seq}
        if dma:
            if pin:
                st["dslot"] = ("pin", self.pin_count)
                self.pin_count += 1
            else:
                st["dslot"] = self.dma_count[eng] % self.NSLOT
                self.dma_count[eng] += 1
        self.seq += 1
        self.steps[eng].append(st)
        ev = (eng, idx)
        for k in reads:
            self.readers.setdefault(k, []).append(ev)
        for k in writes:
            self.last_write[k] = ev
            self.readers[k] = []
        return ev

    def barrier(self):
        lasts = []
        for e in self.ENGINES:
            if not self.steps[e]:
                continue
            lasts.append((e, len(self.steps[e]) - 1))
            seen = set()
            for i in range(len(self.steps[e]) - 1, -1, -1):
                st = self.steps[e][i]
                if st["dma"] and st["dslot"] not in seen:
                    seen.add(st["dslot"])
                    lasts.append((e, i))

        def mknop():
            def f(obj):
                return obj.nop()
            return f
        for e in self.ENGINES:
            deps = [d for d in lasts if d[0] != e] +                    [d for d in lasts if d[0] == e and self.steps[e][d[1]]["dma"]]
            deps = [d for d in deps if not (d[0] == e and d[1] >= len(self.steps[e]))]
            self.steps[e].append({"fn": mknop(), "deps": sorted(set(deps)),
                                  "dma": False, "tag": "barrier",
                                  "seq": self.seq})
            self.seq += 1

    def queue_flush(self, eng):
        deps = []
        seen = set()
        for i in range(len(self.steps[eng]) - 1, -1, -1):
            st = self.steps[eng][i]
            if st["dma"] and st["dslot"] not in seen:
                seen.add(st["dslot"])
                deps.append((eng, i))

        def f(obj):
            return obj.nop()
        idx = len(self.steps[eng])
        self.steps[eng].append({"fn": f, "deps": sorted(deps), "dma": False,
                                "tag": f"flush_{eng}", "seq": self.seq})
        self.seq += 1
        return (eng, idx)

    def final_wait(self, eng, deps):
        def f(obj):
            return obj.nop()
        self.steps[eng].append({"fn": f, "deps": sorted(set(deps)),
                                "dma": False, "tag": "final", "seq": self.seq})
        self.seq += 1

    def finalize(self, nc, es, block):
        kseq = int(os.environ.get("KSEQ", "0"))
        if kseq:
            for e in self.ENGINES:
                self.steps[e] = [st for st in self.steps[e]
                                 if st["seq"] < kseq]
            self.last_write = {k: v for k, v in self.last_write.items()
                               if v[1] < len(self.steps[v[0]])}
            self.readers = {}
            self.barrier()
        consumers = {}
        for e in self.ENGINES:
            for i, st in enumerate(self.steps[e]):
                if st["tag"] in ("barrier", "final") or                         st["tag"].startswith("flush"):
                    continue
                for d in st["deps"]:
                    consumers.setdefault(d, []).append((e, i))
        for e in self.ENGINES:
            prev_by_slot = {}
            for i, st in enumerate(self.steps[e]):
                if not st["dma"] or isinstance(st["dslot"], tuple):
                    continue
                sl = st["dslot"]
                pv = prev_by_slot.get(sl)
                if pv is not None:
                    add = [pv] + consumers.get(pv, [])
                    myseq = st["seq"]
                    extra = []
                    for (de, di) in add:
                        if de == e and di >= i:
                            continue
                        dseq = self.steps[de][di]["seq"]
                        assert dseq < myseq, (
                            "slot reuse forward dep", e, i, st["tag"],
                            de, di, self.steps[de][di]["tag"])
                        extra.append((de, di))
                    st["deps"] = sorted(set(st["deps"]) | set(extra))
                prev_by_slot[sl] = (e, i)
        need_mark = {e: set() for e in self.ENGINES}
        for e in self.ENGINES:
            for st in self.steps[e]:
                for (de, di) in st["deps"]:
                    need_mark[de].add(di)
        sems = {e: es.enter_context(nc.semaphore(f"sem_{e}"))
                for e in self.ENGINES}
        dsems = {}
        for e in self.ENGINES:
            if self.dma_count[e]:
                n = min(self.NSLOT, self.dma_count[e])
                for sl in range(n):
                    dsems[(e, sl)] = es.enter_context(
                        nc.semaphore(f"dsem_{e}{sl}"))
            for i, st in enumerate(self.steps[e]):
                if st.get("dslot") is not None and isinstance(st["dslot"], tuple):
                    dsems[(e, st["dslot"])] = es.enter_context(
                        nc.semaphore(f"dsem_{e}p{st['dslot'][1]}"))
        ev_sem = {}
        slot_counts = {}
        for e in self.ENGINES:
            mcnt = [0, 0]
            nmark = 0
            for i, st in enumerate(self.steps[e]):
                if st["dma"]:
                    sl = st["dslot"]
                    c = slot_counts.get((e, sl), 0) + 16
                    slot_counts[(e, sl)] = c
                    st["inc_sem"] = dsems[(e, sl)]
                    st["inc"] = 16
                    ev_sem[(e, i)] = (st["inc_sem"], c)
                elif i in need_mark[e]:
                    nmark += 1
                    st["inc_sem"] = sems[e]
                    st["inc"] = 1
                    st["chain"] = (sems[e], nmark - 1)
                    ev_sem[(e, i)] = (sems[e], nmark)
                else:
                    st["inc"] = 0
                    ev_sem[(e, i)] = None

        def emit_stream(eng, obj):
            water = {}
            for i, st in enumerate(self.steps[eng]):
                waits = {}
                for d in st["deps"]:
                    sv = ev_sem[d]
                    assert sv is not None, (eng, i, d, st["tag"])
                    sem, v = sv
                    if v > waits.get(id(sem), (None, -1))[1]:
                        waits[id(sem)] = (sem, v)
                ch = st.get("chain")
                if ch is not None and ch[1] > 0:
                    if ch[1] > waits.get(id(ch[0]), (None, -1))[1]:
                        waits[id(ch[0])] = ch
                for sem, v in waits.values():
                    if water.get(id(sem), -1) >= v:
                        continue
                    obj.wait_ge(sem, v)
                    water[id(sem)] = v
                ins = st["fn"](obj)
                if st["inc"]:
                    ins.then_inc(st["inc_sem"], st["inc"])

        @block.tensor
        def _(t):
            emit_stream("t", t)

        @block.vector
        def _(v):
            emit_stream("v", v)

        @block.scalar
        def _(s):
            emit_stream("s", s)

        @block.sync
        def _(y):
            emit_stream("y", y)

        @block.gpsimd
        def _(g):
            emit_stream("g", g)


def build_program(S=4096):
    PH = os.environ.get("KPHASES", "ABCD")
    KT = int(os.environ.get("KTILES", "0")) or 10**9
    NT = S // 128
    NSUB = S // SUB
    nc = bass.Bass("TRN2", target_bir_lowering=False)

    x_in = nc.dram_tensor("x", [S, DM], F32, kind="ExternalInput").ap()
    wq_in = nc.dram_tensor("wq", [DM, DM], F32, kind="ExternalInput").ap()
    wk_in = nc.dram_tensor("wk", [DM, DM], F32, kind="ExternalInput").ap()
    wv_in = nc.dram_tensor("wv", [DM, DM], F32, kind="ExternalInput").ap()
    wr_in = nc.dram_tensor("wr", [DM, DM], F32, kind="ExternalInput").ap()
    gwt_in = nc.dram_tensor("gwt", [DM, 8], F32, kind="ExternalInput").ap()
    gb_in = nc.dram_tensor("gb", [1, 8], F32, kind="ExternalInput").ap()
    ew1t_in = nc.dram_tensor("ew1t", [NE, DM, DM], F32, kind="ExternalInput").ap()
    ew3t_in = nc.dram_tensor("ew3t", [NE, DM, DM], F32, kind="ExternalInput").ap()
    ew2t_in = nc.dram_tensor("ew2t", [NE, DM, DM], F32, kind="ExternalInput").ap()
    fc1t_in = nc.dram_tensor("fc1t", [DM, DM], F32, kind="ExternalInput").ap()
    fc2t_in = nc.dram_tensor("fc2t", [DM, DM], F32, kind="ExternalInput").ap()
    fc3t_in = nc.dram_tensor("fc3t", [DM, DM], F32, kind="ExternalInput").ap()
    ident_in = nc.dram_tensor("ident", [128, 128], BF16, kind="ExternalInput").ap()
    out_hbm = nc.dram_tensor("out", [S, DM], F32, kind="ExternalOutput").ap()
    qx_hbm = nc.dram_tensor("qx_cache", [S, DM], F32, kind="Internal").ap()
    x1_hbm = nc.dram_tensor("x1_cache", [S, DM], F32, kind="Internal").ap()
    ln2t_hbm = nc.dram_tensor("ln2t_cache", [NC_K, 128, S], BF16, kind="Internal").ap()
    ln2t_hbm_v = ln2t_hbm.rearrange("c p t -> p c t")

    es = ExitStack()
    es_abc = ExitStack()
    with es:
        sb = lambda n, shp, dt: es.enter_context(nc.sbuf_tensor(n, shp, dt))
        sba = lambda n, shp, dt: es_abc.enter_context(nc.sbuf_tensor(n, shp, dt))
        psa = lambda n, shp, dt: es.enter_context(nc.psum_tensor(n, shp, dt))
        sb2 = lambda n, shp, dt: [sb(n + "0", shp, dt), sb(n + "1", shp, dt)]
        sba2 = lambda n, shp, dt: [sba(n + "0", shp, dt), sba(n + "1", shp, dt)]

        ident = sb("ident_sb", [128, 128], BF16)
        ones_col = sb("ones_col", [128, 1], BF16)
        ones_row = sb("ones_row", [1, 128], BF16)
        bias_b = sb("bias_b", [128, 8], F32)
        gw_bf = sb("gw_bf", [128, NC_K * 8], BF16)
        probs_all = sb("probs_all", [128, NT * 8], F32)
        ge_all = sb("ge_all", [128, NT * 8], F32)
        mx8a = sb2("mx8a", [128, 8], F32)
        mx8b = sb2("mx8b", [128, 8], F32)
        p2buf = sb2("p2buf", [128, 8], F32)
        p3buf = sb2("p3buf", [128, 8], F32)
        fmask = sb2("fmask", [128, 8], F32)
        smask = sb2("smask", [128, 8], F32)
        gcol = sb2("gcol", [128, 4], F32)
        ptmp = sb2("ptmp", [128, 8], F32)
        etmp = sb2("etmp", [128, 8], F32)
        stats = sb2("stats", [128, 12], F32)
        mv = sb2("mv", [128, 2], F32)
        rstd = sb2("rstd", [128, 2], F32)
        nbias = sb2("nbias", [128, 1], F32)
        mcol = sb2("mcol", [128, 4], F32)
        zcol = sb2("zcol", [128, 4], F32)
        gk_f = sb("gk_f", [128, 8], F32)
        ceps = sb("ceps", [128, 1], F32)
        gb_row = sb("gb_row", [1, 8], F32)
        gb_row_bf = sb("gb_row_bf", [1, 8], BF16)

        P = [psa(f"P{i}", [128, 512], F32) for i in range(6)]
        ptb = psa("ptb", [128, 1024], BF16)
        pg = psa("pg", [128, 8], F32)

        block = es.enter_context(nc.Block())

        pl = Plan()
        E = pl.emit

        ln1T = sba("ln1T", [128, NC_K * S], BF16)
        wA = [sba(f"wA{i}", [128, NC_K * DM], BF16) for i in range(3)]
        xb = [sba(f"xb{i}", [128, DM], F32) for i in range(3)]
        qxst = sba2("qxst", [128, DM], F32)
        ln1_bf = sba2("ln1_bf", [128, DM], BF16)
        expbuf = sba2("expbuf", [128, DM], F32)
        pbuf = sba2("pbuf", [128, DM], F32)
        u_slab = sba2("u_slab", [128, NC_K * SUB], BF16)
        x1t = sba2("x1t", [128, DM], F32)
        ln2_bf = sba2("ln2_bf", [128, DM], BF16)
        ln2st = sba("ln2st", [128, NC_K * SUB], BF16)
        gq_acc = sba("gq_acc", [128, DM], F32)
        gk_acc = sba("gk_acc", [128, DM], F32)
        acc_bf = sba("acc_bf", [128, DM], BF16)
        gq_row_bf = sba("gq_row_bf", [1, DM], BF16)
        gq_bcast = sba("gq_bcast", [128, DM], BF16)

        ln1T_v = ln1T[:].rearrange("p (c t) -> p c t", c=NC_K)
        wA_v = [w[:].rearrange("p (c f) -> p c f", c=NC_K) for w in wA]
        u_v = [u[:].rearrange("p (c t) -> p c t", c=NC_K) for u in u_slab]
        ln2st_v = ln2st[:].rearrange("p (c t) -> p c t", c=NC_K)
        gw_v = gw_bf[:].rearrange("p (c f) -> p c f", c=NC_K)
        ptb_v = ptb[:].rearrange("p (c t) -> p c t", c=NC_K)

        E("y", lambda o: o.dma_start(out=ident[:], in_=ident_in),
          writes=["ident"], dma=True, tag="ld_ident", pin=True)
        E("v", lambda o: o.memset(ones_col[:], 1.0), writes=["ones_col"])
        E("v", lambda o: o.memset(ones_row[:], 1.0), writes=["ones_row"])
        E("v", lambda o: o.memset(gq_acc[:], 0.0), writes=["gq_acc"])
        E("v", lambda o: o.memset(gk_acc[:], 0.0), writes=["gk_acc"])
        E("v", lambda o: o.memset(mx8a[0][:], -3e30), writes=["mx8a0"])
        E("v", lambda o: o.memset(mx8a[1][:], -3e30), writes=["mx8a1"])
        E("v", lambda o: o.memset(mx8b[0][:], -3e30), writes=["mx8b0"])
        E("v", lambda o: o.memset(mx8b[1][:], -3e30), writes=["mx8b1"])
        E("v", lambda o: o.memset(probs_all[:], -2e30), writes=["probs_all"])
        E("v", lambda o: o.memset(ceps[:], 1e-5), writes=["ceps"])
        E("g", lambda o: o.dma_start(
            out=gw_v[:, :, 0:8], in_=gwt_in.rearrange("(c p) f -> p c f", p=128)),
          writes=["gw_bf"], dma=True, tag="ld_gw", pin=True)
        E("y", lambda o: o.dma_start(out=gb_row[0:1, :], in_=gb_in),
          writes=["bias_row"], dma=True, tag="ld_gb", pin=True)
        E("v", lambda o: o.tensor_copy(out=gb_row_bf[:], in_=gb_row[:]),
          reads=["bias_row"], writes=["gb_row_bf"], tag="gb_cast")
        E("t", lambda o: o.matmul(pg[:, 0:8], ones_row[:], gb_row_bf[0:1, :],
                                  start=True, stop=True),
          reads=["gb_row_bf", "ones_row"], writes=["pg"], tag="gb_mm")
        E("v", lambda o: o.tensor_copy(out=bias_b[:], in_=pg[:, 0:8]),
          reads=["pg"], writes=["bias_b"], tag="gb_ev")

        def wload(bufkey, dst, src, tag):
            E("g", lambda o, d=dst, s_=src: o.dma_start(
                out=d, in_=s_.rearrange("(c p) f -> p c f", p=128)),
              writes=[bufkey], dma=True, tag=tag)

        wload("wA0", wA[0][:], wq_in, "ld_wq")
        wload("wA1", wA[1][:], wk_in, "ld_wk")

        def layernorm(par, src_buf, src_key, dst_buf, dst_key, tg):
            st_, mv_, rs_, nb_ = stats[par], mv[par], rstd[par], nbias[par]
            E("v", lambda o: o.bn_stats(out=st_[:, 0:6], in_=src_buf[:, 0:512]),
              reads=[src_key], writes=[f"stats{par}a"], tag=f"{tg}_bnsa")
            E("v", lambda o: o.bn_stats(out=st_[:, 6:12], in_=src_buf[:, 512:1024]),
              reads=[src_key], writes=[f"stats{par}b"], tag=f"{tg}_bnsb")
            E("v", lambda o: o.bn_aggr(out=mv_[:], in_=st_[:]),
              reads=[f"stats{par}a", f"stats{par}b"], writes=[f"mv{par}"],
              tag=f"{tg}_bna")
            E("s", lambda o: o.activation(out=rs_[:, 0:1], in_=mv_[:, 1:2],
                                          func=AF.Ln, bias=ceps[:], scale=1.0),
              reads=[f"mv{par}", "ceps"], writes=[f"rstd{par}0"], tag=f"{tg}_ln_")
            E("s", lambda o: o.activation(out=rs_[:, 1:2], in_=rs_[:, 0:1],
                                          func=AF.Exp, bias=0.0, scale=-0.5),
              reads=[f"rstd{par}0"], writes=[f"rstd{par}1"], tag=f"{tg}_rex")
            E("v", lambda o: o.tensor_scalar(out=nb_[:], in0=mv_[:, 0:1],
                                             scalar1=rs_[:, 1:2], scalar2=-1.0,
                                             op0=OP.mult, op1=OP.mult),
              reads=[f"mv{par}", f"rstd{par}1"], writes=[f"nbias{par}"],
              tag=f"{tg}_nb")
            E("s", lambda o: o.activation(out=dst_buf[:], in_=src_buf[:],
                                          func=AF.Identity, bias=nb_[:],
                                          scale=rs_[:, 1:2]),
              reads=[src_key, f"nbias{par}", f"rstd{par}1"], writes=[dst_key],
              tag=f"{tg}_ln")

        NTA = min(KT, NT) if 'A' in PH else 0

        def A_S1(i):
            par = i % 2
            xk = f"xb{i % 3}"
            lnk = f"ln1_bf{par}"
            E("y", lambda o, i=i: o.dma_start(
                out=xb[i % 3][:], in_=x_in[i*128:(i+1)*128, :]),
              writes=[xk], dma=True, tag=f"A_ldx{i}")
            layernorm(par, xb[i % 3], xk, ln1_bf[par], lnk, f"A{i}")
            for c in range(NC_K):
                E("t", lambda o, c=c, par=par: o.transpose(
                    ptb[:, c*128:(c+1)*128], ln1_bf[par][:, c*128:(c+1)*128],
                    ident[:]),
                  reads=[lnk, "ident"], writes=["ptb"], tag=f"A_tr{i}_{c}")
            E("v", lambda o, i=i: o.tensor_copy(
                out=ln1T_v[:, :, i*128:(i+1)*128], in_=ptb_v),
              reads=["ptb"], writes=[f"ln1T{i}"], tag=f"A_ev{i}")

        def A_S2(i):
            par = i % 2
            xk = f"xb{i % 3}"
            qk = f"qxst{par}"
            pslot = par * 2
            mc, zc = mcol[par], zcol[par]
            for h in range(2):
                bnk, bk = P[pslot + h], f"P{pslot + h}"
                for c in range(NC_K):
                    E("t", lambda o, h=h, c=c, i=i, bnk=bnk: o.matmul(
                        bnk[:], ln1T_v[:, c, i*128:(i+1)*128],
                        wA_v[0][:, c, h*512:(h+1)*512],
                        start=(c == 0), stop=(c == NC_K - 1)),
                      reads=[f"ln1T{i}", "wA0"], writes=[bk],
                      tag=f"A_q{i}_{h}_{c}")
            E("v", lambda o, b=P[pslot], mc=mc: o.reduce_max(
                out=mc[:, 0:1], in_=b[:], axis=AX),
              reads=[f"P{pslot}"], writes=[f"m0_{par}"], tag=f"A_m0{i}")
            E("v", lambda o, b=P[pslot+1], mc=mc: o.reduce_max(
                out=mc[:, 1:2], in_=b[:], axis=AX),
              reads=[f"P{pslot+1}"], writes=[f"m1_{par}"], tag=f"A_m1{i}")
            E("v", lambda o, mc=mc: o.tensor_scalar(
                out=mc[:, 3:4],
                in0=mc[:, 0:1], scalar1=mc[:, 1:2], scalar2=-1.0 / 32.0,
                op0=OP.max, op1=OP.mult),
              reads=[f"m0_{par}", f"m1_{par}"], writes=[f"negM{par}"],
              tag=f"A_nM{i}")
            for h in range(2):
                bnk, bk = P[pslot + h], f"P{pslot + h}"
                E("s", lambda o, h=h, bnk=bnk, par=par, mc=mc, zc=zc: o.activation(
                    out=expbuf[par][:, h*512:(h+1)*512], in_=bnk[:], func=AF.Exp,
                    bias=mc[:, 3:4], scale=1.0/32.0,
                    accum_out=zc[:, h:h+1]),
                  reads=[bk, f"negM{par}"], writes=[f"exp{par}_{h}", f"z{par}{h}"],
                  tag=f"A_e{i}_{h}")
            E("v", lambda o, zc=zc: o.tensor_tensor(
                out=zc[:, 2:3], in0=zc[:, 0:1], in1=zc[:, 1:2], op=OP.add),
              reads=[f"z{par}0", f"z{par}1"], writes=[f"Z{par}"], tag=f"A_Z{i}")
            E("v", lambda o, zc=zc: o.reciprocal(out=zc[:, 3:4], in_=zc[:, 2:3]),
              reads=[f"Z{par}"], writes=[f"rZ{par}"], tag=f"A_rZ{i}")
            for h in range(2):
                bnk, bk = P[pslot + h], f"P{pslot + h}"
                E("v", lambda o, h=h, bnk=bnk, par=par: o.tensor_tensor(
                    out=expbuf[par][:, h*512:(h+1)*512], in0=bnk[:],
                    in1=expbuf[par][:, h*512:(h+1)*512], op=OP.mult),
                  reads=[bk, f"exp{par}_{h}"], writes=[f"exp{par}_{h}"],
                  tag=f"A_w{i}_{h}")
                E("v", lambda o, h=h, par=par, zc=zc: o.scalar_tensor_tensor(
                    out=gq_acc[:, h*512:(h+1)*512],
                    in0=expbuf[par][:, h*512:(h+1)*512], scalar=zc[:, 3:4],
                    in1=gq_acc[:, h*512:(h+1)*512], op0=OP.mult, op1=OP.add),
                  reads=[f"exp{par}_{h}", f"rZ{par}", "gq_acc"],
                  writes=["gq_acc"], tag=f"A_g{i}_{h}")
                E("v", lambda o, h=h, bnk=bnk, par=par, i=i: o.tensor_tensor(
                    out=qxst[par][:, h*512:(h+1)*512], in0=bnk[:],
                    in1=xb[i % 3][:, h*512:(h+1)*512], op=OP.add),
                  reads=[bk, xk], writes=[qk], tag=f"A_qx{i}_{h}")
            E("s", lambda o, i=i, par=par: o.dma_start(
                out=qx_hbm[i*128:(i+1)*128, :], in_=qxst[par][:]),
              reads=[qk], writes=[f"qxhbm{i}"], dma=True, tag=f"A_stq{i}")

        if NTA:
            A_S1(0)
            for i in range(NTA):
                if i + 1 < NTA:
                    A_S1(i + 1)
                A_S2(i)

        wload("wA2", wA[2][:], wv_in, "ld_wv")
        qx_flush = pl.queue_flush("s")

        E("v", lambda o: o.tensor_copy(out=acc_bf[:], in_=gq_acc[:]),
          reads=["gq_acc"], writes=["acc_bf"], tag="gq_cast")
        for h in range(2):
            E("t", lambda o, h=h: o.matmul(P[4][0:1, :], ones_col[:],
                                           acc_bf[:, h*512:(h+1)*512],
                                           start=True, stop=True),
              reads=["acc_bf", "ones_col"], writes=["P4"], tag=f"gq_mm{h}")
            E("v", lambda o, h=h: o.tensor_copy(
                out=gq_row_bf[0:1, h*512:(h+1)*512], in_=P[4][0:1, :]),
              reads=["P4"], writes=["gq_row_bf"], tag=f"gq_ev{h}")
        for h in range(2):
            E("t", lambda o, h=h: o.matmul(P[4][:], ones_row[:],
                                           gq_row_bf[0:1, h*512:(h+1)*512],
                                           start=True, stop=True),
              reads=["gq_row_bf", "ones_row"], writes=["P4"], tag=f"gqb_mm{h}")
            E("v", lambda o, h=h: o.tensor_copy(
                out=gq_bcast[:, h*512:(h+1)*512], in_=P[4][:]),
              reads=["P4"], writes=["gq_bcast"], tag=f"gqb_ev{h}")

        for i in range(NT if 'B' in PH else 0):
            par = i % 2
            pslot = par * 2
            pb = pbuf[par]
            mc, zc = mcol[par], zcol[par]
            for h in range(2):
                bnk, bk = P[pslot + h], f"P{pslot + h}"
                for c in range(NC_K):
                    E("t", lambda o, h=h, c=c, i=i, bnk=bnk: o.matmul(
                        bnk[:], ln1T_v[:, c, i*128:(i+1)*128],
                        wA_v[1][:, c, h*512:(h+1)*512],
                        start=(c == 0), stop=(c == NC_K - 1)),
                      reads=[f"ln1T{i}", "wA1"], writes=[bk],
                      tag=f"B_k{i}_{h}_{c}")
                E("v", lambda o, h=h, bnk=bnk, pb=pb: o.tensor_tensor(
                    out=pb[:, h*512:(h+1)*512], in0=bnk[:],
                    in1=gq_bcast[:, h*512:(h+1)*512], op=OP.mult),
                  reads=[bk, "gq_bcast"], writes=[f"pb{par}_{h}"],
                  tag=f"B_p{i}_{h}")
            E("v", lambda o, pb=pb, mc=mc: o.reduce_max(
                out=mc[:, 2:3], in_=pb[:], axis=AX),
              reads=[f"pb{par}_0", f"pb{par}_1"], writes=[f"M{par}"],
              tag=f"B_M{i}")
            E("v", lambda o, mc=mc: o.tensor_scalar(
                out=mc[:, 3:4], in0=mc[:, 2:3], scalar1=-1.0/32.0, scalar2=None,
                op0=OP.mult),
              reads=[f"M{par}"], writes=[f"negM{par}"], tag=f"B_nM{i}")
            E("s", lambda o, pb=pb, par=par, mc=mc, zc=zc: o.activation(
                out=expbuf[par][:], in_=pb[:], func=AF.Exp,
                bias=mc[:, 3:4], scale=1.0/32.0, accum_out=zc[:, 2:3]),
              reads=[f"pb{par}_0", f"pb{par}_1", f"negM{par}"],
              writes=[f"exp{par}_0", f"exp{par}_1", f"Z{par}"], tag=f"B_e{i}")
            E("v", lambda o, zc=zc: o.reciprocal(out=zc[:, 3:4], in_=zc[:, 2:3]),
              reads=[f"Z{par}"], writes=[f"rZ{par}"], tag=f"B_rZ{i}")
            E("v", lambda o, pb=pb, par=par: o.tensor_tensor(
                out=expbuf[par][:], in0=pb[:], in1=expbuf[par][:], op=OP.mult),
              reads=[f"pb{par}_0", f"pb{par}_1", f"exp{par}_0", f"exp{par}_1"],
              writes=[f"exp{par}_0", f"exp{par}_1"], tag=f"B_w{i}")
            E("v", lambda o, par=par, zc=zc: o.scalar_tensor_tensor(
                out=gk_acc[:], in0=expbuf[par][:], scalar=zc[:, 3:4],
                in1=gk_acc[:], op0=OP.mult, op1=OP.add),
              reads=[f"exp{par}_0", f"exp{par}_1", f"rZ{par}", "gk_acc"],
              writes=["gk_acc"], tag=f"B_g{i}")

        wload("wA0", wA[0][:], wr_in, "ld_wr")

        E("v", lambda o: o.tensor_copy(out=acc_bf[:], in_=gk_acc[:]),
          reads=["gk_acc"], writes=["acc_bf"], tag="gk_cast")
        for c in range(NC_K):
            E("t", lambda o, c=c: o.transpose(ptb[:, c*128:(c+1)*128],
                                              acc_bf[:, c*128:(c+1)*128],
                                              ident[:]),
              reads=["acc_bf", "ident"], writes=["ptb"], tag=f"gk_tr{c}")
            E("v", lambda o, c=c: o.reduce_sum(out=gk_f[:, c:c+1],
                                               in_=ptb[:, c*128:(c+1)*128],
                                               axis=AX),
              reads=["ptb"], writes=["gk_f"], tag=f"gk_rs{c}")

        def C_vu(ch):
            t0 = ch * SUB
            cpar = ch % 2
            uv, uk = u_v[cpar], f"u{cpar}"
            for j in range(NC_K):
                bnk, bk = P[j % 2], f"P{j % 2}"
                for c in range(NC_K):
                    E("t", lambda o, j=j, c=c, t0=t0, bnk=bnk: o.matmul(
                        bnk[:], wA_v[2][:, c, j*128:(j+1)*128],
                        ln1T_v[:, c, t0:t0+SUB],
                        start=(c == 0), stop=(c == NC_K - 1)),
                      reads=["wA2", f"ln1T{t0//128}"],
                      writes=[bk], tag=f"C_v{ch}_{j}_{c}")
                E("v", lambda o, j=j, bnk=bnk, uv=uv: o.tensor_scalar(
                    out=uv[:, j, :], in0=bnk[:], scalar1=gk_f[:, j:j+1],
                    scalar2=None, op0=OP.mult),
                  reads=[bk, "gk_f"], writes=[uk], tag=f"C_u{ch}_{j}")

        def C_S1(ti):
            ch, tt = ti // 4, ti % 4
            par = ti % 2
            uv, uk = u_v[ch % 2], f"u{ch % 2}"
            for h in range(2):
                bnk = P[2 + (tt % 2) * 2 + h]
                bk = f"P{2 + (tt % 2) * 2 + h}"
                for j in range(NC_K):
                    E("t", lambda o, j=j, tt=tt, h=h, bnk=bnk, uv=uv: o.matmul(
                        bnk[:], uv[:, j, tt*128:(tt+1)*128],
                        wA_v[0][:, j, h*512:(h+1)*512],
                        start=(j == 0), stop=(j == NC_K - 1)),
                      reads=[uk, "wA0"], writes=[bk],
                      tag=f"C_r{ch}_{tt}_{h}_{j}")
            E("y", lambda o, ti=ti, par=par: o.dma_start(
                out=qxst[par][:], in_=qx_hbm[ti*128:(ti+1)*128, :]),
              writes=[f"qxst{par}"], dma=True, extra_deps=[qx_flush],
              tag=f"C_ldqx{ti}")
            for h in range(2):
                bnk = P[2 + (tt % 2) * 2 + h]
                bk = f"P{2 + (tt % 2) * 2 + h}"
                E("v", lambda o, h=h, par=par, bnk=bnk: o.scalar_tensor_tensor(
                    out=x1t[par][:, h*512:(h+1)*512], in0=bnk[:],
                    scalar=1.0, in1=qxst[par][:, h*512:(h+1)*512],
                    op0=OP.mult, op1=OP.add),
                  reads=[bk, f"qxst{par}"], writes=[f"x1t{par}"],
                  tag=f"C_x1{ti}_{h}")
            E("s", lambda o, ti=ti, par=par: o.dma_start(
                out=x1_hbm[ti*128:(ti+1)*128, :], in_=x1t[par][:]),
              reads=[f"x1t{par}"], writes=[f"x1hbm{ti}"], dma=True,
              tag=f"C_stx1{ti}")
            layernorm(par, x1t[par], f"x1t{par}", ln2_bf[par], f"ln2_bf{par}",
                      f"C{ti}")

        def C_S2(ti):
            ch, tt = ti // 4, ti % 4
            par = ti % 2
            for c in range(NC_K):
                E("t", lambda o, c=c, par=par: o.transpose(
                    ptb[:, c*128:(c+1)*128],
                    ln2_bf[par][:, c*128:(c+1)*128], ident[:]),
                  reads=[f"ln2_bf{par}", "ident"], writes=["ptb"],
                  tag=f"C_tr{ti}_{c}")
            E("v", lambda o, tt=tt: o.tensor_copy(
                out=ln2st_v[:, :, tt*128:(tt+1)*128], in_=ptb_v),
              reads=["ptb"], writes=[f"ln2st{tt}"], tag=f"C_ev{ti}")
            for c in range(NC_K):
                E("t", lambda o, c=c, tt=tt: o.matmul(
                    pg[:, 0:6], ln2st_v[:, c, tt*128:(tt+1)*128],
                    gw_v[:, c, 0:6], start=(c == 0), stop=(c == NC_K - 1)),
                  reads=[f"ln2st{tt}", "gw_bf"], writes=["pg"],
                  tag=f"C_gmm{ti}_{c}")
            E("v", lambda o, par=par: o.tensor_tensor(
                out=ptmp[par][:, 0:6], in0=pg[:, 0:6],
                in1=bias_b[:, 0:6], op=OP.add),
              reads=["pg", "bias_b"], writes=[f"ptmp{par}"], tag=f"C_gb{ti}")
            E("s", lambda o, par=par: o.activation(
                out=etmp[par][:, 0:6], in_=ptmp[par][:, 0:6],
                func=AF.Exp, bias=0.0, scale=-1.0),
              reads=[f"ptmp{par}"], writes=[f"etmp{par}"], tag=f"C_sg1{ti}")
            E("v", lambda o, par=par: o.tensor_scalar(
                out=etmp[par][:, 0:6], in0=etmp[par][:, 0:6], scalar1=1.0,
                scalar2=None, op0=OP.add),
              reads=[f"etmp{par}"], writes=[f"etmp{par}"], tag=f"C_sg2{ti}")
            E("v", lambda o, ti=ti, par=par: o.reciprocal(
                out=probs_all[:, ti*8:ti*8+6], in_=etmp[par][:, 0:6]),
              reads=[f"etmp{par}"], writes=[f"probs{ti}"], tag=f"C_sg3{ti}")
            o8 = ti * 8
            Pt = probs_all[:, o8:o8+8]
            ma, mb = mx8a[par], mx8b[par]
            p2_, p3_ = p2buf[par], p3buf[par]
            fm, sm, gc = fmask[par], smask[par], gcol[par]
            E("v", lambda o, Pt=Pt, ma=ma: o.reduce_max(
                out=ma[:, 0:1], in_=Pt, axis=AX),
              reads=[f"probs{ti}", "probs_all"], writes=[f"mx8a_v{par}"],
              tag=f"G_m1{ti}")
            E("v", lambda o, Pt=Pt, ma=ma, p2_=p2_: o.match_replace(
                out=p2_[:], in_to_replace=ma[:], in_values=Pt,
                imm_value=-2e30),
              reads=[f"probs{ti}", "probs_all", f"mx8a{par}",
                     f"mx8a_v{par}"],
              writes=[f"p2buf{par}"], tag=f"G_mr1{ti}")
            E("v", lambda o, mb=mb, p2_=p2_: o.reduce_max(
                out=mb[:, 0:1], in_=p2_[:], axis=AX),
              reads=[f"p2buf{par}"], writes=[f"mx8b_v{par}"],
              tag=f"G_m2{ti}")
            E("v", lambda o, Pt=Pt, fm=fm, p2_=p2_: o.tensor_tensor(
                out=fm[:], in0=Pt, in1=p2_[:], op=OP.not_equal),
              reads=[f"probs{ti}", "probs_all", f"p2buf{par}"],
              writes=[f"fmask{par}"], tag=f"G_f{ti}")
            E("v", lambda o, mb=mb, p2_=p2_, p3_=p3_: o.match_replace(
                out=p3_[:], in_to_replace=mb[:], in_values=p2_[:],
                imm_value=-2e30),
              reads=[f"p2buf{par}", f"mx8b{par}", f"mx8b_v{par}"],
              writes=[f"p3buf{par}"], tag=f"G_mr2{ti}")
            E("v", lambda o, sm=sm, p2_=p2_, p3_=p3_: o.tensor_tensor(
                out=sm[:], in0=p2_[:], in1=p3_[:], op=OP.not_equal),
              reads=[f"p2buf{par}", f"p3buf{par}"], writes=[f"smask{par}"],
              tag=f"G_s{ti}")
            E("v", lambda o, gc=gc, ma=ma, mb=mb: o.scalar_tensor_tensor(
                out=gc[:, 0:1], in0=ma[:, 0:1], scalar=1e-8, in1=mb[:, 0:1],
                op0=OP.add, op1=OP.add),
              reads=[f"mx8a_v{par}", f"mx8b_v{par}"], writes=[f"den{par}"],
              tag=f"G_d{ti}")
            E("v", lambda o, gc=gc: o.reciprocal(out=gc[:, 2:3],
                                                 in_=gc[:, 0:1]),
              reads=[f"den{par}"], writes=[f"rden{par}"], tag=f"G_rd{ti}")
            E("v", lambda o, gc=gc, ma=ma: o.tensor_scalar(
                out=gc[:, 3:4], in0=ma[:, 0:1], scalar1=gc[:, 2:3],
                scalar2=None, op0=OP.mult),
              reads=[f"mx8a_v{par}", f"rden{par}"], writes=[f"gm{par}"],
              tag=f"G_g1{ti}")
            E("v", lambda o, o8=o8, fm=fm, gc=gc: o.tensor_scalar(
                out=ge_all[:, o8:o8+8], in0=fm[:], scalar1=gc[:, 3:4],
                scalar2=None, op0=OP.mult),
              reads=[f"fmask{par}", f"gm{par}"], writes=[f"ge{ti}"],
              tag=f"G_ge1{ti}")
            E("v", lambda o, gc=gc, mb=mb: o.tensor_scalar(
                out=gc[:, 3:4], in0=mb[:, 0:1], scalar1=gc[:, 2:3],
                scalar2=None, op0=OP.mult),
              reads=[f"mx8b_v{par}", f"rden{par}"], writes=[f"gm{par}"],
              tag=f"G_g2{ti}")
            E("v", lambda o, o8=o8, sm=sm, gc=gc: o.scalar_tensor_tensor(
                out=ge_all[:, o8:o8+8], in0=sm[:], scalar=gc[:, 3:4],
                in1=ge_all[:, o8:o8+8], op0=OP.mult, op1=OP.add),
              reads=[f"smask{par}", f"gm{par}", f"ge{ti}"],
              writes=[f"ge{ti}"], tag=f"G_ge2{ti}")
            if tt == 3:
                E("s", lambda o, ch=ch: o.dma_start(
                    out=ln2t_hbm_v[:, :, ch*SUB:(ch+1)*SUB],
                    in_=ln2st_v[:, :, :]),
                  reads=[f"ln2st{q}" for q in range(4)],
                  writes=[f"ln2thbm{ch}"], dma=True, tag=f"C_stln2{ch}")

        if 'C' in PH:
            pend = None
            for ch in range(NSUB):
                C_vu(ch)
                for tt in range(4):
                    ti = ch * 4 + tt
                    C_S1(ti)
                    if pend is not None:
                        C_S2(pend)
                    pend = ti
            C_S2(pend)

        es_abc.close()
        pl.barrier()

        y_sb = sb("y_sb", [128, NT * DM], BF16)
        wF = [sb(f"wF{i}", [128, NC_K * DM], BF16) for i in range(4)]
        slab = [sb(f"slab{i}", [128, NC_K * SUB], BF16) for i in range(2)]
        h1s = [sb(f"h1s{i}", [128, SUB], F32) for i in range(2)]
        hb = sb("hb", [128, NC_K * SUB], BF16)
        x1ld = [sb(f"x1ld{i}", [128, DM], F32) for i in range(2)]
        outt = [sb(f"outt{i}", [128, DM], F32) for i in range(2)]
        wF_v = [w[:].rearrange("p (c f) -> p c f", c=NC_K) for w in wF]
        slab_v = [s_[:].rearrange("p (c t) -> p c t", c=NC_K) for s_ in slab]
        hb_v = hb[:].rearrange("p (c t) -> p c t", c=NC_K)
        y_v = y_sb[:].rearrange("p (n f) -> p n f", n=NT)

        E("v", lambda o: o.memset(y_sb[:], 0.0), writes=["y_sb"], tag="D_zy")

        branches = [(ew1t_in[e], ew3t_in[e], ew2t_in[e], e) for e in range(NE)]
        branches.append((fc1t_in, fc2t_in, fc3t_in, None))
        if "D" not in PH:
            branches = branches[-1:] if "d" in PH else []

        wslot = {}

        def emit_branch_loads(b):
            w1, w3, w2, _e = branches[b]
            for j, wsrc in enumerate((w1, w3, w2)):
                slot = (3 * b + j) % 4
                wslot[(b, j)] = slot
                E("g", lambda o, slot=slot, s_=wsrc: o.dma_start(
                    out=wF[slot][:], in_=s_.rearrange("(c p) f -> p c f", p=128)),
                  writes=[f"slot{slot}"], dma=True, tag=f"ld_w{b}_{j}")

        if branches:
            emit_branch_loads(0)

        allhb = [f"hb{j}" for j in range(NC_K)]
        for b, (w1, w3, w2, e) in enumerate(branches):
            shared = e is None
            s1, s3, s2 = wslot[(b, 0)], wslot[(b, 1)], wslot[(b, 2)]
            for sb_i in range(NSUB):
                sl, slv = slab[sb_i % 2], slab_v[sb_i % 2]
                slk = f"slab{sb_i % 2}"
                E("y", lambda o, sb_i=sb_i, sl=sl: o.dma_start(
                    out=sl[:], in_=ln2t_hbm_v[:, :, sb_i*SUB:(sb_i+1)*SUB]),
                  writes=[slk], dma=True, tag=f"D_lds{b}_{sb_i}")
                for j in range(NC_K):
                    b1k, b1n = P[j % 2], f"P{j % 2}"
                    b3k, b3n = P[2 + j % 2], f"P{2 + j % 2}"
                    for c in range(NC_K):
                        E("t", lambda o, j=j, c=c, b1k=b1k, slv=slv, s1=s1: o.matmul(
                            b1k[:], wF_v[s1][:, c, j*128:(j+1)*128],
                            slv[:, c, :],
                            start=(c == 0), stop=(c == NC_K - 1)),
                          reads=[f"slot{s1}", slk], writes=[b1n],
                          tag=f"D_h1_{b}_{sb_i}_{j}_{c}")
                    for c in range(NC_K):
                        E("t", lambda o, j=j, c=c, b3k=b3k, slv=slv, s3=s3: o.matmul(
                            b3k[:], wF_v[s3][:, c, j*128:(j+1)*128],
                            slv[:, c, :],
                            start=(c == 0), stop=(c == NC_K - 1)),
                          reads=[f"slot{s3}", slk], writes=[b3n],
                          tag=f"D_h3_{b}_{sb_i}_{j}_{c}")
                    E("s", lambda o, j=j, b1k=b1k: o.activation(
                        out=h1s[j % 2][:], in_=b1k[:], func=AF.Silu),
                      reads=[b1n], writes=[f"h1s{j % 2}"],
                      tag=f"D_si{b}_{sb_i}_{j}")
                    E("v", lambda o, j=j, b3k=b3k: o.tensor_tensor(
                        out=hb_v[:, j, :], in0=h1s[j % 2][:], in1=b3k[:],
                        op=OP.mult),
                      reads=[f"h1s{j % 2}", b3n], writes=[f"hb{j}"],
                      tag=f"D_hb{b}_{sb_i}_{j}")
                for tt in range(4):
                    ti = sb_i * 4 + tt
                    if shared:
                        E("y", lambda o, ti=ti: o.dma_start(
                            out=x1ld[ti % 2][:],
                            in_=x1_hbm[ti*128:(ti+1)*128, :]),
                          writes=[f"x1ld{ti % 2}"],
                          dma=True, tag=f"D_ldx1{ti}")
                    for h in range(2):
                        bk, bn = P[4 + (2*tt + h) % 2], f"P{4 + (2*tt + h) % 2}"
                        for j in range(NC_K):
                            E("t", lambda o, j=j, tt=tt, h=h, bk=bk, s2=s2: o.matmul(
                                bk[:], hb_v[:, j, tt*128:(tt+1)*128],
                                wF_v[s2][:, j, h*512:(h+1)*512],
                                start=(j == 0), stop=(j == NC_K - 1)),
                              reads=allhb + [f"slot{s2}"], writes=[bn],
                              tag=f"D_y{b}_{sb_i}_{tt}_{h}_{j}")
                        if not shared:
                            E("v", lambda o, ti=ti, h=h, bk=bk, e=e: o.scalar_tensor_tensor(
                                out=y_v[:, ti, h*512:(h+1)*512], in0=bk[:],
                                scalar=ge_all[:, ti*8+e:ti*8+e+1],
                                in1=y_v[:, ti, h*512:(h+1)*512],
                                op0=OP.mult, op1=OP.add),
                              reads=[bn, f"ge{ti}", "y_sb"], writes=["y_sb"],
                              tag=f"D_ye{b}_{ti}_{h}")
                        else:
                            E("v", lambda o, ti=ti, h=h, bk=bk: o.scalar_tensor_tensor(
                                out=outt[ti % 2][:, h*512:(h+1)*512], in0=bk[:],
                                scalar=1.0, in1=y_v[:, ti, h*512:(h+1)*512],
                                op0=OP.mult, op1=OP.add),
                              reads=[bn, "y_sb"], writes=[f"outt{ti % 2}"],
                              tag=f"D_yz{ti}_{h}")
                            E("v", lambda o, ti=ti, h=h: o.tensor_tensor(
                                out=outt[ti % 2][:, h*512:(h+1)*512],
                                in0=outt[ti % 2][:, h*512:(h+1)*512],
                                in1=x1ld[ti % 2][:, h*512:(h+1)*512],
                                op=OP.add),
                              reads=[f"outt{ti % 2}", f"x1ld{ti % 2}"],
                              writes=[f"outt{ti % 2}"], tag=f"D_out{ti}_{h}")
                    if shared:
                        E("s", lambda o, ti=ti: o.dma_start(
                            out=out_hbm[ti*128:(ti+1)*128, :],
                            in_=outt[ti % 2][:]),
                          reads=[f"outt{ti % 2}"], writes=[f"outhbm{ti}"],
                          dma=True, tag=f"D_sto{ti}")
                if sb_i == 0 and b + 1 < len(branches):
                    emit_branch_loads(b + 1)

        pl.barrier()
        if not branches:
            E("v", lambda o: o.memset(outt[0][:], 0.0), writes=["outt0"],
              tag="dbg_z")
            for ti in range(NT):
                E("s", lambda o, ti=ti: o.dma_start(
                    out=out_hbm[ti*128:(ti+1)*128, :], in_=outt[0][:]),
                  reads=["outt0"], dma=True, tag=f"dbg_sto{ti}")
            pl.barrier()
        pl.finalize(nc, es, block)
    return nc


_NC_CACHE = {}


def build_cached(S=4096):
    if S not in _NC_CACHE:
        _NC_CACHE[S] = build_program(S)
    return _NC_CACHE[S]


def make_in_maps(L0_attn_x, L0_attn_Wq, L0_attn_Wk, L0_attn_Wv, L0_attn_Wr,
                 L0_ffn_gate_w, L0_ffn_gate_b, L0_ffn_ew1, L0_ffn_ew2,
                 L0_ffn_ew3, L0_ffn_fc1, L0_ffn_fc2, L0_ffn_fc3):
    x = np.ascontiguousarray(np.asarray(L0_attn_x, dtype=np.float32))
    f32 = lambda a: np.ascontiguousarray(np.asarray(a, dtype=np.float32))
    T2 = lambda a: np.ascontiguousarray(np.swapaxes(f32(a), -1, -2))
    gwt = np.zeros((DM, 8), np.float32)
    gwt[:, 0:6] = T2(L0_ffn_gate_w)
    gb = np.zeros((1, 8), np.float32)
    gb[0, 0:6] = f32(L0_ffn_gate_b)
    shared = {"wq": f32(L0_attn_Wq), "wk": f32(L0_attn_Wk),
              "wv": f32(L0_attn_Wv), "wr": f32(L0_attn_Wr),
              "gwt": gwt, "gb": gb,
              "ew1t": T2(L0_ffn_ew1), "ew3t": T2(L0_ffn_ew3),
              "ew2t": T2(L0_ffn_ew2),
              "fc1t": T2(L0_ffn_fc1), "fc2t": T2(L0_ffn_fc2),
              "fc3t": T2(L0_ffn_fc3),
              "ident": np.eye(128, dtype=ml_dtypes.bfloat16)}
    return [dict(shared, x=np.ascontiguousarray(x[b]))
            for b in range(x.shape[0])]


def kernel(**inputs):
    in_maps = make_in_maps(**inputs)
    nc = build_cached(4096)
    res = run_bass_kernel_spmd(nc, in_maps, core_ids=list(range(len(in_maps))))
    return np.stack([r["out"] for r in res.results], axis=0)


# version 06
# speedup vs baseline: 2.2873x; 2.2873x over previous
import os
import numpy as np
import ml_dtypes
from contextlib import ExitStack
import concourse.bass as bass
import concourse.mybir as mybir
from concourse.bass_utils import run_bass_kernel_spmd

F32 = mybir.dt.float32
BF16 = mybir.dt.bfloat16
AX = mybir.AxisListType.X
OP = mybir.AluOpType
AF = mybir.ActivationFunctionType

DM = 1024
NC_K = DM // 128
NE = 6
SUB = 512


class Plan:

    ENGINES = ("t", "v", "s", "y", "g")
    NSLOT = 8

    def __init__(self):
        self.steps = {e: [] for e in self.ENGINES}
        self.last_write = {}
        self.readers = {}
        self.dma_count = {e: 0 for e in self.ENGINES}
        self.pin_count = 0
        self.seq = 0

    def emit(self, eng, fn, reads=(), writes=(), dma=False, tag="", extra_deps=(),
             pin=False):
        idx = len(self.steps[eng])
        deps = set(extra_deps)
        for k in reads:
            w = self.last_write.get(k)
            if w is not None:
                deps.add(w)
        for k in writes:
            w = self.last_write.get(k)
            if w is not None:
                deps.add(w)
            for r in self.readers.get(k, ()):
                deps.add(r)
        deps = {(de, di) for (de, di) in deps if not (de == eng and di >= idx)}
        if eng == "t":
            deps = {(de, di) for (de, di) in deps if de != "t"}
        st = {"fn": fn, "deps": sorted(deps), "dma": dma, "tag": tag,
              "seq": self.seq}
        if dma:
            if pin:
                st["dslot"] = ("pin", self.pin_count)
                self.pin_count += 1
            else:
                st["dslot"] = self.dma_count[eng] % self.NSLOT
                self.dma_count[eng] += 1
        self.seq += 1
        self.steps[eng].append(st)
        ev = (eng, idx)
        for k in reads:
            self.readers.setdefault(k, []).append(ev)
        for k in writes:
            self.last_write[k] = ev
            self.readers[k] = []
        return ev

    def barrier(self):
        lasts = []
        for e in self.ENGINES:
            if not self.steps[e]:
                continue
            lasts.append((e, len(self.steps[e]) - 1))
            seen = set()
            for i in range(len(self.steps[e]) - 1, -1, -1):
                st = self.steps[e][i]
                if st["dma"] and st["dslot"] not in seen:
                    seen.add(st["dslot"])
                    lasts.append((e, i))

        def mknop():
            def f(obj):
                return obj.nop()
            return f
        for e in self.ENGINES:
            deps = [d for d in lasts if d[0] != e] +                    [d for d in lasts if d[0] == e and self.steps[e][d[1]]["dma"]]
            deps = [d for d in deps if not (d[0] == e and d[1] >= len(self.steps[e]))]
            self.steps[e].append({"fn": mknop(), "deps": sorted(set(deps)),
                                  "dma": False, "tag": "barrier",
                                  "seq": self.seq})
            self.seq += 1

    def queue_flush(self, eng):
        deps = []
        seen = set()
        for i in range(len(self.steps[eng]) - 1, -1, -1):
            st = self.steps[eng][i]
            if st["dma"] and st["dslot"] not in seen:
                seen.add(st["dslot"])
                deps.append((eng, i))

        def f(obj):
            return obj.nop()
        idx = len(self.steps[eng])
        self.steps[eng].append({"fn": f, "deps": sorted(deps), "dma": False,
                                "tag": f"flush_{eng}", "seq": self.seq})
        self.seq += 1
        return (eng, idx)

    def final_wait(self, eng, deps):
        def f(obj):
            return obj.nop()
        self.steps[eng].append({"fn": f, "deps": sorted(set(deps)),
                                "dma": False, "tag": "final", "seq": self.seq})
        self.seq += 1

    def finalize(self, nc, es, block):
        kseq = int(os.environ.get("KSEQ", "0"))
        if kseq:
            for e in self.ENGINES:
                self.steps[e] = [st for st in self.steps[e]
                                 if st["seq"] < kseq]
            self.last_write = {k: v for k, v in self.last_write.items()
                               if v[1] < len(self.steps[v[0]])}
            self.readers = {}
            self.barrier()
        consumers = {}
        for e in self.ENGINES:
            for i, st in enumerate(self.steps[e]):
                if st["tag"] in ("barrier", "final") or                         st["tag"].startswith("flush"):
                    continue
                for d in st["deps"]:
                    consumers.setdefault(d, []).append((e, i))
        for e in self.ENGINES:
            prev_by_slot = {}
            for i, st in enumerate(self.steps[e]):
                if not st["dma"] or isinstance(st["dslot"], tuple):
                    continue
                sl = st["dslot"]
                pv = prev_by_slot.get(sl)
                if pv is not None:
                    add = [pv] + consumers.get(pv, [])
                    myseq = st["seq"]
                    extra = []
                    for (de, di) in add:
                        if de == e and di >= i:
                            continue
                        dseq = self.steps[de][di]["seq"]
                        assert dseq < myseq, (
                            "slot reuse forward dep", e, i, st["tag"],
                            de, di, self.steps[de][di]["tag"])
                        extra.append((de, di))
                    st["deps"] = sorted(set(st["deps"]) | set(extra))
                prev_by_slot[sl] = (e, i)
        need_mark = {e: set() for e in self.ENGINES}
        for e in self.ENGINES:
            for st in self.steps[e]:
                for (de, di) in st["deps"]:
                    need_mark[de].add(di)
        sems = {(e, p): es.enter_context(nc.semaphore(f"sem_{e}{p}"))
                for e in self.ENGINES for p in range(2)}
        dsems = {}
        for e in self.ENGINES:
            if self.dma_count[e]:
                n = min(self.NSLOT, self.dma_count[e])
                for sl in range(n):
                    dsems[(e, sl)] = es.enter_context(
                        nc.semaphore(f"dsem_{e}{sl}"))
            for i, st in enumerate(self.steps[e]):
                if st.get("dslot") is not None and isinstance(st["dslot"], tuple):
                    dsems[(e, st["dslot"])] = es.enter_context(
                        nc.semaphore(f"dsem_{e}p{st['dslot'][1]}"))
        ev_sem = {}
        slot_counts = {}
        for e in self.ENGINES:
            mcnt = [0, 0]
            nmark = 0
            for i, st in enumerate(self.steps[e]):
                if st["dma"]:
                    sl = st["dslot"]
                    c = slot_counts.get((e, sl), 0) + 16
                    slot_counts[(e, sl)] = c
                    st["inc_sem"] = dsems[(e, sl)]
                    st["inc"] = 16
                    ev_sem[(e, i)] = (st["inc_sem"], c)
                elif i in need_mark[e]:
                    p = nmark % 2
                    nmark += 1
                    mcnt[p] += 1
                    st["inc_sem"] = sems[(e, p)]
                    st["inc"] = 1
                    st["chain"] = (sems[(e, p)], mcnt[p] - 1)
                    ev_sem[(e, i)] = (sems[(e, p)], mcnt[p])
                else:
                    st["inc"] = 0
                    ev_sem[(e, i)] = None

        def emit_stream(eng, obj):
            water = {}
            for i, st in enumerate(self.steps[eng]):
                waits = {}
                for d in st["deps"]:
                    sv = ev_sem[d]
                    assert sv is not None, (eng, i, d, st["tag"])
                    sem, v = sv
                    if v > waits.get(id(sem), (None, -1))[1]:
                        waits[id(sem)] = (sem, v)
                ch = st.get("chain")
                if ch is not None and ch[1] > 0:
                    if ch[1] > waits.get(id(ch[0]), (None, -1))[1]:
                        waits[id(ch[0])] = ch
                for sem, v in waits.values():
                    if water.get(id(sem), -1) >= v:
                        continue
                    obj.wait_ge(sem, v)
                    water[id(sem)] = v
                ins = st["fn"](obj)
                if st["inc"]:
                    ins.then_inc(st["inc_sem"], st["inc"])

        @block.tensor
        def _(t):
            emit_stream("t", t)

        @block.vector
        def _(v):
            emit_stream("v", v)

        @block.scalar
        def _(s):
            emit_stream("s", s)

        @block.sync
        def _(y):
            emit_stream("y", y)

        @block.gpsimd
        def _(g):
            emit_stream("g", g)


def build_program(S=4096):
    PH = os.environ.get("KPHASES", "ABCD")
    KT = int(os.environ.get("KTILES", "0")) or 10**9
    NT = S // 128
    NSUB = S // SUB
    nc = bass.Bass("TRN2", target_bir_lowering=False)

    x_in = nc.dram_tensor("x", [S, DM], F32, kind="ExternalInput").ap()
    wq_in = nc.dram_tensor("wq", [DM, DM], F32, kind="ExternalInput").ap()
    wk_in = nc.dram_tensor("wk", [DM, DM], F32, kind="ExternalInput").ap()
    wv_in = nc.dram_tensor("wv", [DM, DM], F32, kind="ExternalInput").ap()
    wr_in = nc.dram_tensor("wr", [DM, DM], F32, kind="ExternalInput").ap()
    gwt_in = nc.dram_tensor("gwt", [DM, 8], F32, kind="ExternalInput").ap()
    gb_in = nc.dram_tensor("gb", [1, 8], F32, kind="ExternalInput").ap()
    ew1t_in = nc.dram_tensor("ew1t", [NE, DM, DM], F32, kind="ExternalInput").ap()
    ew3t_in = nc.dram_tensor("ew3t", [NE, DM, DM], F32, kind="ExternalInput").ap()
    ew2t_in = nc.dram_tensor("ew2t", [NE, DM, DM], F32, kind="ExternalInput").ap()
    fc1t_in = nc.dram_tensor("fc1t", [DM, DM], F32, kind="ExternalInput").ap()
    fc2t_in = nc.dram_tensor("fc2t", [DM, DM], F32, kind="ExternalInput").ap()
    fc3t_in = nc.dram_tensor("fc3t", [DM, DM], F32, kind="ExternalInput").ap()
    ident_in = nc.dram_tensor("ident", [128, 128], BF16, kind="ExternalInput").ap()
    out_hbm = nc.dram_tensor("out", [S, DM], F32, kind="ExternalOutput").ap()
    qx_hbm = nc.dram_tensor("qx_cache", [S, DM], F32, kind="Internal").ap()
    x1_hbm = nc.dram_tensor("x1_cache", [S, DM], F32, kind="Internal").ap()
    ln2t_hbm = nc.dram_tensor("ln2t_cache", [NC_K, 128, S], BF16, kind="Internal").ap()
    ln2t_hbm_v = ln2t_hbm.rearrange("c p t -> p c t")

    es = ExitStack()
    es_abc = ExitStack()
    with es:
        sb = lambda n, shp, dt: es.enter_context(nc.sbuf_tensor(n, shp, dt))
        sba = lambda n, shp, dt: es_abc.enter_context(nc.sbuf_tensor(n, shp, dt))
        psa = lambda n, shp, dt: es.enter_context(nc.psum_tensor(n, shp, dt))
        sb2 = lambda n, shp, dt: [sb(n + "0", shp, dt), sb(n + "1", shp, dt)]
        sba2 = lambda n, shp, dt: [sba(n + "0", shp, dt), sba(n + "1", shp, dt)]

        ident = sb("ident_sb", [128, 128], BF16)
        ones_col = sb("ones_col", [128, 1], BF16)
        ones_row = sb("ones_row", [1, 128], BF16)
        bias_b = sb("bias_b", [128, 8], F32)
        gw_bf = sb("gw_bf", [128, NC_K * 8], BF16)
        probs_all = sb("probs_all", [128, NT * 8], F32)
        ge_all = sb("ge_all", [128, NT * 8], F32)
        mx8a = sb2("mx8a", [128, 8], F32)
        mx8b = sb2("mx8b", [128, 8], F32)
        p2buf = sb2("p2buf", [128, 8], F32)
        p3buf = sb2("p3buf", [128, 8], F32)
        fmask = sb2("fmask", [128, 8], F32)
        smask = sb2("smask", [128, 8], F32)
        gcol = sb2("gcol", [128, 4], F32)
        ptmp = sb2("ptmp", [128, 8], F32)
        etmp = sb2("etmp", [128, 8], F32)
        stats = sb2("stats", [128, 12], F32)
        mv = sb2("mv", [128, 2], F32)
        rstd = sb2("rstd", [128, 2], F32)
        nbias = sb2("nbias", [128, 1], F32)
        mcol = sb2("mcol", [128, 4], F32)
        zcol = sb2("zcol", [128, 4], F32)
        gk_f = sb("gk_f", [128, 8], F32)
        ceps = sb("ceps", [128, 1], F32)
        gb_row = sb("gb_row", [1, 8], F32)
        gb_row_bf = sb("gb_row_bf", [1, 8], BF16)

        P = [psa(f"P{i}", [128, 512], F32) for i in range(6)]
        ptb = psa("ptb", [128, 1024], BF16)
        pg = psa("pg", [128, 8], F32)

        block = es.enter_context(nc.Block())

        pl = Plan()
        E = pl.emit

        ln1T = sba("ln1T", [128, NC_K * S], BF16)
        wA = [sba(f"wA{i}", [128, NC_K * DM], BF16) for i in range(3)]
        xb = [sba(f"xb{i}", [128, DM], F32) for i in range(3)]
        qxst = sba2("qxst", [128, DM], F32)
        ln1_bf = sba2("ln1_bf", [128, DM], BF16)
        expbuf = sba2("expbuf", [128, DM], F32)
        pbuf = sba2("pbuf", [128, DM], F32)
        u_slab = sba2("u_slab", [128, NC_K * SUB], BF16)
        x1t = sba2("x1t", [128, DM], F32)
        ln2_bf = sba2("ln2_bf", [128, DM], BF16)
        ln2st = sba("ln2st", [128, NC_K * SUB], BF16)
        gq_acc = sba("gq_acc", [128, DM], F32)
        gk_acc = sba("gk_acc", [128, DM], F32)
        acc_bf = sba("acc_bf", [128, DM], BF16)
        gq_row_bf = sba("gq_row_bf", [1, DM], BF16)
        gq_bcast = sba("gq_bcast", [128, DM], BF16)

        ln1T_v = ln1T[:].rearrange("p (c t) -> p c t", c=NC_K)
        wA_v = [w[:].rearrange("p (c f) -> p c f", c=NC_K) for w in wA]
        u_v = [u[:].rearrange("p (c t) -> p c t", c=NC_K) for u in u_slab]
        ln2st_v = ln2st[:].rearrange("p (c t) -> p c t", c=NC_K)
        gw_v = gw_bf[:].rearrange("p (c f) -> p c f", c=NC_K)
        ptb_v = ptb[:].rearrange("p (c t) -> p c t", c=NC_K)

        E("y", lambda o: o.dma_start(out=ident[:], in_=ident_in),
          writes=["ident"], dma=True, tag="ld_ident", pin=True)
        E("v", lambda o: o.memset(ones_col[:], 1.0), writes=["ones_col"])
        E("v", lambda o: o.memset(ones_row[:], 1.0), writes=["ones_row"])
        E("v", lambda o: o.memset(gq_acc[:], 0.0), writes=["gq_acc"])
        E("v", lambda o: o.memset(gk_acc[:], 0.0), writes=["gk_acc"])
        E("v", lambda o: o.memset(mx8a[0][:], -3e30), writes=["mx8a0"])
        E("v", lambda o: o.memset(mx8a[1][:], -3e30), writes=["mx8a1"])
        E("v", lambda o: o.memset(mx8b[0][:], -3e30), writes=["mx8b0"])
        E("v", lambda o: o.memset(mx8b[1][:], -3e30), writes=["mx8b1"])
        E("v", lambda o: o.memset(probs_all[:], -2e30), writes=["probs_all"])
        E("v", lambda o: o.memset(ceps[:], 1e-5), writes=["ceps"])
        E("g", lambda o: o.dma_start(
            out=gw_v[:, :, 0:8], in_=gwt_in.rearrange("(c p) f -> p c f", p=128)),
          writes=["gw_bf"], dma=True, tag="ld_gw", pin=True)
        E("y", lambda o: o.dma_start(out=gb_row[0:1, :], in_=gb_in),
          writes=["bias_row"], dma=True, tag="ld_gb", pin=True)
        E("v", lambda o: o.tensor_copy(out=gb_row_bf[:], in_=gb_row[:]),
          reads=["bias_row"], writes=["gb_row_bf"], tag="gb_cast")
        E("t", lambda o: o.matmul(pg[:, 0:8], ones_row[:], gb_row_bf[0:1, :],
                                  start=True, stop=True),
          reads=["gb_row_bf", "ones_row"], writes=["pg"], tag="gb_mm")
        E("v", lambda o: o.tensor_copy(out=bias_b[:], in_=pg[:, 0:8]),
          reads=["pg"], writes=["bias_b"], tag="gb_ev")

        def wload(bufkey, dst, src, tag):
            E("g", lambda o, d=dst, s_=src: o.dma_start(
                out=d, in_=s_.rearrange("(c p) f -> p c f", p=128)),
              writes=[bufkey], dma=True, tag=tag)

        wload("wA0", wA[0][:], wq_in, "ld_wq")
        wload("wA1", wA[1][:], wk_in, "ld_wk")

        def layernorm(par, src_buf, src_key, dst_buf, dst_key, tg):
            st_, mv_, rs_, nb_ = stats[par], mv[par], rstd[par], nbias[par]
            E("v", lambda o: o.bn_stats(out=st_[:, 0:6], in_=src_buf[:, 0:512]),
              reads=[src_key], writes=[f"stats{par}a"], tag=f"{tg}_bnsa")
            E("v", lambda o: o.bn_stats(out=st_[:, 6:12], in_=src_buf[:, 512:1024]),
              reads=[src_key], writes=[f"stats{par}b"], tag=f"{tg}_bnsb")
            E("v", lambda o: o.bn_aggr(out=mv_[:], in_=st_[:]),
              reads=[f"stats{par}a", f"stats{par}b"], writes=[f"mv{par}"],
              tag=f"{tg}_bna")
            E("s", lambda o: o.activation(out=rs_[:, 0:1], in_=mv_[:, 1:2],
                                          func=AF.Ln, bias=ceps[:], scale=1.0),
              reads=[f"mv{par}", "ceps"], writes=[f"rstd{par}0"], tag=f"{tg}_ln_")
            E("s", lambda o: o.activation(out=rs_[:, 1:2], in_=rs_[:, 0:1],
                                          func=AF.Exp, bias=0.0, scale=-0.5),
              reads=[f"rstd{par}0"], writes=[f"rstd{par}1"], tag=f"{tg}_rex")
            E("v", lambda o: o.tensor_scalar(out=nb_[:], in0=mv_[:, 0:1],
                                             scalar1=rs_[:, 1:2], scalar2=-1.0,
                                             op0=OP.mult, op1=OP.mult),
              reads=[f"mv{par}", f"rstd{par}1"], writes=[f"nbias{par}"],
              tag=f"{tg}_nb")
            E("s", lambda o: o.activation(out=dst_buf[:], in_=src_buf[:],
                                          func=AF.Identity, bias=nb_[:],
                                          scale=rs_[:, 1:2]),
              reads=[src_key, f"nbias{par}", f"rstd{par}1"], writes=[dst_key],
              tag=f"{tg}_ln")

        NTA = min(KT, NT) if 'A' in PH else 0

        def A_S1(i):
            par = i % 2
            xk = f"xb{i % 3}"
            lnk = f"ln1_bf{par}"
            E("y", lambda o, i=i: o.dma_start(
                out=xb[i % 3][:], in_=x_in[i*128:(i+1)*128, :]),
              writes=[xk], dma=True, tag=f"A_ldx{i}")
            layernorm(par, xb[i % 3], xk, ln1_bf[par], lnk, f"A{i}")
            for c in range(NC_K):
                E("t", lambda o, c=c, par=par: o.transpose(
                    ptb[:, c*128:(c+1)*128], ln1_bf[par][:, c*128:(c+1)*128],
                    ident[:]),
                  reads=[lnk, "ident"], writes=["ptb"], tag=f"A_tr{i}_{c}")
            E("v", lambda o, i=i: o.tensor_copy(
                out=ln1T_v[:, :, i*128:(i+1)*128], in_=ptb_v),
              reads=["ptb"], writes=[f"ln1T{i}"], tag=f"A_ev{i}")

        def A_S2(i):
            par = i % 2
            xk = f"xb{i % 3}"
            qk = f"qxst{par}"
            pslot = par * 2
            mc, zc = mcol[par], zcol[par]
            for h in range(2):
                bnk, bk = P[pslot + h], f"P{pslot + h}"
                for c in range(NC_K):
                    E("t", lambda o, h=h, c=c, i=i, bnk=bnk: o.matmul(
                        bnk[:], ln1T_v[:, c, i*128:(i+1)*128],
                        wA_v[0][:, c, h*512:(h+1)*512],
                        start=(c == 0), stop=(c == NC_K - 1)),
                      reads=[f"ln1T{i}", "wA0"], writes=[bk],
                      tag=f"A_q{i}_{h}_{c}")
            E("v", lambda o, b=P[pslot], mc=mc: o.reduce_max(
                out=mc[:, 0:1], in_=b[:], axis=AX),
              reads=[f"P{pslot}"], writes=[f"m0_{par}"], tag=f"A_m0{i}")
            E("v", lambda o, b=P[pslot+1], mc=mc: o.reduce_max(
                out=mc[:, 1:2], in_=b[:], axis=AX),
              reads=[f"P{pslot+1}"], writes=[f"m1_{par}"], tag=f"A_m1{i}")
            E("v", lambda o, mc=mc: o.tensor_scalar(
                out=mc[:, 3:4],
                in0=mc[:, 0:1], scalar1=mc[:, 1:2], scalar2=-1.0 / 32.0,
                op0=OP.max, op1=OP.mult),
              reads=[f"m0_{par}", f"m1_{par}"], writes=[f"negM{par}"],
              tag=f"A_nM{i}")
            for h in range(2):
                bnk, bk = P[pslot + h], f"P{pslot + h}"
                E("s", lambda o, h=h, bnk=bnk, par=par, mc=mc, zc=zc: o.activation(
                    out=expbuf[par][:, h*512:(h+1)*512], in_=bnk[:], func=AF.Exp,
                    bias=mc[:, 3:4], scale=1.0/32.0,
                    accum_out=zc[:, h:h+1]),
                  reads=[bk, f"negM{par}"], writes=[f"exp{par}_{h}", f"z{par}{h}"],
                  tag=f"A_e{i}_{h}")
            E("v", lambda o, zc=zc: o.tensor_tensor(
                out=zc[:, 2:3], in0=zc[:, 0:1], in1=zc[:, 1:2], op=OP.add),
              reads=[f"z{par}0", f"z{par}1"], writes=[f"Z{par}"], tag=f"A_Z{i}")
            E("v", lambda o, zc=zc: o.reciprocal(out=zc[:, 3:4], in_=zc[:, 2:3]),
              reads=[f"Z{par}"], writes=[f"rZ{par}"], tag=f"A_rZ{i}")
            for h in range(2):
                bnk, bk = P[pslot + h], f"P{pslot + h}"
                E("v", lambda o, h=h, bnk=bnk, par=par: o.tensor_tensor(
                    out=expbuf[par][:, h*512:(h+1)*512], in0=bnk[:],
                    in1=expbuf[par][:, h*512:(h+1)*512], op=OP.mult),
                  reads=[bk, f"exp{par}_{h}"], writes=[f"exp{par}_{h}"],
                  tag=f"A_w{i}_{h}")
                E("v", lambda o, h=h, par=par, zc=zc: o.scalar_tensor_tensor(
                    out=gq_acc[:, h*512:(h+1)*512],
                    in0=expbuf[par][:, h*512:(h+1)*512], scalar=zc[:, 3:4],
                    in1=gq_acc[:, h*512:(h+1)*512], op0=OP.mult, op1=OP.add),
                  reads=[f"exp{par}_{h}", f"rZ{par}", "gq_acc"],
                  writes=["gq_acc"], tag=f"A_g{i}_{h}")
                E("v", lambda o, h=h, bnk=bnk, par=par, i=i: o.tensor_tensor(
                    out=qxst[par][:, h*512:(h+1)*512], in0=bnk[:],
                    in1=xb[i % 3][:, h*512:(h+1)*512], op=OP.add),
                  reads=[bk, xk], writes=[qk], tag=f"A_qx{i}_{h}")
            E("s", lambda o, i=i, par=par: o.dma_start(
                out=qx_hbm[i*128:(i+1)*128, :], in_=qxst[par][:]),
              reads=[qk], writes=[f"qxhbm{i}"], dma=True, tag=f"A_stq{i}")

        if NTA:
            A_S1(0)
            for i in range(NTA):
                if i + 1 < NTA:
                    A_S1(i + 1)
                A_S2(i)

        wload("wA2", wA[2][:], wv_in, "ld_wv")
        qx_flush = pl.queue_flush("s")

        E("v", lambda o: o.tensor_copy(out=acc_bf[:], in_=gq_acc[:]),
          reads=["gq_acc"], writes=["acc_bf"], tag="gq_cast")
        for h in range(2):
            E("t", lambda o, h=h: o.matmul(P[4][0:1, :], ones_col[:],
                                           acc_bf[:, h*512:(h+1)*512],
                                           start=True, stop=True),
              reads=["acc_bf", "ones_col"], writes=["P4"], tag=f"gq_mm{h}")
            E("v", lambda o, h=h: o.tensor_copy(
                out=gq_row_bf[0:1, h*512:(h+1)*512], in_=P[4][0:1, :]),
              reads=["P4"], writes=["gq_row_bf"], tag=f"gq_ev{h}")
        for h in range(2):
            E("t", lambda o, h=h: o.matmul(P[4][:], ones_row[:],
                                           gq_row_bf[0:1, h*512:(h+1)*512],
                                           start=True, stop=True),
              reads=["gq_row_bf", "ones_row"], writes=["P4"], tag=f"gqb_mm{h}")
            E("v", lambda o, h=h: o.tensor_copy(
                out=gq_bcast[:, h*512:(h+1)*512], in_=P[4][:]),
              reads=["P4"], writes=["gq_bcast"], tag=f"gqb_ev{h}")

        for i in range(NT if 'B' in PH else 0):
            par = i % 2
            pslot = par * 2
            pb = pbuf[par]
            mc, zc = mcol[par], zcol[par]
            for h in range(2):
                bnk, bk = P[pslot + h], f"P{pslot + h}"
                for c in range(NC_K):
                    E("t", lambda o, h=h, c=c, i=i, bnk=bnk: o.matmul(
                        bnk[:], ln1T_v[:, c, i*128:(i+1)*128],
                        wA_v[1][:, c, h*512:(h+1)*512],
                        start=(c == 0), stop=(c == NC_K - 1)),
                      reads=[f"ln1T{i}", "wA1"], writes=[bk],
                      tag=f"B_k{i}_{h}_{c}")
                E("v", lambda o, h=h, bnk=bnk, pb=pb: o.tensor_tensor(
                    out=pb[:, h*512:(h+1)*512], in0=bnk[:],
                    in1=gq_bcast[:, h*512:(h+1)*512], op=OP.mult),
                  reads=[bk, "gq_bcast"], writes=[f"pb{par}_{h}"],
                  tag=f"B_p{i}_{h}")
            E("v", lambda o, pb=pb, mc=mc: o.reduce_max(
                out=mc[:, 2:3], in_=pb[:], axis=AX),
              reads=[f"pb{par}_0", f"pb{par}_1"], writes=[f"M{par}"],
              tag=f"B_M{i}")
            E("v", lambda o, mc=mc: o.tensor_scalar(
                out=mc[:, 3:4], in0=mc[:, 2:3], scalar1=-1.0/32.0, scalar2=None,
                op0=OP.mult),
              reads=[f"M{par}"], writes=[f"negM{par}"], tag=f"B_nM{i}")
            E("s", lambda o, pb=pb, par=par, mc=mc, zc=zc: o.activation(
                out=expbuf[par][:], in_=pb[:], func=AF.Exp,
                bias=mc[:, 3:4], scale=1.0/32.0, accum_out=zc[:, 2:3]),
              reads=[f"pb{par}_0", f"pb{par}_1", f"negM{par}"],
              writes=[f"exp{par}_0", f"exp{par}_1", f"Z{par}"], tag=f"B_e{i}")
            E("v", lambda o, zc=zc: o.reciprocal(out=zc[:, 3:4], in_=zc[:, 2:3]),
              reads=[f"Z{par}"], writes=[f"rZ{par}"], tag=f"B_rZ{i}")
            E("v", lambda o, pb=pb, par=par: o.tensor_tensor(
                out=expbuf[par][:], in0=pb[:], in1=expbuf[par][:], op=OP.mult),
              reads=[f"pb{par}_0", f"pb{par}_1", f"exp{par}_0", f"exp{par}_1"],
              writes=[f"exp{par}_0", f"exp{par}_1"], tag=f"B_w{i}")
            E("v", lambda o, par=par, zc=zc: o.scalar_tensor_tensor(
                out=gk_acc[:], in0=expbuf[par][:], scalar=zc[:, 3:4],
                in1=gk_acc[:], op0=OP.mult, op1=OP.add),
              reads=[f"exp{par}_0", f"exp{par}_1", f"rZ{par}", "gk_acc"],
              writes=["gk_acc"], tag=f"B_g{i}")

        wload("wA0", wA[0][:], wr_in, "ld_wr")

        E("v", lambda o: o.tensor_copy(out=acc_bf[:], in_=gk_acc[:]),
          reads=["gk_acc"], writes=["acc_bf"], tag="gk_cast")
        for c in range(NC_K):
            E("t", lambda o, c=c: o.transpose(ptb[:, c*128:(c+1)*128],
                                              acc_bf[:, c*128:(c+1)*128],
                                              ident[:]),
              reads=["acc_bf", "ident"], writes=["ptb"], tag=f"gk_tr{c}")
            E("v", lambda o, c=c: o.reduce_sum(out=gk_f[:, c:c+1],
                                               in_=ptb[:, c*128:(c+1)*128],
                                               axis=AX),
              reads=["ptb"], writes=["gk_f"], tag=f"gk_rs{c}")

        def C_vu(ch):
            t0 = ch * SUB
            cpar = ch % 2
            uv, uk = u_v[cpar], f"u{cpar}"
            for j in range(NC_K):
                bnk, bk = P[j % 2], f"P{j % 2}"
                for c in range(NC_K):
                    E("t", lambda o, j=j, c=c, t0=t0, bnk=bnk: o.matmul(
                        bnk[:], wA_v[2][:, c, j*128:(j+1)*128],
                        ln1T_v[:, c, t0:t0+SUB],
                        start=(c == 0), stop=(c == NC_K - 1)),
                      reads=["wA2", f"ln1T{t0//128}"],
                      writes=[bk], tag=f"C_v{ch}_{j}_{c}")
                E("v", lambda o, j=j, bnk=bnk, uv=uv: o.tensor_scalar(
                    out=uv[:, j, :], in0=bnk[:], scalar1=gk_f[:, j:j+1],
                    scalar2=None, op0=OP.mult),
                  reads=[bk, "gk_f"], writes=[uk], tag=f"C_u{ch}_{j}")

        def C_S1(ti):
            ch, tt = ti // 4, ti % 4
            par = ti % 2
            uv, uk = u_v[ch % 2], f"u{ch % 2}"
            for h in range(2):
                bnk = P[2 + (tt % 2) * 2 + h]
                bk = f"P{2 + (tt % 2) * 2 + h}"
                for j in range(NC_K):
                    E("t", lambda o, j=j, tt=tt, h=h, bnk=bnk, uv=uv: o.matmul(
                        bnk[:], uv[:, j, tt*128:(tt+1)*128],
                        wA_v[0][:, j, h*512:(h+1)*512],
                        start=(j == 0), stop=(j == NC_K - 1)),
                      reads=[uk, "wA0"], writes=[bk],
                      tag=f"C_r{ch}_{tt}_{h}_{j}")
            E("y", lambda o, ti=ti, par=par: o.dma_start(
                out=qxst[par][:], in_=qx_hbm[ti*128:(ti+1)*128, :]),
              writes=[f"qxst{par}"], dma=True, extra_deps=[qx_flush],
              tag=f"C_ldqx{ti}")
            for h in range(2):
                bnk = P[2 + (tt % 2) * 2 + h]
                bk = f"P{2 + (tt % 2) * 2 + h}"
                E("v", lambda o, h=h, par=par, bnk=bnk: o.scalar_tensor_tensor(
                    out=x1t[par][:, h*512:(h+1)*512], in0=bnk[:],
                    scalar=1.0, in1=qxst[par][:, h*512:(h+1)*512],
                    op0=OP.mult, op1=OP.add),
                  reads=[bk, f"qxst{par}"], writes=[f"x1t{par}"],
                  tag=f"C_x1{ti}_{h}")
            E("s", lambda o, ti=ti, par=par: o.dma_start(
                out=x1_hbm[ti*128:(ti+1)*128, :], in_=x1t[par][:]),
              reads=[f"x1t{par}"], writes=[f"x1hbm{ti}"], dma=True,
              tag=f"C_stx1{ti}")
            layernorm(par, x1t[par], f"x1t{par}", ln2_bf[par], f"ln2_bf{par}",
                      f"C{ti}")

        def C_S2(ti):
            ch, tt = ti // 4, ti % 4
            par = ti % 2
            for c in range(NC_K):
                E("t", lambda o, c=c, par=par: o.transpose(
                    ptb[:, c*128:(c+1)*128],
                    ln2_bf[par][:, c*128:(c+1)*128], ident[:]),
                  reads=[f"ln2_bf{par}", "ident"], writes=["ptb"],
                  tag=f"C_tr{ti}_{c}")
            E("v", lambda o, tt=tt: o.tensor_copy(
                out=ln2st_v[:, :, tt*128:(tt+1)*128], in_=ptb_v),
              reads=["ptb"], writes=[f"ln2st{tt}"], tag=f"C_ev{ti}")
            for c in range(NC_K):
                E("t", lambda o, c=c, tt=tt: o.matmul(
                    pg[:, 0:6], ln2st_v[:, c, tt*128:(tt+1)*128],
                    gw_v[:, c, 0:6], start=(c == 0), stop=(c == NC_K - 1)),
                  reads=[f"ln2st{tt}", "gw_bf"], writes=["pg"],
                  tag=f"C_gmm{ti}_{c}")
            E("v", lambda o, par=par: o.tensor_tensor(
                out=ptmp[par][:, 0:6], in0=pg[:, 0:6],
                in1=bias_b[:, 0:6], op=OP.add),
              reads=["pg", "bias_b"], writes=[f"ptmp{par}"], tag=f"C_gb{ti}")
            E("s", lambda o, par=par: o.activation(
                out=etmp[par][:, 0:6], in_=ptmp[par][:, 0:6],
                func=AF.Exp, bias=0.0, scale=-1.0),
              reads=[f"ptmp{par}"], writes=[f"etmp{par}"], tag=f"C_sg1{ti}")
            E("v", lambda o, par=par: o.tensor_scalar(
                out=etmp[par][:, 0:6], in0=etmp[par][:, 0:6], scalar1=1.0,
                scalar2=None, op0=OP.add),
              reads=[f"etmp{par}"], writes=[f"etmp{par}"], tag=f"C_sg2{ti}")
            E("v", lambda o, ti=ti, par=par: o.reciprocal(
                out=probs_all[:, ti*8:ti*8+6], in_=etmp[par][:, 0:6]),
              reads=[f"etmp{par}"], writes=[f"probs{ti}"], tag=f"C_sg3{ti}")
            o8 = ti * 8
            Pt = probs_all[:, o8:o8+8]
            ma, mb = mx8a[par], mx8b[par]
            p2_, p3_ = p2buf[par], p3buf[par]
            fm, sm, gc = fmask[par], smask[par], gcol[par]
            E("v", lambda o, Pt=Pt, ma=ma: o.reduce_max(
                out=ma[:, 0:1], in_=Pt, axis=AX),
              reads=[f"probs{ti}", "probs_all"], writes=[f"mx8a_v{par}"],
              tag=f"G_m1{ti}")
            E("v", lambda o, Pt=Pt, ma=ma, p2_=p2_: o.match_replace(
                out=p2_[:], in_to_replace=ma[:], in_values=Pt,
                imm_value=-2e30),
              reads=[f"probs{ti}", "probs_all", f"mx8a{par}",
                     f"mx8a_v{par}"],
              writes=[f"p2buf{par}"], tag=f"G_mr1{ti}")
            E("v", lambda o, mb=mb, p2_=p2_: o.reduce_max(
                out=mb[:, 0:1], in_=p2_[:], axis=AX),
              reads=[f"p2buf{par}"], writes=[f"mx8b_v{par}"],
              tag=f"G_m2{ti}")
            E("v", lambda o, Pt=Pt, fm=fm, p2_=p2_: o.tensor_tensor(
                out=fm[:], in0=Pt, in1=p2_[:], op=OP.not_equal),
              reads=[f"probs{ti}", "probs_all", f"p2buf{par}"],
              writes=[f"fmask{par}"], tag=f"G_f{ti}")
            E("v", lambda o, mb=mb, p2_=p2_, p3_=p3_: o.match_replace(
                out=p3_[:], in_to_replace=mb[:], in_values=p2_[:],
                imm_value=-2e30),
              reads=[f"p2buf{par}", f"mx8b{par}", f"mx8b_v{par}"],
              writes=[f"p3buf{par}"], tag=f"G_mr2{ti}")
            E("v", lambda o, sm=sm, p2_=p2_, p3_=p3_: o.tensor_tensor(
                out=sm[:], in0=p2_[:], in1=p3_[:], op=OP.not_equal),
              reads=[f"p2buf{par}", f"p3buf{par}"], writes=[f"smask{par}"],
              tag=f"G_s{ti}")
            E("v", lambda o, gc=gc, ma=ma, mb=mb: o.scalar_tensor_tensor(
                out=gc[:, 0:1], in0=ma[:, 0:1], scalar=1e-8, in1=mb[:, 0:1],
                op0=OP.add, op1=OP.add),
              reads=[f"mx8a_v{par}", f"mx8b_v{par}"], writes=[f"den{par}"],
              tag=f"G_d{ti}")
            E("v", lambda o, gc=gc: o.reciprocal(out=gc[:, 2:3],
                                                 in_=gc[:, 0:1]),
              reads=[f"den{par}"], writes=[f"rden{par}"], tag=f"G_rd{ti}")
            E("v", lambda o, gc=gc, ma=ma: o.tensor_scalar(
                out=gc[:, 3:4], in0=ma[:, 0:1], scalar1=gc[:, 2:3],
                scalar2=None, op0=OP.mult),
              reads=[f"mx8a_v{par}", f"rden{par}"], writes=[f"gm{par}"],
              tag=f"G_g1{ti}")
            E("v", lambda o, o8=o8, fm=fm, gc=gc: o.tensor_scalar(
                out=ge_all[:, o8:o8+8], in0=fm[:], scalar1=gc[:, 3:4],
                scalar2=None, op0=OP.mult),
              reads=[f"fmask{par}", f"gm{par}"], writes=[f"ge{ti}"],
              tag=f"G_ge1{ti}")
            E("v", lambda o, gc=gc, mb=mb: o.tensor_scalar(
                out=gc[:, 3:4], in0=mb[:, 0:1], scalar1=gc[:, 2:3],
                scalar2=None, op0=OP.mult),
              reads=[f"mx8b_v{par}", f"rden{par}"], writes=[f"gm{par}"],
              tag=f"G_g2{ti}")
            E("v", lambda o, o8=o8, sm=sm, gc=gc: o.scalar_tensor_tensor(
                out=ge_all[:, o8:o8+8], in0=sm[:], scalar=gc[:, 3:4],
                in1=ge_all[:, o8:o8+8], op0=OP.mult, op1=OP.add),
              reads=[f"smask{par}", f"gm{par}", f"ge{ti}"],
              writes=[f"ge{ti}"], tag=f"G_ge2{ti}")
            if tt == 3:
                E("s", lambda o, ch=ch: o.dma_start(
                    out=ln2t_hbm_v[:, :, ch*SUB:(ch+1)*SUB],
                    in_=ln2st_v[:, :, :]),
                  reads=[f"ln2st{q}" for q in range(4)],
                  writes=[f"ln2thbm{ch}"], dma=True, tag=f"C_stln2{ch}")

        if 'C' in PH:
            pend = None
            for ch in range(NSUB):
                C_vu(ch)
                for tt in range(4):
                    ti = ch * 4 + tt
                    C_S1(ti)
                    if pend is not None:
                        C_S2(pend)
                    pend = ti
            C_S2(pend)

        es_abc.close()
        pl.barrier()

        y_sb = sb("y_sb", [128, NT * DM], BF16)
        wF = [sb(f"wF{i}", [128, NC_K * DM], BF16) for i in range(4)]
        slab = [sb(f"slab{i}", [128, NC_K * SUB], BF16) for i in range(2)]
        h1s = [sb(f"h1s{i}", [128, SUB], F32) for i in range(2)]
        hb = sb("hb", [128, NC_K * SUB], BF16)
        x1ld = [sb(f"x1ld{i}", [128, DM], F32) for i in range(2)]
        outt = [sb(f"outt{i}", [128, DM], F32) for i in range(2)]
        wF_v = [w[:].rearrange("p (c f) -> p c f", c=NC_K) for w in wF]
        slab_v = [s_[:].rearrange("p (c t) -> p c t", c=NC_K) for s_ in slab]
        hb_v = hb[:].rearrange("p (c t) -> p c t", c=NC_K)
        y_v = y_sb[:].rearrange("p (n f) -> p n f", n=NT)

        E("v", lambda o: o.memset(y_sb[:], 0.0), writes=["y_sb"], tag="D_zy")

        branches = [(ew1t_in[e], ew3t_in[e], ew2t_in[e], e) for e in range(NE)]
        branches.append((fc1t_in, fc2t_in, fc3t_in, None))
        if "D" not in PH:
            branches = branches[-1:] if "d" in PH else []

        wslot = {}

        def emit_branch_loads(b):
            w1, w3, w2, _e = branches[b]
            for j, wsrc in enumerate((w1, w3, w2)):
                slot = (3 * b + j) % 4
                wslot[(b, j)] = slot
                E("g", lambda o, slot=slot, s_=wsrc: o.dma_start(
                    out=wF[slot][:], in_=s_.rearrange("(c p) f -> p c f", p=128)),
                  writes=[f"slot{slot}"], dma=True, tag=f"ld_w{b}_{j}")

        if branches:
            emit_branch_loads(0)

        allhb = [f"hb{j}" for j in range(NC_K)]
        for b, (w1, w3, w2, e) in enumerate(branches):
            shared = e is None
            s1, s3, s2 = wslot[(b, 0)], wslot[(b, 1)], wslot[(b, 2)]
            for sb_i in range(NSUB):
                sl, slv = slab[sb_i % 2], slab_v[sb_i % 2]
                slk = f"slab{sb_i % 2}"
                E("y", lambda o, sb_i=sb_i, sl=sl: o.dma_start(
                    out=sl[:], in_=ln2t_hbm_v[:, :, sb_i*SUB:(sb_i+1)*SUB]),
                  writes=[slk], dma=True, tag=f"D_lds{b}_{sb_i}")
                for j in range(NC_K):
                    b1k, b1n = P[j % 2], f"P{j % 2}"
                    b3k, b3n = P[2 + j % 2], f"P{2 + j % 2}"
                    for c in range(NC_K):
                        E("t", lambda o, j=j, c=c, b1k=b1k, slv=slv, s1=s1: o.matmul(
                            b1k[:], wF_v[s1][:, c, j*128:(j+1)*128],
                            slv[:, c, :],
                            start=(c == 0), stop=(c == NC_K - 1)),
                          reads=[f"slot{s1}", slk], writes=[b1n],
                          tag=f"D_h1_{b}_{sb_i}_{j}_{c}")
                    for c in range(NC_K):
                        E("t", lambda o, j=j, c=c, b3k=b3k, slv=slv, s3=s3: o.matmul(
                            b3k[:], wF_v[s3][:, c, j*128:(j+1)*128],
                            slv[:, c, :],
                            start=(c == 0), stop=(c == NC_K - 1)),
                          reads=[f"slot{s3}", slk], writes=[b3n],
                          tag=f"D_h3_{b}_{sb_i}_{j}_{c}")
                    E("s", lambda o, j=j, b1k=b1k: o.activation(
                        out=h1s[j % 2][:], in_=b1k[:], func=AF.Silu),
                      reads=[b1n], writes=[f"h1s{j % 2}"],
                      tag=f"D_si{b}_{sb_i}_{j}")
                    E("v", lambda o, j=j, b3k=b3k: o.tensor_tensor(
                        out=hb_v[:, j, :], in0=h1s[j % 2][:], in1=b3k[:],
                        op=OP.mult),
                      reads=[f"h1s{j % 2}", b3n], writes=[f"hb{j}"],
                      tag=f"D_hb{b}_{sb_i}_{j}")
                for tt in range(4):
                    ti = sb_i * 4 + tt
                    if shared:
                        E("y", lambda o, ti=ti: o.dma_start(
                            out=x1ld[ti % 2][:],
                            in_=x1_hbm[ti*128:(ti+1)*128, :]),
                          writes=[f"x1ld{ti % 2}"],
                          dma=True, tag=f"D_ldx1{ti}")
                    for h in range(2):
                        bk, bn = P[4 + (2*tt + h) % 2], f"P{4 + (2*tt + h) % 2}"
                        for j in range(NC_K):
                            E("t", lambda o, j=j, tt=tt, h=h, bk=bk, s2=s2: o.matmul(
                                bk[:], hb_v[:, j, tt*128:(tt+1)*128],
                                wF_v[s2][:, j, h*512:(h+1)*512],
                                start=(j == 0), stop=(j == NC_K - 1)),
                              reads=allhb + [f"slot{s2}"], writes=[bn],
                              tag=f"D_y{b}_{sb_i}_{tt}_{h}_{j}")
                        if not shared:
                            E("v", lambda o, ti=ti, h=h, bk=bk, e=e: o.scalar_tensor_tensor(
                                out=y_v[:, ti, h*512:(h+1)*512], in0=bk[:],
                                scalar=ge_all[:, ti*8+e:ti*8+e+1],
                                in1=y_v[:, ti, h*512:(h+1)*512],
                                op0=OP.mult, op1=OP.add),
                              reads=[bn, f"ge{ti}", "y_sb"], writes=["y_sb"],
                              tag=f"D_ye{b}_{ti}_{h}")
                        else:
                            E("v", lambda o, ti=ti, h=h, bk=bk: o.scalar_tensor_tensor(
                                out=outt[ti % 2][:, h*512:(h+1)*512], in0=bk[:],
                                scalar=1.0, in1=y_v[:, ti, h*512:(h+1)*512],
                                op0=OP.mult, op1=OP.add),
                              reads=[bn, "y_sb"], writes=[f"outt{ti % 2}"],
                              tag=f"D_yz{ti}_{h}")
                            E("v", lambda o, ti=ti, h=h: o.tensor_tensor(
                                out=outt[ti % 2][:, h*512:(h+1)*512],
                                in0=outt[ti % 2][:, h*512:(h+1)*512],
                                in1=x1ld[ti % 2][:, h*512:(h+1)*512],
                                op=OP.add),
                              reads=[f"outt{ti % 2}", f"x1ld{ti % 2}"],
                              writes=[f"outt{ti % 2}"], tag=f"D_out{ti}_{h}")
                    if shared:
                        E("s", lambda o, ti=ti: o.dma_start(
                            out=out_hbm[ti*128:(ti+1)*128, :],
                            in_=outt[ti % 2][:]),
                          reads=[f"outt{ti % 2}"], writes=[f"outhbm{ti}"],
                          dma=True, tag=f"D_sto{ti}")
                if sb_i == 0 and b + 1 < len(branches):
                    emit_branch_loads(b + 1)

        pl.barrier()
        if not branches:
            E("v", lambda o: o.memset(outt[0][:], 0.0), writes=["outt0"],
              tag="dbg_z")
            for ti in range(NT):
                E("s", lambda o, ti=ti: o.dma_start(
                    out=out_hbm[ti*128:(ti+1)*128, :], in_=outt[0][:]),
                  reads=["outt0"], dma=True, tag=f"dbg_sto{ti}")
            pl.barrier()
        pl.finalize(nc, es, block)
    return nc


_NC_CACHE = {}


def build_cached(S=4096):
    if S not in _NC_CACHE:
        _NC_CACHE[S] = build_program(S)
    return _NC_CACHE[S]


def make_in_maps(L0_attn_x, L0_attn_Wq, L0_attn_Wk, L0_attn_Wv, L0_attn_Wr,
                 L0_ffn_gate_w, L0_ffn_gate_b, L0_ffn_ew1, L0_ffn_ew2,
                 L0_ffn_ew3, L0_ffn_fc1, L0_ffn_fc2, L0_ffn_fc3):
    x = np.ascontiguousarray(np.asarray(L0_attn_x, dtype=np.float32))
    f32 = lambda a: np.ascontiguousarray(np.asarray(a, dtype=np.float32))
    T2 = lambda a: np.ascontiguousarray(np.swapaxes(f32(a), -1, -2))
    gwt = np.zeros((DM, 8), np.float32)
    gwt[:, 0:6] = T2(L0_ffn_gate_w)
    gb = np.zeros((1, 8), np.float32)
    gb[0, 0:6] = f32(L0_ffn_gate_b)
    shared = {"wq": f32(L0_attn_Wq), "wk": f32(L0_attn_Wk),
              "wv": f32(L0_attn_Wv), "wr": f32(L0_attn_Wr),
              "gwt": gwt, "gb": gb,
              "ew1t": T2(L0_ffn_ew1), "ew3t": T2(L0_ffn_ew3),
              "ew2t": T2(L0_ffn_ew2),
              "fc1t": T2(L0_ffn_fc1), "fc2t": T2(L0_ffn_fc2),
              "fc3t": T2(L0_ffn_fc3),
              "ident": np.eye(128, dtype=ml_dtypes.bfloat16)}
    return [dict(shared, x=np.ascontiguousarray(x[b]))
            for b in range(x.shape[0])]


def kernel(**inputs):
    in_maps = make_in_maps(**inputs)
    nc = build_cached(4096)
    res = run_bass_kernel_spmd(nc, in_maps, core_ids=list(range(len(in_maps))))
    return np.stack([r["out"] for r in res.results], axis=0)


# version 07
# speedup vs baseline: 2.2905x; 1.0014x over previous
import os
import numpy as np
import ml_dtypes
from contextlib import ExitStack
import concourse.bass as bass
import concourse.mybir as mybir
from concourse.bass_utils import run_bass_kernel_spmd

F32 = mybir.dt.float32
BF16 = mybir.dt.bfloat16
AX = mybir.AxisListType.X
OP = mybir.AluOpType
AF = mybir.ActivationFunctionType

DM = 1024
NC_K = DM // 128
NE = 6
SUB = 512


class Plan:

    ENGINES = ("t", "v", "s", "y", "g")
    NSLOT = 8

    def __init__(self):
        self.steps = {e: [] for e in self.ENGINES}
        self.last_write = {}
        self.readers = {}
        self.dma_count = {e: 0 for e in self.ENGINES}
        self.pin_count = 0
        self.seq = 0

    def emit(self, eng, fn, reads=(), writes=(), dma=False, tag="", extra_deps=(),
             pin=False):
        idx = len(self.steps[eng])
        deps = set(extra_deps)
        for k in reads:
            w = self.last_write.get(k)
            if w is not None:
                deps.add(w)
        for k in writes:
            w = self.last_write.get(k)
            if w is not None:
                deps.add(w)
            for r in self.readers.get(k, ()):
                deps.add(r)
        deps = {(de, di) for (de, di) in deps if not (de == eng and di >= idx)}
        if eng == "t":
            deps = {(de, di) for (de, di) in deps if de != "t"}
        st = {"fn": fn, "deps": sorted(deps), "dma": dma, "tag": tag,
              "seq": self.seq}
        if dma:
            if pin:
                st["dslot"] = ("pin", self.pin_count)
                self.pin_count += 1
            else:
                st["dslot"] = self.dma_count[eng] % self.NSLOT
                self.dma_count[eng] += 1
        self.seq += 1
        self.steps[eng].append(st)
        ev = (eng, idx)
        for k in reads:
            self.readers.setdefault(k, []).append(ev)
        for k in writes:
            self.last_write[k] = ev
            self.readers[k] = []
        return ev

    def barrier(self):
        lasts = []
        for e in self.ENGINES:
            if not self.steps[e]:
                continue
            lasts.append((e, len(self.steps[e]) - 1))
            seen = set()
            for i in range(len(self.steps[e]) - 1, -1, -1):
                st = self.steps[e][i]
                if st["dma"] and st["dslot"] not in seen:
                    seen.add(st["dslot"])
                    lasts.append((e, i))

        def mknop():
            def f(obj):
                return obj.nop()
            return f
        for e in self.ENGINES:
            deps = [d for d in lasts if d[0] != e] +                    [d for d in lasts if d[0] == e and self.steps[e][d[1]]["dma"]]
            deps = [d for d in deps if not (d[0] == e and d[1] >= len(self.steps[e]))]
            self.steps[e].append({"fn": mknop(), "deps": sorted(set(deps)),
                                  "dma": False, "tag": "barrier",
                                  "seq": self.seq})
            self.seq += 1

    def queue_flush(self, eng):
        deps = []
        seen = set()
        for i in range(len(self.steps[eng]) - 1, -1, -1):
            st = self.steps[eng][i]
            if st["dma"] and st["dslot"] not in seen:
                seen.add(st["dslot"])
                deps.append((eng, i))

        def f(obj):
            return obj.nop()
        idx = len(self.steps[eng])
        self.steps[eng].append({"fn": f, "deps": sorted(deps), "dma": False,
                                "tag": f"flush_{eng}", "seq": self.seq})
        self.seq += 1
        return (eng, idx)

    def final_wait(self, eng, deps):
        def f(obj):
            return obj.nop()
        self.steps[eng].append({"fn": f, "deps": sorted(set(deps)),
                                "dma": False, "tag": "final", "seq": self.seq})
        self.seq += 1

    def finalize(self, nc, es, block):
        kseq = int(os.environ.get("KSEQ", "0"))
        if kseq:
            for e in self.ENGINES:
                self.steps[e] = [st for st in self.steps[e]
                                 if st["seq"] < kseq]
            self.last_write = {k: v for k, v in self.last_write.items()
                               if v[1] < len(self.steps[v[0]])}
            self.readers = {}
            self.barrier()
        consumers = {}
        for e in self.ENGINES:
            for i, st in enumerate(self.steps[e]):
                if st["tag"] in ("barrier", "final") or                         st["tag"].startswith("flush"):
                    continue
                for d in st["deps"]:
                    consumers.setdefault(d, []).append((e, i))
        def reduce_consumers(evs):
            keep, last = [], {}
            for (de, di) in evs:
                if self.steps[de][di]["dma"]:
                    keep.append((de, di))
                elif di > last.get(de, -1):
                    last[de] = di
            return keep + [(de, di) for de, di in last.items()]

        for e in self.ENGINES:
            prev_by_slot = {}
            for i, st in enumerate(self.steps[e]):
                if not st["dma"] or isinstance(st["dslot"], tuple):
                    continue
                sl = st["dslot"]
                pv = prev_by_slot.get(sl)
                if pv is not None:
                    add = [pv] + reduce_consumers(consumers.get(pv, []))
                    myseq = st["seq"]
                    extra = []
                    for (de, di) in add:
                        if de == e and di >= i:
                            continue
                        dseq = self.steps[de][di]["seq"]
                        assert dseq < myseq, (
                            "slot reuse forward dep", e, i, st["tag"],
                            de, di, self.steps[de][di]["tag"])
                        extra.append((de, di))
                    st["deps"] = sorted(set(st["deps"]) | set(extra))
                prev_by_slot[sl] = (e, i)
        need_mark = {e: set() for e in self.ENGINES}
        for e in self.ENGINES:
            for st in self.steps[e]:
                for (de, di) in st["deps"]:
                    need_mark[de].add(di)
        sems = {(e, p): es.enter_context(nc.semaphore(f"sem_{e}{p}"))
                for e in self.ENGINES for p in range(2)}
        dsems = {}
        for e in self.ENGINES:
            if self.dma_count[e]:
                n = min(self.NSLOT, self.dma_count[e])
                for sl in range(n):
                    dsems[(e, sl)] = es.enter_context(
                        nc.semaphore(f"dsem_{e}{sl}"))
            for i, st in enumerate(self.steps[e]):
                if st.get("dslot") is not None and isinstance(st["dslot"], tuple):
                    dsems[(e, st["dslot"])] = es.enter_context(
                        nc.semaphore(f"dsem_{e}p{st['dslot'][1]}"))
        ev_sem = {}
        slot_counts = {}
        for e in self.ENGINES:
            mcnt = [0, 0]
            nmark = 0
            for i, st in enumerate(self.steps[e]):
                if st["dma"]:
                    sl = st["dslot"]
                    c = slot_counts.get((e, sl), 0) + 16
                    slot_counts[(e, sl)] = c
                    st["inc_sem"] = dsems[(e, sl)]
                    st["inc"] = 16
                    ev_sem[(e, i)] = (st["inc_sem"], c)
                elif i in need_mark[e]:
                    p = nmark % 2
                    nmark += 1
                    mcnt[p] += 1
                    st["inc_sem"] = sems[(e, p)]
                    st["inc"] = 1
                    st["chain"] = (sems[(e, p)], mcnt[p] - 1)
                    ev_sem[(e, i)] = (sems[(e, p)], mcnt[p])
                else:
                    st["inc"] = 0
                    ev_sem[(e, i)] = None

        def emit_stream(eng, obj):
            water = {}
            for i, st in enumerate(self.steps[eng]):
                waits = {}
                for d in st["deps"]:
                    sv = ev_sem[d]
                    assert sv is not None, (eng, i, d, st["tag"])
                    sem, v = sv
                    if v > waits.get(id(sem), (None, -1))[1]:
                        waits[id(sem)] = (sem, v)
                ch = st.get("chain")
                if ch is not None and ch[1] > 0:
                    if ch[1] > waits.get(id(ch[0]), (None, -1))[1]:
                        waits[id(ch[0])] = ch
                for sem, v in waits.values():
                    if water.get(id(sem), -1) >= v:
                        continue
                    obj.wait_ge(sem, v)
                    water[id(sem)] = v
                ins = st["fn"](obj)
                if st["inc"]:
                    ins.then_inc(st["inc_sem"], st["inc"])

        @block.tensor
        def _(t):
            emit_stream("t", t)

        @block.vector
        def _(v):
            emit_stream("v", v)

        @block.scalar
        def _(s):
            emit_stream("s", s)

        @block.sync
        def _(y):
            emit_stream("y", y)

        @block.gpsimd
        def _(g):
            emit_stream("g", g)


def build_program(S=4096):
    PH = os.environ.get("KPHASES", "ABCD")
    KT = int(os.environ.get("KTILES", "0")) or 10**9
    NT = S // 128
    NSUB = S // SUB
    nc = bass.Bass("TRN2", target_bir_lowering=False)

    x_in = nc.dram_tensor("x", [S, DM], F32, kind="ExternalInput").ap()
    wq_in = nc.dram_tensor("wq", [DM, DM], F32, kind="ExternalInput").ap()
    wk_in = nc.dram_tensor("wk", [DM, DM], F32, kind="ExternalInput").ap()
    wv_in = nc.dram_tensor("wv", [DM, DM], F32, kind="ExternalInput").ap()
    wr_in = nc.dram_tensor("wr", [DM, DM], F32, kind="ExternalInput").ap()
    gwt_in = nc.dram_tensor("gwt", [DM, 8], F32, kind="ExternalInput").ap()
    gb_in = nc.dram_tensor("gb", [1, 8], F32, kind="ExternalInput").ap()
    ew1t_in = nc.dram_tensor("ew1t", [NE, DM, DM], F32, kind="ExternalInput").ap()
    ew3t_in = nc.dram_tensor("ew3t", [NE, DM, DM], F32, kind="ExternalInput").ap()
    ew2t_in = nc.dram_tensor("ew2t", [NE, DM, DM], F32, kind="ExternalInput").ap()
    fc1t_in = nc.dram_tensor("fc1t", [DM, DM], F32, kind="ExternalInput").ap()
    fc2t_in = nc.dram_tensor("fc2t", [DM, DM], F32, kind="ExternalInput").ap()
    fc3t_in = nc.dram_tensor("fc3t", [DM, DM], F32, kind="ExternalInput").ap()
    ident_in = nc.dram_tensor("ident", [128, 128], BF16, kind="ExternalInput").ap()
    out_hbm = nc.dram_tensor("out", [S, DM], F32, kind="ExternalOutput").ap()
    qx_hbm = nc.dram_tensor("qx_cache", [S, DM], F32, kind="Internal").ap()
    x1_hbm = nc.dram_tensor("x1_cache", [S, DM], F32, kind="Internal").ap()
    ln2t_hbm = nc.dram_tensor("ln2t_cache", [NC_K, 128, S], BF16, kind="Internal").ap()
    ln2t_hbm_v = ln2t_hbm.rearrange("c p t -> p c t")

    es = ExitStack()
    es_abc = ExitStack()
    with es:
        sb = lambda n, shp, dt: es.enter_context(nc.sbuf_tensor(n, shp, dt))
        sba = lambda n, shp, dt: es_abc.enter_context(nc.sbuf_tensor(n, shp, dt))
        psa = lambda n, shp, dt: es.enter_context(nc.psum_tensor(n, shp, dt))
        sb2 = lambda n, shp, dt: [sb(n + "0", shp, dt), sb(n + "1", shp, dt)]
        sba2 = lambda n, shp, dt: [sba(n + "0", shp, dt), sba(n + "1", shp, dt)]

        ident = sb("ident_sb", [128, 128], BF16)
        ones_col = sb("ones_col", [128, 1], BF16)
        ones_row = sb("ones_row", [1, 128], BF16)
        bias_b = sb("bias_b", [128, 8], F32)
        gw_bf = sb("gw_bf", [128, NC_K * 8], BF16)
        probs_all = sb("probs_all", [128, NT * 8], F32)
        ge_all = sb("ge_all", [128, NT * 8], F32)
        mx8a = sb2("mx8a", [128, 8], F32)
        mx8b = sb2("mx8b", [128, 8], F32)
        p2buf = sb2("p2buf", [128, 8], F32)
        p3buf = sb2("p3buf", [128, 8], F32)
        fmask = sb2("fmask", [128, 8], F32)
        smask = sb2("smask", [128, 8], F32)
        gcol = sb2("gcol", [128, 4], F32)
        ptmp = sb2("ptmp", [128, 8], F32)
        etmp = sb2("etmp", [128, 8], F32)
        stats = sb2("stats", [128, 12], F32)
        mv = sb2("mv", [128, 2], F32)
        rstd = sb2("rstd", [128, 2], F32)
        nbias = sb2("nbias", [128, 1], F32)
        mcol = sb2("mcol", [128, 4], F32)
        zcol = sb2("zcol", [128, 4], F32)
        gk_f = sb("gk_f", [128, 8], F32)
        ceps = sb("ceps", [128, 1], F32)
        gb_row = sb("gb_row", [1, 8], F32)
        gb_row_bf = sb("gb_row_bf", [1, 8], BF16)

        P = [psa(f"P{i}", [128, 512], F32) for i in range(6)]
        ptb = psa("ptb", [128, 1024], BF16)
        pg = psa("pg", [128, 8], F32)

        block = es.enter_context(nc.Block())

        pl = Plan()
        E = pl.emit

        ln1T = sba("ln1T", [128, NC_K * S], BF16)
        wA = [sba(f"wA{i}", [128, NC_K * DM], BF16) for i in range(3)]
        xb = [sba(f"xb{i}", [128, DM], F32) for i in range(3)]
        qxst = sba2("qxst", [128, DM], F32)
        ln1_bf = sba2("ln1_bf", [128, DM], BF16)
        expbuf = sba2("expbuf", [128, DM], F32)
        pbuf = sba2("pbuf", [128, DM], F32)
        u_slab = sba2("u_slab", [128, NC_K * SUB], BF16)
        x1t = sba2("x1t", [128, DM], F32)
        ln2_bf = sba2("ln2_bf", [128, DM], BF16)
        ln2st = sba("ln2st", [128, NC_K * SUB], BF16)
        gq_acc = sba("gq_acc", [128, DM], F32)
        gk_acc = sba("gk_acc", [128, DM], F32)
        acc_bf = sba("acc_bf", [128, DM], BF16)
        gq_row_bf = sba("gq_row_bf", [1, DM], BF16)
        gq_bcast = sba("gq_bcast", [128, DM], BF16)

        ln1T_v = ln1T[:].rearrange("p (c t) -> p c t", c=NC_K)
        wA_v = [w[:].rearrange("p (c f) -> p c f", c=NC_K) for w in wA]
        u_v = [u[:].rearrange("p (c t) -> p c t", c=NC_K) for u in u_slab]
        ln2st_v = ln2st[:].rearrange("p (c t) -> p c t", c=NC_K)
        gw_v = gw_bf[:].rearrange("p (c f) -> p c f", c=NC_K)
        ptb_v = ptb[:].rearrange("p (c t) -> p c t", c=NC_K)

        E("y", lambda o: o.dma_start(out=ident[:], in_=ident_in),
          writes=["ident"], dma=True, tag="ld_ident", pin=True)
        E("v", lambda o: o.memset(ones_col[:], 1.0), writes=["ones_col"])
        E("v", lambda o: o.memset(ones_row[:], 1.0), writes=["ones_row"])
        E("v", lambda o: o.memset(gq_acc[:], 0.0), writes=["gq_acc"])
        E("v", lambda o: o.memset(gk_acc[:], 0.0), writes=["gk_acc"])
        E("v", lambda o: o.memset(mx8a[0][:], -3e30), writes=["mx8a0"])
        E("v", lambda o: o.memset(mx8a[1][:], -3e30), writes=["mx8a1"])
        E("v", lambda o: o.memset(mx8b[0][:], -3e30), writes=["mx8b0"])
        E("v", lambda o: o.memset(mx8b[1][:], -3e30), writes=["mx8b1"])
        E("v", lambda o: o.memset(probs_all[:], -2e30), writes=["probs_all"])
        E("v", lambda o: o.memset(ceps[:], 1e-5), writes=["ceps"])
        E("g", lambda o: o.dma_start(
            out=gw_v[:, :, 0:8], in_=gwt_in.rearrange("(c p) f -> p c f", p=128)),
          writes=["gw_bf"], dma=True, tag="ld_gw", pin=True)
        E("y", lambda o: o.dma_start(out=gb_row[0:1, :], in_=gb_in),
          writes=["bias_row"], dma=True, tag="ld_gb", pin=True)
        E("v", lambda o: o.tensor_copy(out=gb_row_bf[:], in_=gb_row[:]),
          reads=["bias_row"], writes=["gb_row_bf"], tag="gb_cast")
        E("t", lambda o: o.matmul(pg[:, 0:8], ones_row[:], gb_row_bf[0:1, :],
                                  start=True, stop=True),
          reads=["gb_row_bf", "ones_row"], writes=["pg"], tag="gb_mm")
        E("v", lambda o: o.tensor_copy(out=bias_b[:], in_=pg[:, 0:8]),
          reads=["pg"], writes=["bias_b"], tag="gb_ev")

        def wload(bufkey, dst, src, tag):
            E("g", lambda o, d=dst, s_=src: o.dma_start(
                out=d, in_=s_.rearrange("(c p) f -> p c f", p=128)),
              writes=[bufkey], dma=True, tag=tag)

        wload("wA0", wA[0][:], wq_in, "ld_wq")
        wload("wA1", wA[1][:], wk_in, "ld_wk")

        def layernorm(par, src_buf, src_key, dst_buf, dst_key, tg):
            st_, mv_, rs_, nb_ = stats[par], mv[par], rstd[par], nbias[par]
            E("v", lambda o: o.bn_stats(out=st_[:, 0:6], in_=src_buf[:, 0:512]),
              reads=[src_key], writes=[f"stats{par}a"], tag=f"{tg}_bnsa")
            E("v", lambda o: o.bn_stats(out=st_[:, 6:12], in_=src_buf[:, 512:1024]),
              reads=[src_key], writes=[f"stats{par}b"], tag=f"{tg}_bnsb")
            E("v", lambda o: o.bn_aggr(out=mv_[:], in_=st_[:]),
              reads=[f"stats{par}a", f"stats{par}b"], writes=[f"mv{par}"],
              tag=f"{tg}_bna")
            E("s", lambda o: o.activation(out=rs_[:, 0:1], in_=mv_[:, 1:2],
                                          func=AF.Ln, bias=ceps[:], scale=1.0),
              reads=[f"mv{par}", "ceps"], writes=[f"rstd{par}0"], tag=f"{tg}_ln_")
            E("s", lambda o: o.activation(out=rs_[:, 1:2], in_=rs_[:, 0:1],
                                          func=AF.Exp, bias=0.0, scale=-0.5),
              reads=[f"rstd{par}0"], writes=[f"rstd{par}1"], tag=f"{tg}_rex")
            E("v", lambda o: o.tensor_scalar(out=nb_[:], in0=mv_[:, 0:1],
                                             scalar1=rs_[:, 1:2], scalar2=-1.0,
                                             op0=OP.mult, op1=OP.mult),
              reads=[f"mv{par}", f"rstd{par}1"], writes=[f"nbias{par}"],
              tag=f"{tg}_nb")
            E("s", lambda o: o.activation(out=dst_buf[:], in_=src_buf[:],
                                          func=AF.Identity, bias=nb_[:],
                                          scale=rs_[:, 1:2]),
              reads=[src_key, f"nbias{par}", f"rstd{par}1"], writes=[dst_key],
              tag=f"{tg}_ln")

        NTA = min(KT, NT) if 'A' in PH else 0

        def A_S1(i):
            par = i % 2
            xk = f"xb{i % 3}"
            lnk = f"ln1_bf{par}"
            E("y", lambda o, i=i: o.dma_start(
                out=xb[i % 3][:], in_=x_in[i*128:(i+1)*128, :]),
              writes=[xk], dma=True, tag=f"A_ldx{i}")
            layernorm(par, xb[i % 3], xk, ln1_bf[par], lnk, f"A{i}")
            for c in range(NC_K):
                E("t", lambda o, c=c, par=par: o.transpose(
                    ptb[:, c*128:(c+1)*128], ln1_bf[par][:, c*128:(c+1)*128],
                    ident[:]),
                  reads=[lnk, "ident"], writes=["ptb"], tag=f"A_tr{i}_{c}")
            E("v", lambda o, i=i: o.tensor_copy(
                out=ln1T_v[:, :, i*128:(i+1)*128], in_=ptb_v),
              reads=["ptb"], writes=[f"ln1T{i}"], tag=f"A_ev{i}")

        def A_S2(i):
            par = i % 2
            xk = f"xb{i % 3}"
            qk = f"qxst{par}"
            pslot = par * 2
            mc, zc = mcol[par], zcol[par]
            for h in range(2):
                bnk, bk = P[pslot + h], f"P{pslot + h}"
                for c in range(NC_K):
                    E("t", lambda o, h=h, c=c, i=i, bnk=bnk: o.matmul(
                        bnk[:], ln1T_v[:, c, i*128:(i+1)*128],
                        wA_v[0][:, c, h*512:(h+1)*512],
                        start=(c == 0), stop=(c == NC_K - 1)),
                      reads=[f"ln1T{i}", "wA0"], writes=[bk],
                      tag=f"A_q{i}_{h}_{c}")
            E("v", lambda o, b=P[pslot], mc=mc: o.reduce_max(
                out=mc[:, 0:1], in_=b[:], axis=AX),
              reads=[f"P{pslot}"], writes=[f"m0_{par}"], tag=f"A_m0{i}")
            E("v", lambda o, b=P[pslot+1], mc=mc: o.reduce_max(
                out=mc[:, 1:2], in_=b[:], axis=AX),
              reads=[f"P{pslot+1}"], writes=[f"m1_{par}"], tag=f"A_m1{i}")
            E("v", lambda o, mc=mc: o.tensor_scalar(
                out=mc[:, 3:4],
                in0=mc[:, 0:1], scalar1=mc[:, 1:2], scalar2=-1.0 / 32.0,
                op0=OP.max, op1=OP.mult),
              reads=[f"m0_{par}", f"m1_{par}"], writes=[f"negM{par}"],
              tag=f"A_nM{i}")
            for h in range(2):
                bnk, bk = P[pslot + h], f"P{pslot + h}"
                E("s", lambda o, h=h, bnk=bnk, par=par, mc=mc, zc=zc: o.activation(
                    out=expbuf[par][:, h*512:(h+1)*512], in_=bnk[:], func=AF.Exp,
                    bias=mc[:, 3:4], scale=1.0/32.0,
                    accum_out=zc[:, h:h+1]),
                  reads=[bk, f"negM{par}"], writes=[f"exp{par}_{h}", f"z{par}{h}"],
                  tag=f"A_e{i}_{h}")
            E("v", lambda o, zc=zc: o.tensor_tensor(
                out=zc[:, 2:3], in0=zc[:, 0:1], in1=zc[:, 1:2], op=OP.add),
              reads=[f"z{par}0", f"z{par}1"], writes=[f"Z{par}"], tag=f"A_Z{i}")
            E("v", lambda o, zc=zc: o.reciprocal(out=zc[:, 3:4], in_=zc[:, 2:3]),
              reads=[f"Z{par}"], writes=[f"rZ{par}"], tag=f"A_rZ{i}")
            for h in range(2):
                bnk, bk = P[pslot + h], f"P{pslot + h}"
                E("v", lambda o, h=h, bnk=bnk, par=par: o.tensor_tensor(
                    out=expbuf[par][:, h*512:(h+1)*512], in0=bnk[:],
                    in1=expbuf[par][:, h*512:(h+1)*512], op=OP.mult),
                  reads=[bk, f"exp{par}_{h}"], writes=[f"exp{par}_{h}"],
                  tag=f"A_w{i}_{h}")
                E("v", lambda o, h=h, par=par, zc=zc: o.scalar_tensor_tensor(
                    out=gq_acc[:, h*512:(h+1)*512],
                    in0=expbuf[par][:, h*512:(h+1)*512], scalar=zc[:, 3:4],
                    in1=gq_acc[:, h*512:(h+1)*512], op0=OP.mult, op1=OP.add),
                  reads=[f"exp{par}_{h}", f"rZ{par}", "gq_acc"],
                  writes=["gq_acc"], tag=f"A_g{i}_{h}")
                E("v", lambda o, h=h, bnk=bnk, par=par, i=i: o.tensor_tensor(
                    out=qxst[par][:, h*512:(h+1)*512], in0=bnk[:],
                    in1=xb[i % 3][:, h*512:(h+1)*512], op=OP.add),
                  reads=[bk, xk], writes=[qk], tag=f"A_qx{i}_{h}")
            E("s", lambda o, i=i, par=par: o.dma_start(
                out=qx_hbm[i*128:(i+1)*128, :], in_=qxst[par][:]),
              reads=[qk], writes=[f"qxhbm{i}"], dma=True, tag=f"A_stq{i}")

        if NTA:
            A_S1(0)
            for i in range(NTA):
                if i + 1 < NTA:
                    A_S1(i + 1)
                A_S2(i)

        wload("wA2", wA[2][:], wv_in, "ld_wv")
        qx_flush = pl.queue_flush("s")

        E("v", lambda o: o.tensor_copy(out=acc_bf[:], in_=gq_acc[:]),
          reads=["gq_acc"], writes=["acc_bf"], tag="gq_cast")
        for h in range(2):
            E("t", lambda o, h=h: o.matmul(P[4][0:1, :], ones_col[:],
                                           acc_bf[:, h*512:(h+1)*512],
                                           start=True, stop=True),
              reads=["acc_bf", "ones_col"], writes=["P4"], tag=f"gq_mm{h}")
            E("v", lambda o, h=h: o.tensor_copy(
                out=gq_row_bf[0:1, h*512:(h+1)*512], in_=P[4][0:1, :]),
              reads=["P4"], writes=["gq_row_bf"], tag=f"gq_ev{h}")
        for h in range(2):
            E("t", lambda o, h=h: o.matmul(P[4][:], ones_row[:],
                                           gq_row_bf[0:1, h*512:(h+1)*512],
                                           start=True, stop=True),
              reads=["gq_row_bf", "ones_row"], writes=["P4"], tag=f"gqb_mm{h}")
            E("v", lambda o, h=h: o.tensor_copy(
                out=gq_bcast[:, h*512:(h+1)*512], in_=P[4][:]),
              reads=["P4"], writes=["gq_bcast"], tag=f"gqb_ev{h}")

        for i in range(NT if 'B' in PH else 0):
            par = i % 2
            pslot = par * 2
            pb = pbuf[par]
            mc, zc = mcol[par], zcol[par]
            for h in range(2):
                bnk, bk = P[pslot + h], f"P{pslot + h}"
                for c in range(NC_K):
                    E("t", lambda o, h=h, c=c, i=i, bnk=bnk: o.matmul(
                        bnk[:], ln1T_v[:, c, i*128:(i+1)*128],
                        wA_v[1][:, c, h*512:(h+1)*512],
                        start=(c == 0), stop=(c == NC_K - 1)),
                      reads=[f"ln1T{i}", "wA1"], writes=[bk],
                      tag=f"B_k{i}_{h}_{c}")
                E("v", lambda o, h=h, bnk=bnk, pb=pb: o.tensor_tensor(
                    out=pb[:, h*512:(h+1)*512], in0=bnk[:],
                    in1=gq_bcast[:, h*512:(h+1)*512], op=OP.mult),
                  reads=[bk, "gq_bcast"], writes=[f"pb{par}_{h}"],
                  tag=f"B_p{i}_{h}")
            E("v", lambda o, pb=pb, mc=mc: o.reduce_max(
                out=mc[:, 2:3], in_=pb[:], axis=AX),
              reads=[f"pb{par}_0", f"pb{par}_1"], writes=[f"M{par}"],
              tag=f"B_M{i}")
            E("v", lambda o, mc=mc: o.tensor_scalar(
                out=mc[:, 3:4], in0=mc[:, 2:3], scalar1=-1.0/32.0, scalar2=None,
                op0=OP.mult),
              reads=[f"M{par}"], writes=[f"negM{par}"], tag=f"B_nM{i}")
            E("s", lambda o, pb=pb, par=par, mc=mc, zc=zc: o.activation(
                out=expbuf[par][:], in_=pb[:], func=AF.Exp,
                bias=mc[:, 3:4], scale=1.0/32.0, accum_out=zc[:, 2:3]),
              reads=[f"pb{par}_0", f"pb{par}_1", f"negM{par}"],
              writes=[f"exp{par}_0", f"exp{par}_1", f"Z{par}"], tag=f"B_e{i}")
            E("v", lambda o, zc=zc: o.reciprocal(out=zc[:, 3:4], in_=zc[:, 2:3]),
              reads=[f"Z{par}"], writes=[f"rZ{par}"], tag=f"B_rZ{i}")
            E("v", lambda o, pb=pb, par=par: o.tensor_tensor(
                out=expbuf[par][:], in0=pb[:], in1=expbuf[par][:], op=OP.mult),
              reads=[f"pb{par}_0", f"pb{par}_1", f"exp{par}_0", f"exp{par}_1"],
              writes=[f"exp{par}_0", f"exp{par}_1"], tag=f"B_w{i}")
            E("v", lambda o, par=par, zc=zc: o.scalar_tensor_tensor(
                out=gk_acc[:], in0=expbuf[par][:], scalar=zc[:, 3:4],
                in1=gk_acc[:], op0=OP.mult, op1=OP.add),
              reads=[f"exp{par}_0", f"exp{par}_1", f"rZ{par}", "gk_acc"],
              writes=["gk_acc"], tag=f"B_g{i}")

        wload("wA0", wA[0][:], wr_in, "ld_wr")

        E("v", lambda o: o.tensor_copy(out=acc_bf[:], in_=gk_acc[:]),
          reads=["gk_acc"], writes=["acc_bf"], tag="gk_cast")
        for c in range(NC_K):
            E("t", lambda o, c=c: o.transpose(ptb[:, c*128:(c+1)*128],
                                              acc_bf[:, c*128:(c+1)*128],
                                              ident[:]),
              reads=["acc_bf", "ident"], writes=["ptb"], tag=f"gk_tr{c}")
            E("v", lambda o, c=c: o.reduce_sum(out=gk_f[:, c:c+1],
                                               in_=ptb[:, c*128:(c+1)*128],
                                               axis=AX),
              reads=["ptb"], writes=["gk_f"], tag=f"gk_rs{c}")

        def C_vu(ch):
            t0 = ch * SUB
            cpar = ch % 2
            uv, uk = u_v[cpar], f"u{cpar}"
            for j in range(NC_K):
                bnk, bk = P[j % 2], f"P{j % 2}"
                for c in range(NC_K):
                    E("t", lambda o, j=j, c=c, t0=t0, bnk=bnk: o.matmul(
                        bnk[:], wA_v[2][:, c, j*128:(j+1)*128],
                        ln1T_v[:, c, t0:t0+SUB],
                        start=(c == 0), stop=(c == NC_K - 1)),
                      reads=["wA2", f"ln1T{t0//128}"],
                      writes=[bk], tag=f"C_v{ch}_{j}_{c}")
                E("v", lambda o, j=j, bnk=bnk, uv=uv: o.tensor_scalar(
                    out=uv[:, j, :], in0=bnk[:], scalar1=gk_f[:, j:j+1],
                    scalar2=None, op0=OP.mult),
                  reads=[bk, "gk_f"], writes=[uk], tag=f"C_u{ch}_{j}")

        def C_S1(ti):
            ch, tt = ti // 4, ti % 4
            par = ti % 2
            uv, uk = u_v[ch % 2], f"u{ch % 2}"
            for h in range(2):
                bnk = P[2 + (tt % 2) * 2 + h]
                bk = f"P{2 + (tt % 2) * 2 + h}"
                for j in range(NC_K):
                    E("t", lambda o, j=j, tt=tt, h=h, bnk=bnk, uv=uv: o.matmul(
                        bnk[:], uv[:, j, tt*128:(tt+1)*128],
                        wA_v[0][:, j, h*512:(h+1)*512],
                        start=(j == 0), stop=(j == NC_K - 1)),
                      reads=[uk, "wA0"], writes=[bk],
                      tag=f"C_r{ch}_{tt}_{h}_{j}")
            E("y", lambda o, ti=ti, par=par: o.dma_start(
                out=qxst[par][:], in_=qx_hbm[ti*128:(ti+1)*128, :]),
              writes=[f"qxst{par}"], dma=True, extra_deps=[qx_flush],
              tag=f"C_ldqx{ti}")
            for h in range(2):
                bnk = P[2 + (tt % 2) * 2 + h]
                bk = f"P{2 + (tt % 2) * 2 + h}"
                E("v", lambda o, h=h, par=par, bnk=bnk: o.scalar_tensor_tensor(
                    out=x1t[par][:, h*512:(h+1)*512], in0=bnk[:],
                    scalar=1.0, in1=qxst[par][:, h*512:(h+1)*512],
                    op0=OP.mult, op1=OP.add),
                  reads=[bk, f"qxst{par}"], writes=[f"x1t{par}"],
                  tag=f"C_x1{ti}_{h}")
            E("s", lambda o, ti=ti, par=par: o.dma_start(
                out=x1_hbm[ti*128:(ti+1)*128, :], in_=x1t[par][:]),
              reads=[f"x1t{par}"], writes=[f"x1hbm{ti}"], dma=True,
              tag=f"C_stx1{ti}")
            layernorm(par, x1t[par], f"x1t{par}", ln2_bf[par], f"ln2_bf{par}",
                      f"C{ti}")

        def C_S2(ti):
            ch, tt = ti // 4, ti % 4
            par = ti % 2
            for c in range(NC_K):
                E("t", lambda o, c=c, par=par: o.transpose(
                    ptb[:, c*128:(c+1)*128],
                    ln2_bf[par][:, c*128:(c+1)*128], ident[:]),
                  reads=[f"ln2_bf{par}", "ident"], writes=["ptb"],
                  tag=f"C_tr{ti}_{c}")
            E("v", lambda o, tt=tt: o.tensor_copy(
                out=ln2st_v[:, :, tt*128:(tt+1)*128], in_=ptb_v),
              reads=["ptb"], writes=[f"ln2st{tt}"], tag=f"C_ev{ti}")
            for c in range(NC_K):
                E("t", lambda o, c=c, tt=tt: o.matmul(
                    pg[:, 0:6], ln2st_v[:, c, tt*128:(tt+1)*128],
                    gw_v[:, c, 0:6], start=(c == 0), stop=(c == NC_K - 1)),
                  reads=[f"ln2st{tt}", "gw_bf"], writes=["pg"],
                  tag=f"C_gmm{ti}_{c}")
            E("v", lambda o, par=par: o.tensor_tensor(
                out=ptmp[par][:, 0:6], in0=pg[:, 0:6],
                in1=bias_b[:, 0:6], op=OP.add),
              reads=["pg", "bias_b"], writes=[f"ptmp{par}"], tag=f"C_gb{ti}")
            E("s", lambda o, par=par: o.activation(
                out=etmp[par][:, 0:6], in_=ptmp[par][:, 0:6],
                func=AF.Exp, bias=0.0, scale=-1.0),
              reads=[f"ptmp{par}"], writes=[f"etmp{par}"], tag=f"C_sg1{ti}")
            E("v", lambda o, par=par: o.tensor_scalar(
                out=etmp[par][:, 0:6], in0=etmp[par][:, 0:6], scalar1=1.0,
                scalar2=None, op0=OP.add),
              reads=[f"etmp{par}"], writes=[f"etmp{par}"], tag=f"C_sg2{ti}")
            E("v", lambda o, ti=ti, par=par: o.reciprocal(
                out=probs_all[:, ti*8:ti*8+6], in_=etmp[par][:, 0:6]),
              reads=[f"etmp{par}"], writes=[f"probs{ti}"], tag=f"C_sg3{ti}")
            o8 = ti * 8
            Pt = probs_all[:, o8:o8+8]
            ma, mb = mx8a[par], mx8b[par]
            p2_, p3_ = p2buf[par], p3buf[par]
            fm, sm, gc = fmask[par], smask[par], gcol[par]
            E("v", lambda o, Pt=Pt, ma=ma: o.reduce_max(
                out=ma[:, 0:1], in_=Pt, axis=AX),
              reads=[f"probs{ti}", "probs_all"], writes=[f"mx8a_v{par}"],
              tag=f"G_m1{ti}")
            E("v", lambda o, Pt=Pt, ma=ma, p2_=p2_: o.match_replace(
                out=p2_[:], in_to_replace=ma[:], in_values=Pt,
                imm_value=-2e30),
              reads=[f"probs{ti}", "probs_all", f"mx8a{par}",
                     f"mx8a_v{par}"],
              writes=[f"p2buf{par}"], tag=f"G_mr1{ti}")
            E("v", lambda o, mb=mb, p2_=p2_: o.reduce_max(
                out=mb[:, 0:1], in_=p2_[:], axis=AX),
              reads=[f"p2buf{par}"], writes=[f"mx8b_v{par}"],
              tag=f"G_m2{ti}")
            E("v", lambda o, Pt=Pt, fm=fm, p2_=p2_: o.tensor_tensor(
                out=fm[:], in0=Pt, in1=p2_[:], op=OP.not_equal),
              reads=[f"probs{ti}", "probs_all", f"p2buf{par}"],
              writes=[f"fmask{par}"], tag=f"G_f{ti}")
            E("v", lambda o, mb=mb, p2_=p2_, p3_=p3_: o.match_replace(
                out=p3_[:], in_to_replace=mb[:], in_values=p2_[:],
                imm_value=-2e30),
              reads=[f"p2buf{par}", f"mx8b{par}", f"mx8b_v{par}"],
              writes=[f"p3buf{par}"], tag=f"G_mr2{ti}")
            E("v", lambda o, sm=sm, p2_=p2_, p3_=p3_: o.tensor_tensor(
                out=sm[:], in0=p2_[:], in1=p3_[:], op=OP.not_equal),
              reads=[f"p2buf{par}", f"p3buf{par}"], writes=[f"smask{par}"],
              tag=f"G_s{ti}")
            E("v", lambda o, gc=gc, ma=ma, mb=mb: o.scalar_tensor_tensor(
                out=gc[:, 0:1], in0=ma[:, 0:1], scalar=1e-8, in1=mb[:, 0:1],
                op0=OP.add, op1=OP.add),
              reads=[f"mx8a_v{par}", f"mx8b_v{par}"], writes=[f"den{par}"],
              tag=f"G_d{ti}")
            E("v", lambda o, gc=gc: o.reciprocal(out=gc[:, 2:3],
                                                 in_=gc[:, 0:1]),
              reads=[f"den{par}"], writes=[f"rden{par}"], tag=f"G_rd{ti}")
            E("v", lambda o, gc=gc, ma=ma: o.tensor_scalar(
                out=gc[:, 3:4], in0=ma[:, 0:1], scalar1=gc[:, 2:3],
                scalar2=None, op0=OP.mult),
              reads=[f"mx8a_v{par}", f"rden{par}"], writes=[f"gm{par}"],
              tag=f"G_g1{ti}")
            E("v", lambda o, o8=o8, fm=fm, gc=gc: o.tensor_scalar(
                out=ge_all[:, o8:o8+8], in0=fm[:], scalar1=gc[:, 3:4],
                scalar2=None, op0=OP.mult),
              reads=[f"fmask{par}", f"gm{par}"], writes=[f"ge{ti}"],
              tag=f"G_ge1{ti}")
            E("v", lambda o, gc=gc, mb=mb: o.tensor_scalar(
                out=gc[:, 3:4], in0=mb[:, 0:1], scalar1=gc[:, 2:3],
                scalar2=None, op0=OP.mult),
              reads=[f"mx8b_v{par}", f"rden{par}"], writes=[f"gm{par}"],
              tag=f"G_g2{ti}")
            E("v", lambda o, o8=o8, sm=sm, gc=gc: o.scalar_tensor_tensor(
                out=ge_all[:, o8:o8+8], in0=sm[:], scalar=gc[:, 3:4],
                in1=ge_all[:, o8:o8+8], op0=OP.mult, op1=OP.add),
              reads=[f"smask{par}", f"gm{par}", f"ge{ti}"],
              writes=[f"ge{ti}"], tag=f"G_ge2{ti}")
            if tt == 3:
                E("s", lambda o, ch=ch: o.dma_start(
                    out=ln2t_hbm_v[:, :, ch*SUB:(ch+1)*SUB],
                    in_=ln2st_v[:, :, :]),
                  reads=[f"ln2st{q}" for q in range(4)],
                  writes=[f"ln2thbm{ch}"], dma=True, tag=f"C_stln2{ch}")

        if 'C' in PH:
            pend = None
            for ch in range(NSUB):
                C_vu(ch)
                for tt in range(4):
                    ti = ch * 4 + tt
                    C_S1(ti)
                    if pend is not None:
                        C_S2(pend)
                    pend = ti
            C_S2(pend)

        es_abc.close()
        pl.barrier()

        y_sb = sb("y_sb", [128, NT * DM], BF16)
        wF = [sb(f"wF{i}", [128, NC_K * DM], BF16) for i in range(4)]
        slab = [sb(f"slab{i}", [128, NC_K * SUB], BF16) for i in range(2)]
        h1s = [sb(f"h1s{i}", [128, SUB], F32) for i in range(2)]
        hb = sb("hb", [128, NC_K * SUB], BF16)
        x1ld = [sb(f"x1ld{i}", [128, DM], F32) for i in range(2)]
        outt = [sb(f"outt{i}", [128, DM], F32) for i in range(2)]
        wF_v = [w[:].rearrange("p (c f) -> p c f", c=NC_K) for w in wF]
        slab_v = [s_[:].rearrange("p (c t) -> p c t", c=NC_K) for s_ in slab]
        hb_v = hb[:].rearrange("p (c t) -> p c t", c=NC_K)
        y_v = y_sb[:].rearrange("p (n f) -> p n f", n=NT)

        E("v", lambda o: o.memset(y_sb[:], 0.0), writes=["y_sb"], tag="D_zy")

        branches = [(ew1t_in[e], ew3t_in[e], ew2t_in[e], e) for e in range(NE)]
        branches.append((fc1t_in, fc2t_in, fc3t_in, None))
        if "D" not in PH:
            branches = branches[-1:] if "d" in PH else []

        wslot = {}

        def emit_branch_loads(b):
            w1, w3, w2, _e = branches[b]
            for j, wsrc in enumerate((w1, w3, w2)):
                slot = (3 * b + j) % 4
                wslot[(b, j)] = slot
                E("g", lambda o, slot=slot, s_=wsrc: o.dma_start(
                    out=wF[slot][:], in_=s_.rearrange("(c p) f -> p c f", p=128)),
                  writes=[f"slot{slot}"], dma=True, tag=f"ld_w{b}_{j}")

        if branches:
            emit_branch_loads(0)

        allhb = [f"hb{j}" for j in range(NC_K)]
        for b, (w1, w3, w2, e) in enumerate(branches):
            shared = e is None
            s1, s3, s2 = wslot[(b, 0)], wslot[(b, 1)], wslot[(b, 2)]
            for sb_i in range(NSUB):
                sl, slv = slab[sb_i % 2], slab_v[sb_i % 2]
                slk = f"slab{sb_i % 2}"
                E("y", lambda o, sb_i=sb_i, sl=sl: o.dma_start(
                    out=sl[:], in_=ln2t_hbm_v[:, :, sb_i*SUB:(sb_i+1)*SUB]),
                  writes=[slk], dma=True, tag=f"D_lds{b}_{sb_i}")
                for j in range(NC_K):
                    b1k, b1n = P[j % 2], f"P{j % 2}"
                    b3k, b3n = P[2 + j % 2], f"P{2 + j % 2}"
                    for c in range(NC_K):
                        E("t", lambda o, j=j, c=c, b1k=b1k, slv=slv, s1=s1: o.matmul(
                            b1k[:], wF_v[s1][:, c, j*128:(j+1)*128],
                            slv[:, c, :],
                            start=(c == 0), stop=(c == NC_K - 1)),
                          reads=[f"slot{s1}", slk], writes=[b1n],
                          tag=f"D_h1_{b}_{sb_i}_{j}_{c}")
                    for c in range(NC_K):
                        E("t", lambda o, j=j, c=c, b3k=b3k, slv=slv, s3=s3: o.matmul(
                            b3k[:], wF_v[s3][:, c, j*128:(j+1)*128],
                            slv[:, c, :],
                            start=(c == 0), stop=(c == NC_K - 1)),
                          reads=[f"slot{s3}", slk], writes=[b3n],
                          tag=f"D_h3_{b}_{sb_i}_{j}_{c}")
                    E("s", lambda o, j=j, b1k=b1k: o.activation(
                        out=h1s[j % 2][:], in_=b1k[:], func=AF.Silu),
                      reads=[b1n], writes=[f"h1s{j % 2}"],
                      tag=f"D_si{b}_{sb_i}_{j}")
                    E("v", lambda o, j=j, b3k=b3k: o.tensor_tensor(
                        out=hb_v[:, j, :], in0=h1s[j % 2][:], in1=b3k[:],
                        op=OP.mult),
                      reads=[f"h1s{j % 2}", b3n], writes=[f"hb{j}"],
                      tag=f"D_hb{b}_{sb_i}_{j}")
                for tt in range(4):
                    ti = sb_i * 4 + tt
                    if shared:
                        E("y", lambda o, ti=ti: o.dma_start(
                            out=x1ld[ti % 2][:],
                            in_=x1_hbm[ti*128:(ti+1)*128, :]),
                          writes=[f"x1ld{ti % 2}"],
                          dma=True, tag=f"D_ldx1{ti}")
                    for h in range(2):
                        bk, bn = P[4 + (2*tt + h) % 2], f"P{4 + (2*tt + h) % 2}"
                        for j in range(NC_K):
                            E("t", lambda o, j=j, tt=tt, h=h, bk=bk, s2=s2: o.matmul(
                                bk[:], hb_v[:, j, tt*128:(tt+1)*128],
                                wF_v[s2][:, j, h*512:(h+1)*512],
                                start=(j == 0), stop=(j == NC_K - 1)),
                              reads=allhb + [f"slot{s2}"], writes=[bn],
                              tag=f"D_y{b}_{sb_i}_{tt}_{h}_{j}")
                        if not shared:
                            E("v", lambda o, ti=ti, h=h, bk=bk, e=e: o.scalar_tensor_tensor(
                                out=y_v[:, ti, h*512:(h+1)*512], in0=bk[:],
                                scalar=ge_all[:, ti*8+e:ti*8+e+1],
                                in1=y_v[:, ti, h*512:(h+1)*512],
                                op0=OP.mult, op1=OP.add),
                              reads=[bn, f"ge{ti}", "y_sb"], writes=["y_sb"],
                              tag=f"D_ye{b}_{ti}_{h}")
                        else:
                            E("v", lambda o, ti=ti, h=h, bk=bk: o.scalar_tensor_tensor(
                                out=outt[ti % 2][:, h*512:(h+1)*512], in0=bk[:],
                                scalar=1.0, in1=y_v[:, ti, h*512:(h+1)*512],
                                op0=OP.mult, op1=OP.add),
                              reads=[bn, "y_sb"], writes=[f"outt{ti % 2}"],
                              tag=f"D_yz{ti}_{h}")
                            E("v", lambda o, ti=ti, h=h: o.tensor_tensor(
                                out=outt[ti % 2][:, h*512:(h+1)*512],
                                in0=outt[ti % 2][:, h*512:(h+1)*512],
                                in1=x1ld[ti % 2][:, h*512:(h+1)*512],
                                op=OP.add),
                              reads=[f"outt{ti % 2}", f"x1ld{ti % 2}"],
                              writes=[f"outt{ti % 2}"], tag=f"D_out{ti}_{h}")
                    if shared:
                        E("s", lambda o, ti=ti: o.dma_start(
                            out=out_hbm[ti*128:(ti+1)*128, :],
                            in_=outt[ti % 2][:]),
                          reads=[f"outt{ti % 2}"], writes=[f"outhbm{ti}"],
                          dma=True, tag=f"D_sto{ti}")
                if sb_i == 0 and b + 1 < len(branches):
                    emit_branch_loads(b + 1)

        pl.barrier()
        if not branches:
            E("v", lambda o: o.memset(outt[0][:], 0.0), writes=["outt0"],
              tag="dbg_z")
            for ti in range(NT):
                E("s", lambda o, ti=ti: o.dma_start(
                    out=out_hbm[ti*128:(ti+1)*128, :], in_=outt[0][:]),
                  reads=["outt0"], dma=True, tag=f"dbg_sto{ti}")
            pl.barrier()
        pl.finalize(nc, es, block)
    return nc


_NC_CACHE = {}


def build_cached(S=4096):
    if S not in _NC_CACHE:
        _NC_CACHE[S] = build_program(S)
    return _NC_CACHE[S]


def make_in_maps(L0_attn_x, L0_attn_Wq, L0_attn_Wk, L0_attn_Wv, L0_attn_Wr,
                 L0_ffn_gate_w, L0_ffn_gate_b, L0_ffn_ew1, L0_ffn_ew2,
                 L0_ffn_ew3, L0_ffn_fc1, L0_ffn_fc2, L0_ffn_fc3):
    x = np.ascontiguousarray(np.asarray(L0_attn_x, dtype=np.float32))
    f32 = lambda a: np.ascontiguousarray(np.asarray(a, dtype=np.float32))
    T2 = lambda a: np.ascontiguousarray(np.swapaxes(f32(a), -1, -2))
    gwt = np.zeros((DM, 8), np.float32)
    gwt[:, 0:6] = T2(L0_ffn_gate_w)
    gb = np.zeros((1, 8), np.float32)
    gb[0, 0:6] = f32(L0_ffn_gate_b)
    shared = {"wq": f32(L0_attn_Wq), "wk": f32(L0_attn_Wk),
              "wv": f32(L0_attn_Wv), "wr": f32(L0_attn_Wr),
              "gwt": gwt, "gb": gb,
              "ew1t": T2(L0_ffn_ew1), "ew3t": T2(L0_ffn_ew3),
              "ew2t": T2(L0_ffn_ew2),
              "fc1t": T2(L0_ffn_fc1), "fc2t": T2(L0_ffn_fc2),
              "fc3t": T2(L0_ffn_fc3),
              "ident": np.eye(128, dtype=ml_dtypes.bfloat16)}
    return [dict(shared, x=np.ascontiguousarray(x[b]))
            for b in range(x.shape[0])]


def kernel(**inputs):
    in_maps = make_in_maps(**inputs)
    nc = build_cached(4096)
    res = run_bass_kernel_spmd(nc, in_maps, core_ids=list(range(len(in_maps))))
    return np.stack([r["out"] for r in res.results], axis=0)


# version 08
# speedup vs baseline: 2.6213x; 1.1444x over previous
import os
import numpy as np
import ml_dtypes
from contextlib import ExitStack
import concourse.bass as bass
import concourse.mybir as mybir
from concourse.bass_utils import run_bass_kernel_spmd

F32 = mybir.dt.float32
BF16 = mybir.dt.bfloat16
AX = mybir.AxisListType.X
OP = mybir.AluOpType
AF = mybir.ActivationFunctionType

DM = 1024
NC_K = DM // 128
NE = 6
SUB = 512


class Plan:

    ENGINES = ("t", "v", "s", "y", "g")
    NSLOT = 8

    def __init__(self):
        self.steps = {e: [] for e in self.ENGINES}
        self.last_write = {}
        self.readers = {}
        self.dma_count = {e: 0 for e in self.ENGINES}
        self.pin_count = 0
        self.seq = 0

    def emit(self, eng, fn, reads=(), writes=(), dma=False, tag="", extra_deps=(),
             pin=False):
        idx = len(self.steps[eng])
        deps = set(extra_deps)
        for k in reads:
            w = self.last_write.get(k)
            if w is not None:
                deps.add(w)
        for k in writes:
            w = self.last_write.get(k)
            if w is not None:
                deps.add(w)
            last = {}
            for (de, di) in self.readers.get(k, ()):
                if self.steps[de][di]["dma"]:
                    deps.add((de, di))
                elif di > last.get(de, -1):
                    last[de] = di
            deps.update(last.items())
        deps = {(de, di) for (de, di) in deps if not (de == eng and di >= idx)}
        if eng == "t":
            deps = {(de, di) for (de, di) in deps if de != "t"}
        st = {"fn": fn, "deps": sorted(deps), "dma": dma, "tag": tag,
              "seq": self.seq}
        if dma:
            if pin:
                st["dslot"] = ("pin", self.pin_count)
                self.pin_count += 1
            else:
                st["dslot"] = self.dma_count[eng] % self.NSLOT
                self.dma_count[eng] += 1
        self.seq += 1
        self.steps[eng].append(st)
        ev = (eng, idx)
        for k in reads:
            self.readers.setdefault(k, []).append(ev)
        for k in writes:
            self.last_write[k] = ev
            self.readers[k] = []
        return ev

    def barrier(self):
        lasts = []
        for e in self.ENGINES:
            if not self.steps[e]:
                continue
            lasts.append((e, len(self.steps[e]) - 1))
            seen = set()
            for i in range(len(self.steps[e]) - 1, -1, -1):
                st = self.steps[e][i]
                if st["dma"] and st["dslot"] not in seen:
                    seen.add(st["dslot"])
                    lasts.append((e, i))

        def mknop():
            def f(obj):
                return obj.nop()
            return f
        for e in self.ENGINES:
            deps = [d for d in lasts if d[0] != e] +                    [d for d in lasts if d[0] == e and self.steps[e][d[1]]["dma"]]
            deps = [d for d in deps if not (d[0] == e and d[1] >= len(self.steps[e]))]
            self.steps[e].append({"fn": mknop(), "deps": sorted(set(deps)),
                                  "dma": False, "tag": "barrier",
                                  "seq": self.seq})
            self.seq += 1

    def queue_flush(self, eng):
        deps = []
        seen = set()
        for i in range(len(self.steps[eng]) - 1, -1, -1):
            st = self.steps[eng][i]
            if st["dma"] and st["dslot"] not in seen:
                seen.add(st["dslot"])
                deps.append((eng, i))

        def f(obj):
            return obj.nop()
        idx = len(self.steps[eng])
        self.steps[eng].append({"fn": f, "deps": sorted(deps), "dma": False,
                                "tag": f"flush_{eng}", "seq": self.seq})
        self.seq += 1
        return (eng, idx)

    def final_wait(self, eng, deps):
        def f(obj):
            return obj.nop()
        self.steps[eng].append({"fn": f, "deps": sorted(set(deps)),
                                "dma": False, "tag": "final", "seq": self.seq})
        self.seq += 1

    def finalize(self, nc, es, block):
        kseq = int(os.environ.get("KSEQ", "0"))
        if kseq:
            for e in self.ENGINES:
                self.steps[e] = [st for st in self.steps[e]
                                 if st["seq"] < kseq]
            self.last_write = {k: v for k, v in self.last_write.items()
                               if v[1] < len(self.steps[v[0]])}
            self.readers = {}
            self.barrier()
        consumers = {}
        for e in self.ENGINES:
            for i, st in enumerate(self.steps[e]):
                if st["tag"] in ("barrier", "final") or                         st["tag"].startswith("flush"):
                    continue
                for d in st["deps"]:
                    consumers.setdefault(d, []).append((e, i))
        def reduce_consumers(evs):
            keep, last = [], {}
            for (de, di) in evs:
                if self.steps[de][di]["dma"]:
                    keep.append((de, di))
                elif di > last.get(de, -1):
                    last[de] = di
            return keep + [(de, di) for de, di in last.items()]

        for e in self.ENGINES:
            prev_by_slot = {}
            for i, st in enumerate(self.steps[e]):
                if not st["dma"] or isinstance(st["dslot"], tuple):
                    continue
                sl = st["dslot"]
                pv = prev_by_slot.get(sl)
                if pv is not None:
                    add = [pv] + reduce_consumers(consumers.get(pv, []))
                    myseq = st["seq"]
                    extra = []
                    for (de, di) in add:
                        if de == e and di >= i:
                            continue
                        dseq = self.steps[de][di]["seq"]
                        assert dseq < myseq, (
                            "slot reuse forward dep", e, i, st["tag"],
                            de, di, self.steps[de][di]["tag"])
                        extra.append((de, di))
                    st["deps"] = sorted(set(st["deps"]) | set(extra))
                prev_by_slot[sl] = (e, i)
        need_mark = {e: set() for e in self.ENGINES}
        for e in self.ENGINES:
            for st in self.steps[e]:
                for (de, di) in st["deps"]:
                    need_mark[de].add(di)
        sems = {(e, p): es.enter_context(nc.semaphore(f"sem_{e}{p}"))
                for e in self.ENGINES for p in range(2)}
        dsems = {}
        for e in self.ENGINES:
            if self.dma_count[e]:
                n = min(self.NSLOT, self.dma_count[e])
                for sl in range(n):
                    dsems[(e, sl)] = es.enter_context(
                        nc.semaphore(f"dsem_{e}{sl}"))
            for i, st in enumerate(self.steps[e]):
                if st.get("dslot") is not None and isinstance(st["dslot"], tuple):
                    dsems[(e, st["dslot"])] = es.enter_context(
                        nc.semaphore(f"dsem_{e}p{st['dslot'][1]}"))
        ev_sem = {}
        slot_counts = {}
        for e in self.ENGINES:
            mcnt = [0, 0]
            nmark = 0
            for i, st in enumerate(self.steps[e]):
                if st["dma"]:
                    sl = st["dslot"]
                    c = slot_counts.get((e, sl), 0) + 16
                    slot_counts[(e, sl)] = c
                    st["inc_sem"] = dsems[(e, sl)]
                    st["inc"] = 16
                    ev_sem[(e, i)] = (st["inc_sem"], c)
                elif i in need_mark[e]:
                    p = nmark % 2
                    nmark += 1
                    mcnt[p] += 1
                    st["inc_sem"] = sems[(e, p)]
                    st["inc"] = 1
                    st["chain"] = (sems[(e, p)], mcnt[p] - 1)
                    ev_sem[(e, i)] = (sems[(e, p)], mcnt[p])
                else:
                    st["inc"] = 0
                    ev_sem[(e, i)] = None

        def emit_stream(eng, obj):
            water = {}
            for i, st in enumerate(self.steps[eng]):
                waits = {}
                for d in st["deps"]:
                    sv = ev_sem[d]
                    assert sv is not None, (eng, i, d, st["tag"])
                    sem, v = sv
                    if v > waits.get(id(sem), (None, -1))[1]:
                        waits[id(sem)] = (sem, v)
                ch = st.get("chain")
                if ch is not None and ch[1] > 0:
                    if ch[1] > waits.get(id(ch[0]), (None, -1))[1]:
                        waits[id(ch[0])] = ch
                for sem, v in waits.values():
                    if water.get(id(sem), -1) >= v:
                        continue
                    obj.wait_ge(sem, v)
                    water[id(sem)] = v
                ins = st["fn"](obj)
                if st["inc"]:
                    ins.then_inc(st["inc_sem"], st["inc"])

        @block.tensor
        def _(t):
            emit_stream("t", t)

        @block.vector
        def _(v):
            emit_stream("v", v)

        @block.scalar
        def _(s):
            emit_stream("s", s)

        @block.sync
        def _(y):
            emit_stream("y", y)

        @block.gpsimd
        def _(g):
            emit_stream("g", g)


def build_program(S=4096):
    PH = os.environ.get("KPHASES", "ABCD")
    KT = int(os.environ.get("KTILES", "0")) or 10**9
    NT = S // 128
    NSUB = S // SUB
    nc = bass.Bass("TRN2", target_bir_lowering=False)

    x_in = nc.dram_tensor("x", [S, DM], F32, kind="ExternalInput").ap()
    wq_in = nc.dram_tensor("wq", [DM, DM], F32, kind="ExternalInput").ap()
    wk_in = nc.dram_tensor("wk", [DM, DM], F32, kind="ExternalInput").ap()
    wv_in = nc.dram_tensor("wv", [DM, DM], F32, kind="ExternalInput").ap()
    wr_in = nc.dram_tensor("wr", [DM, DM], F32, kind="ExternalInput").ap()
    gwt_in = nc.dram_tensor("gwt", [DM, 8], F32, kind="ExternalInput").ap()
    gb_in = nc.dram_tensor("gb", [1, 8], F32, kind="ExternalInput").ap()
    ew1t_in = nc.dram_tensor("ew1t", [NE, DM, DM], F32, kind="ExternalInput").ap()
    ew3t_in = nc.dram_tensor("ew3t", [NE, DM, DM], F32, kind="ExternalInput").ap()
    ew2t_in = nc.dram_tensor("ew2t", [NE, DM, DM], F32, kind="ExternalInput").ap()
    fc1t_in = nc.dram_tensor("fc1t", [DM, DM], F32, kind="ExternalInput").ap()
    fc2t_in = nc.dram_tensor("fc2t", [DM, DM], F32, kind="ExternalInput").ap()
    fc3t_in = nc.dram_tensor("fc3t", [DM, DM], F32, kind="ExternalInput").ap()
    ident_in = nc.dram_tensor("ident", [128, 128], BF16, kind="ExternalInput").ap()
    out_hbm = nc.dram_tensor("out", [S, DM], F32, kind="ExternalOutput").ap()
    qx_hbm = nc.dram_tensor("qx_cache", [S, DM], F32, kind="Internal").ap()
    x1_hbm = nc.dram_tensor("x1_cache", [S, DM], F32, kind="Internal").ap()
    ln2t_hbm = nc.dram_tensor("ln2t_cache", [NC_K, 128, S], BF16, kind="Internal").ap()
    ln2t_hbm_v = ln2t_hbm.rearrange("c p t -> p c t")

    es = ExitStack()
    es_abc = ExitStack()
    with es:
        sb = lambda n, shp, dt: es.enter_context(nc.sbuf_tensor(n, shp, dt))
        sba = lambda n, shp, dt: es_abc.enter_context(nc.sbuf_tensor(n, shp, dt))
        psa = lambda n, shp, dt: es.enter_context(nc.psum_tensor(n, shp, dt))
        sb2 = lambda n, shp, dt: [sb(n + "0", shp, dt), sb(n + "1", shp, dt)]
        sba2 = lambda n, shp, dt: [sba(n + "0", shp, dt), sba(n + "1", shp, dt)]

        ident = sb("ident_sb", [128, 128], BF16)
        ones_col = sb("ones_col", [128, 1], BF16)
        ones_row = sb("ones_row", [1, 128], BF16)
        bias_b = sb("bias_b", [128, 8], F32)
        gw_bf = sb("gw_bf", [128, NC_K * 8], BF16)
        probs_all = sb("probs_all", [128, NT * 8], F32)
        ge_all = sb("ge_all", [128, NT * 8], F32)
        mx8a = sb2("mx8a", [128, 8], F32)
        mx8b = sb2("mx8b", [128, 8], F32)
        p2buf = sb2("p2buf", [128, 8], F32)
        p3buf = sb2("p3buf", [128, 8], F32)
        fmask = sb2("fmask", [128, 8], F32)
        smask = sb2("smask", [128, 8], F32)
        gcol = sb2("gcol", [128, 4], F32)
        ptmp = sb2("ptmp", [128, 8], F32)
        etmp = sb2("etmp", [128, 8], F32)
        stats = sb2("stats", [128, 12], F32)
        mv = sb2("mv", [128, 2], F32)
        rstd = sb2("rstd", [128, 2], F32)
        nbias = sb2("nbias", [128, 1], F32)
        mcol = sb2("mcol", [128, 4], F32)
        zcol = sb2("zcol", [128, 4], F32)
        gk_f = sb("gk_f", [128, 8], F32)
        ceps = sb("ceps", [128, 1], F32)
        gb_row = sb("gb_row", [1, 8], F32)
        gb_row_bf = sb("gb_row_bf", [1, 8], BF16)

        P = [psa(f"P{i}", [128, 512], F32) for i in range(6)]
        ptb = psa("ptb", [128, 1024], BF16)
        pg = psa("pg", [128, 8], F32)

        block = es.enter_context(nc.Block())

        pl = Plan()
        E = pl.emit

        ln1T = sba("ln1T", [128, NC_K * S], BF16)
        wA = [sba(f"wA{i}", [128, NC_K * DM], BF16) for i in range(3)]
        xb = [sba(f"xb{i}", [128, DM], F32) for i in range(3)]
        qxst = sba2("qxst", [128, DM], F32)
        ln1_bf = sba2("ln1_bf", [128, DM], BF16)
        expbuf = sba2("expbuf", [128, DM], F32)
        pbuf = sba2("pbuf", [128, DM], F32)
        u_slab = sba2("u_slab", [128, NC_K * SUB], BF16)
        x1t = sba2("x1t", [128, DM], F32)
        ln2_bf = sba2("ln2_bf", [128, DM], BF16)
        ln2st = sba("ln2st", [128, NC_K * SUB], BF16)
        gq_acc = sba("gq_acc", [128, DM], F32)
        gk_acc = sba("gk_acc", [128, DM], F32)
        acc_bf = sba("acc_bf", [128, DM], BF16)
        gq_row_bf = sba("gq_row_bf", [1, DM], BF16)
        gq_bcast = sba("gq_bcast", [128, DM], BF16)

        ln1T_v = ln1T[:].rearrange("p (c t) -> p c t", c=NC_K)
        wA_v = [w[:].rearrange("p (c f) -> p c f", c=NC_K) for w in wA]
        u_v = [u[:].rearrange("p (c t) -> p c t", c=NC_K) for u in u_slab]
        ln2st_v = ln2st[:].rearrange("p (c t) -> p c t", c=NC_K)
        gw_v = gw_bf[:].rearrange("p (c f) -> p c f", c=NC_K)
        ptb_v = ptb[:].rearrange("p (c t) -> p c t", c=NC_K)

        E("y", lambda o: o.dma_start(out=ident[:], in_=ident_in),
          writes=["ident"], dma=True, tag="ld_ident", pin=True)
        E("v", lambda o: o.memset(ones_col[:], 1.0), writes=["ones_col"])
        E("v", lambda o: o.memset(ones_row[:], 1.0), writes=["ones_row"])
        E("v", lambda o: o.memset(gq_acc[:], 0.0), writes=["gq_acc"])
        E("v", lambda o: o.memset(gk_acc[:], 0.0), writes=["gk_acc"])
        E("v", lambda o: o.memset(mx8a[0][:], -3e30), writes=["mx8a0"])
        E("v", lambda o: o.memset(mx8a[1][:], -3e30), writes=["mx8a1"])
        E("v", lambda o: o.memset(mx8b[0][:], -3e30), writes=["mx8b0"])
        E("v", lambda o: o.memset(mx8b[1][:], -3e30), writes=["mx8b1"])
        E("v", lambda o: o.memset(probs_all[:], -2e30), writes=["probs_all"])
        E("v", lambda o: o.memset(ceps[:], 1e-5), writes=["ceps"])
        E("g", lambda o: o.dma_start(
            out=gw_v[:, :, 0:8], in_=gwt_in.rearrange("(c p) f -> p c f", p=128)),
          writes=["gw_bf"], dma=True, tag="ld_gw", pin=True)
        E("y", lambda o: o.dma_start(out=gb_row[0:1, :], in_=gb_in),
          writes=["bias_row"], dma=True, tag="ld_gb", pin=True)
        E("v", lambda o: o.tensor_copy(out=gb_row_bf[:], in_=gb_row[:]),
          reads=["bias_row"], writes=["gb_row_bf"], tag="gb_cast")
        E("t", lambda o: o.matmul(pg[:, 0:8], ones_row[:], gb_row_bf[0:1, :],
                                  start=True, stop=True),
          reads=["gb_row_bf", "ones_row"], writes=["pg"], tag="gb_mm")
        E("v", lambda o: o.tensor_copy(out=bias_b[:], in_=pg[:, 0:8]),
          reads=["pg"], writes=["bias_b"], tag="gb_ev")

        def wload(bufkey, dst, src, tag):
            E("g", lambda o, d=dst, s_=src: o.dma_start(
                out=d, in_=s_.rearrange("(c p) f -> p c f", p=128)),
              writes=[bufkey], dma=True, tag=tag)

        wload("wA0", wA[0][:], wq_in, "ld_wq")
        wload("wA1", wA[1][:], wk_in, "ld_wk")

        def layernorm(par, src_buf, src_key, dst_buf, dst_key, tg):
            st_, mv_, rs_, nb_ = stats[par], mv[par], rstd[par], nbias[par]
            E("v", lambda o: o.bn_stats(out=st_[:, 0:6], in_=src_buf[:, 0:512]),
              reads=[src_key], writes=[f"stats{par}a"], tag=f"{tg}_bnsa")
            E("v", lambda o: o.bn_stats(out=st_[:, 6:12], in_=src_buf[:, 512:1024]),
              reads=[src_key], writes=[f"stats{par}b"], tag=f"{tg}_bnsb")
            E("v", lambda o: o.bn_aggr(out=mv_[:], in_=st_[:]),
              reads=[f"stats{par}a", f"stats{par}b"], writes=[f"mv{par}"],
              tag=f"{tg}_bna")
            E("s", lambda o: o.activation(out=rs_[:, 0:1], in_=mv_[:, 1:2],
                                          func=AF.Ln, bias=ceps[:], scale=1.0),
              reads=[f"mv{par}", "ceps"], writes=[f"rstd{par}0"], tag=f"{tg}_ln_")
            E("s", lambda o: o.activation(out=rs_[:, 1:2], in_=rs_[:, 0:1],
                                          func=AF.Exp, bias=0.0, scale=-0.5),
              reads=[f"rstd{par}0"], writes=[f"rstd{par}1"], tag=f"{tg}_rex")
            E("v", lambda o: o.tensor_scalar(out=nb_[:], in0=mv_[:, 0:1],
                                             scalar1=rs_[:, 1:2], scalar2=-1.0,
                                             op0=OP.mult, op1=OP.mult),
              reads=[f"mv{par}", f"rstd{par}1"], writes=[f"nbias{par}"],
              tag=f"{tg}_nb")
            E("s", lambda o: o.activation(out=dst_buf[:], in_=src_buf[:],
                                          func=AF.Identity, bias=nb_[:],
                                          scale=rs_[:, 1:2]),
              reads=[src_key, f"nbias{par}", f"rstd{par}1"], writes=[dst_key],
              tag=f"{tg}_ln")

        NTA = min(KT, NT) if 'A' in PH else 0

        def A_S1(i):
            par = i % 2
            xk = f"xb{i % 3}"
            lnk = f"ln1_bf{par}"
            E("y", lambda o, i=i: o.dma_start(
                out=xb[i % 3][:], in_=x_in[i*128:(i+1)*128, :]),
              writes=[xk], dma=True, tag=f"A_ldx{i}")
            layernorm(par, xb[i % 3], xk, ln1_bf[par], lnk, f"A{i}")
            for c in range(NC_K):
                E("t", lambda o, c=c, par=par: o.transpose(
                    ptb[:, c*128:(c+1)*128], ln1_bf[par][:, c*128:(c+1)*128],
                    ident[:]),
                  reads=[lnk, "ident"], writes=["ptb"], tag=f"A_tr{i}_{c}")
            E("v", lambda o, i=i: o.tensor_copy(
                out=ln1T_v[:, :, i*128:(i+1)*128], in_=ptb_v),
              reads=["ptb"], writes=[f"ln1T{i}"], tag=f"A_ev{i}")

        def A_S2(i):
            par = i % 2
            xk = f"xb{i % 3}"
            qk = f"qxst{par}"
            pslot = par * 2
            mc, zc = mcol[par], zcol[par]
            for h in range(2):
                bnk, bk = P[pslot + h], f"P{pslot + h}"
                for c in range(NC_K):
                    E("t", lambda o, h=h, c=c, i=i, bnk=bnk: o.matmul(
                        bnk[:], ln1T_v[:, c, i*128:(i+1)*128],
                        wA_v[0][:, c, h*512:(h+1)*512],
                        start=(c == 0), stop=(c == NC_K - 1)),
                      reads=[f"ln1T{i}", "wA0"], writes=[bk],
                      tag=f"A_q{i}_{h}_{c}")
            E("v", lambda o, b=P[pslot], mc=mc: o.reduce_max(
                out=mc[:, 0:1], in_=b[:], axis=AX),
              reads=[f"P{pslot}"], writes=[f"m0_{par}"], tag=f"A_m0{i}")
            E("v", lambda o, b=P[pslot+1], mc=mc: o.reduce_max(
                out=mc[:, 1:2], in_=b[:], axis=AX),
              reads=[f"P{pslot+1}"], writes=[f"m1_{par}"], tag=f"A_m1{i}")
            E("v", lambda o, mc=mc: o.tensor_scalar(
                out=mc[:, 3:4],
                in0=mc[:, 0:1], scalar1=mc[:, 1:2], scalar2=-1.0 / 32.0,
                op0=OP.max, op1=OP.mult),
              reads=[f"m0_{par}", f"m1_{par}"], writes=[f"negM{par}"],
              tag=f"A_nM{i}")
            for h in range(2):
                bnk, bk = P[pslot + h], f"P{pslot + h}"
                E("s", lambda o, h=h, bnk=bnk, par=par, mc=mc, zc=zc: o.activation(
                    out=expbuf[par][:, h*512:(h+1)*512], in_=bnk[:], func=AF.Exp,
                    bias=mc[:, 3:4], scale=1.0/32.0,
                    accum_out=zc[:, h:h+1]),
                  reads=[bk, f"negM{par}"], writes=[f"exp{par}_{h}", f"z{par}{h}"],
                  tag=f"A_e{i}_{h}")
            E("v", lambda o, zc=zc: o.tensor_tensor(
                out=zc[:, 2:3], in0=zc[:, 0:1], in1=zc[:, 1:2], op=OP.add),
              reads=[f"z{par}0", f"z{par}1"], writes=[f"Z{par}"], tag=f"A_Z{i}")
            E("v", lambda o, zc=zc: o.reciprocal(out=zc[:, 3:4], in_=zc[:, 2:3]),
              reads=[f"Z{par}"], writes=[f"rZ{par}"], tag=f"A_rZ{i}")
            for h in range(2):
                bnk, bk = P[pslot + h], f"P{pslot + h}"
                E("v", lambda o, h=h, bnk=bnk, par=par: o.tensor_tensor(
                    out=expbuf[par][:, h*512:(h+1)*512], in0=bnk[:],
                    in1=expbuf[par][:, h*512:(h+1)*512], op=OP.mult),
                  reads=[bk, f"exp{par}_{h}"], writes=[f"exp{par}_{h}"],
                  tag=f"A_w{i}_{h}")
                E("v", lambda o, h=h, par=par, zc=zc: o.scalar_tensor_tensor(
                    out=gq_acc[:, h*512:(h+1)*512],
                    in0=expbuf[par][:, h*512:(h+1)*512], scalar=zc[:, 3:4],
                    in1=gq_acc[:, h*512:(h+1)*512], op0=OP.mult, op1=OP.add),
                  reads=[f"exp{par}_{h}", f"rZ{par}", "gq_acc"],
                  writes=["gq_acc"], tag=f"A_g{i}_{h}")
                E("v", lambda o, h=h, bnk=bnk, par=par, i=i: o.tensor_tensor(
                    out=qxst[par][:, h*512:(h+1)*512], in0=bnk[:],
                    in1=xb[i % 3][:, h*512:(h+1)*512], op=OP.add),
                  reads=[bk, xk], writes=[qk], tag=f"A_qx{i}_{h}")
            E("s", lambda o, i=i, par=par: o.dma_start(
                out=qx_hbm[i*128:(i+1)*128, :], in_=qxst[par][:]),
              reads=[qk], writes=[f"qxhbm{i}"], dma=True, tag=f"A_stq{i}")

        if NTA:
            A_S1(0)
            for i in range(NTA):
                if i + 1 < NTA:
                    A_S1(i + 1)
                A_S2(i)

        wload("wA2", wA[2][:], wv_in, "ld_wv")
        qx_flush = pl.queue_flush("s")

        E("v", lambda o: o.tensor_copy(out=acc_bf[:], in_=gq_acc[:]),
          reads=["gq_acc"], writes=["acc_bf"], tag="gq_cast")
        for h in range(2):
            E("t", lambda o, h=h: o.matmul(P[4][0:1, :], ones_col[:],
                                           acc_bf[:, h*512:(h+1)*512],
                                           start=True, stop=True),
              reads=["acc_bf", "ones_col"], writes=["P4"], tag=f"gq_mm{h}")
            E("v", lambda o, h=h: o.tensor_copy(
                out=gq_row_bf[0:1, h*512:(h+1)*512], in_=P[4][0:1, :]),
              reads=["P4"], writes=["gq_row_bf"], tag=f"gq_ev{h}")
        for h in range(2):
            E("t", lambda o, h=h: o.matmul(P[4][:], ones_row[:],
                                           gq_row_bf[0:1, h*512:(h+1)*512],
                                           start=True, stop=True),
              reads=["gq_row_bf", "ones_row"], writes=["P4"], tag=f"gqb_mm{h}")
            E("v", lambda o, h=h: o.tensor_copy(
                out=gq_bcast[:, h*512:(h+1)*512], in_=P[4][:]),
              reads=["P4"], writes=["gq_bcast"], tag=f"gqb_ev{h}")

        for i in range(NT if 'B' in PH else 0):
            par = i % 2
            pslot = par * 2
            pb = pbuf[par]
            mc, zc = mcol[par], zcol[par]
            for h in range(2):
                bnk, bk = P[pslot + h], f"P{pslot + h}"
                for c in range(NC_K):
                    E("t", lambda o, h=h, c=c, i=i, bnk=bnk: o.matmul(
                        bnk[:], ln1T_v[:, c, i*128:(i+1)*128],
                        wA_v[1][:, c, h*512:(h+1)*512],
                        start=(c == 0), stop=(c == NC_K - 1)),
                      reads=[f"ln1T{i}", "wA1"], writes=[bk],
                      tag=f"B_k{i}_{h}_{c}")
                E("v", lambda o, h=h, bnk=bnk, pb=pb: o.tensor_tensor(
                    out=pb[:, h*512:(h+1)*512], in0=bnk[:],
                    in1=gq_bcast[:, h*512:(h+1)*512], op=OP.mult),
                  reads=[bk, "gq_bcast"], writes=[f"pb{par}_{h}"],
                  tag=f"B_p{i}_{h}")
            E("v", lambda o, pb=pb, mc=mc: o.reduce_max(
                out=mc[:, 2:3], in_=pb[:], axis=AX),
              reads=[f"pb{par}_0", f"pb{par}_1"], writes=[f"M{par}"],
              tag=f"B_M{i}")
            E("v", lambda o, mc=mc: o.tensor_scalar(
                out=mc[:, 3:4], in0=mc[:, 2:3], scalar1=-1.0/32.0, scalar2=None,
                op0=OP.mult),
              reads=[f"M{par}"], writes=[f"negM{par}"], tag=f"B_nM{i}")
            E("s", lambda o, pb=pb, par=par, mc=mc, zc=zc: o.activation(
                out=expbuf[par][:], in_=pb[:], func=AF.Exp,
                bias=mc[:, 3:4], scale=1.0/32.0, accum_out=zc[:, 2:3]),
              reads=[f"pb{par}_0", f"pb{par}_1", f"negM{par}"],
              writes=[f"exp{par}_0", f"exp{par}_1", f"Z{par}"], tag=f"B_e{i}")
            E("v", lambda o, zc=zc: o.reciprocal(out=zc[:, 3:4], in_=zc[:, 2:3]),
              reads=[f"Z{par}"], writes=[f"rZ{par}"], tag=f"B_rZ{i}")
            E("v", lambda o, pb=pb, par=par: o.tensor_tensor(
                out=expbuf[par][:], in0=pb[:], in1=expbuf[par][:], op=OP.mult),
              reads=[f"pb{par}_0", f"pb{par}_1", f"exp{par}_0", f"exp{par}_1"],
              writes=[f"exp{par}_0", f"exp{par}_1"], tag=f"B_w{i}")
            E("v", lambda o, par=par, zc=zc: o.scalar_tensor_tensor(
                out=gk_acc[:], in0=expbuf[par][:], scalar=zc[:, 3:4],
                in1=gk_acc[:], op0=OP.mult, op1=OP.add),
              reads=[f"exp{par}_0", f"exp{par}_1", f"rZ{par}", "gk_acc"],
              writes=["gk_acc"], tag=f"B_g{i}")

        wload("wA0", wA[0][:], wr_in, "ld_wr")

        E("v", lambda o: o.tensor_copy(out=acc_bf[:], in_=gk_acc[:]),
          reads=["gk_acc"], writes=["acc_bf"], tag="gk_cast")
        for c in range(NC_K):
            E("t", lambda o, c=c: o.transpose(ptb[:, c*128:(c+1)*128],
                                              acc_bf[:, c*128:(c+1)*128],
                                              ident[:]),
              reads=["acc_bf", "ident"], writes=["ptb"], tag=f"gk_tr{c}")
            E("v", lambda o, c=c: o.reduce_sum(out=gk_f[:, c:c+1],
                                               in_=ptb[:, c*128:(c+1)*128],
                                               axis=AX),
              reads=["ptb"], writes=["gk_f"], tag=f"gk_rs{c}")

        def C_vu(ch):
            t0 = ch * SUB
            cpar = ch % 2
            uv, uk = u_v[cpar], f"u{cpar}"
            for j in range(NC_K):
                bnk, bk = P[j % 2], f"P{j % 2}"
                for c in range(NC_K):
                    E("t", lambda o, j=j, c=c, t0=t0, bnk=bnk: o.matmul(
                        bnk[:], wA_v[2][:, c, j*128:(j+1)*128],
                        ln1T_v[:, c, t0:t0+SUB],
                        start=(c == 0), stop=(c == NC_K - 1)),
                      reads=["wA2", f"ln1T{t0//128}"],
                      writes=[bk], tag=f"C_v{ch}_{j}_{c}")
                E("v", lambda o, j=j, bnk=bnk, uv=uv: o.tensor_scalar(
                    out=uv[:, j, :], in0=bnk[:], scalar1=gk_f[:, j:j+1],
                    scalar2=None, op0=OP.mult),
                  reads=[bk, "gk_f"], writes=[uk], tag=f"C_u{ch}_{j}")

        def C_S1(ti):
            ch, tt = ti // 4, ti % 4
            par = ti % 2
            uv, uk = u_v[ch % 2], f"u{ch % 2}"
            for h in range(2):
                bnk = P[2 + (tt % 2) * 2 + h]
                bk = f"P{2 + (tt % 2) * 2 + h}"
                for j in range(NC_K):
                    E("t", lambda o, j=j, tt=tt, h=h, bnk=bnk, uv=uv: o.matmul(
                        bnk[:], uv[:, j, tt*128:(tt+1)*128],
                        wA_v[0][:, j, h*512:(h+1)*512],
                        start=(j == 0), stop=(j == NC_K - 1)),
                      reads=[uk, "wA0"], writes=[bk],
                      tag=f"C_r{ch}_{tt}_{h}_{j}")
            E("y", lambda o, ti=ti, par=par: o.dma_start(
                out=qxst[par][:], in_=qx_hbm[ti*128:(ti+1)*128, :]),
              writes=[f"qxst{par}"], dma=True, extra_deps=[qx_flush],
              tag=f"C_ldqx{ti}")
            for h in range(2):
                bnk = P[2 + (tt % 2) * 2 + h]
                bk = f"P{2 + (tt % 2) * 2 + h}"
                E("v", lambda o, h=h, par=par, bnk=bnk: o.scalar_tensor_tensor(
                    out=x1t[par][:, h*512:(h+1)*512], in0=bnk[:],
                    scalar=1.0, in1=qxst[par][:, h*512:(h+1)*512],
                    op0=OP.mult, op1=OP.add),
                  reads=[bk, f"qxst{par}"], writes=[f"x1t{par}"],
                  tag=f"C_x1{ti}_{h}")
            E("s", lambda o, ti=ti, par=par: o.dma_start(
                out=x1_hbm[ti*128:(ti+1)*128, :], in_=x1t[par][:]),
              reads=[f"x1t{par}"], writes=[f"x1hbm{ti}"], dma=True,
              tag=f"C_stx1{ti}")
            layernorm(par, x1t[par], f"x1t{par}", ln2_bf[par], f"ln2_bf{par}",
                      f"C{ti}")

        def C_S2(ti):
            ch, tt = ti // 4, ti % 4
            par = ti % 2
            for c in range(NC_K):
                E("t", lambda o, c=c, par=par: o.transpose(
                    ptb[:, c*128:(c+1)*128],
                    ln2_bf[par][:, c*128:(c+1)*128], ident[:]),
                  reads=[f"ln2_bf{par}", "ident"], writes=["ptb"],
                  tag=f"C_tr{ti}_{c}")
            E("v", lambda o, tt=tt: o.tensor_copy(
                out=ln2st_v[:, :, tt*128:(tt+1)*128], in_=ptb_v),
              reads=["ptb"], writes=[f"ln2st{tt}"], tag=f"C_ev{ti}")
            for c in range(NC_K):
                E("t", lambda o, c=c, tt=tt: o.matmul(
                    pg[:, 0:6], ln2st_v[:, c, tt*128:(tt+1)*128],
                    gw_v[:, c, 0:6], start=(c == 0), stop=(c == NC_K - 1)),
                  reads=[f"ln2st{tt}", "gw_bf"], writes=["pg"],
                  tag=f"C_gmm{ti}_{c}")
            E("v", lambda o, par=par: o.tensor_tensor(
                out=ptmp[par][:, 0:6], in0=pg[:, 0:6],
                in1=bias_b[:, 0:6], op=OP.add),
              reads=["pg", "bias_b"], writes=[f"ptmp{par}"], tag=f"C_gb{ti}")
            E("s", lambda o, par=par: o.activation(
                out=etmp[par][:, 0:6], in_=ptmp[par][:, 0:6],
                func=AF.Exp, bias=0.0, scale=-1.0),
              reads=[f"ptmp{par}"], writes=[f"etmp{par}"], tag=f"C_sg1{ti}")
            E("v", lambda o, par=par: o.tensor_scalar(
                out=etmp[par][:, 0:6], in0=etmp[par][:, 0:6], scalar1=1.0,
                scalar2=None, op0=OP.add),
              reads=[f"etmp{par}"], writes=[f"etmp{par}"], tag=f"C_sg2{ti}")
            E("v", lambda o, ti=ti, par=par: o.reciprocal(
                out=probs_all[:, ti*8:ti*8+6], in_=etmp[par][:, 0:6]),
              reads=[f"etmp{par}"], writes=[f"probs{ti}"], tag=f"C_sg3{ti}")
            o8 = ti * 8
            Pt = probs_all[:, o8:o8+8]
            ma, mb = mx8a[par], mx8b[par]
            p2_, p3_ = p2buf[par], p3buf[par]
            fm, sm, gc = fmask[par], smask[par], gcol[par]
            E("v", lambda o, Pt=Pt, ma=ma: o.reduce_max(
                out=ma[:, 0:1], in_=Pt, axis=AX),
              reads=[f"probs{ti}", "probs_all"], writes=[f"mx8a_v{par}"],
              tag=f"G_m1{ti}")
            E("v", lambda o, Pt=Pt, ma=ma, p2_=p2_: o.match_replace(
                out=p2_[:], in_to_replace=ma[:], in_values=Pt,
                imm_value=-2e30),
              reads=[f"probs{ti}", "probs_all", f"mx8a{par}",
                     f"mx8a_v{par}"],
              writes=[f"p2buf{par}"], tag=f"G_mr1{ti}")
            E("v", lambda o, mb=mb, p2_=p2_: o.reduce_max(
                out=mb[:, 0:1], in_=p2_[:], axis=AX),
              reads=[f"p2buf{par}"], writes=[f"mx8b_v{par}"],
              tag=f"G_m2{ti}")
            E("v", lambda o, Pt=Pt, fm=fm, p2_=p2_: o.tensor_tensor(
                out=fm[:], in0=Pt, in1=p2_[:], op=OP.not_equal),
              reads=[f"probs{ti}", "probs_all", f"p2buf{par}"],
              writes=[f"fmask{par}"], tag=f"G_f{ti}")
            E("v", lambda o, mb=mb, p2_=p2_, p3_=p3_: o.match_replace(
                out=p3_[:], in_to_replace=mb[:], in_values=p2_[:],
                imm_value=-2e30),
              reads=[f"p2buf{par}", f"mx8b{par}", f"mx8b_v{par}"],
              writes=[f"p3buf{par}"], tag=f"G_mr2{ti}")
            E("v", lambda o, sm=sm, p2_=p2_, p3_=p3_: o.tensor_tensor(
                out=sm[:], in0=p2_[:], in1=p3_[:], op=OP.not_equal),
              reads=[f"p2buf{par}", f"p3buf{par}"], writes=[f"smask{par}"],
              tag=f"G_s{ti}")
            E("v", lambda o, gc=gc, ma=ma, mb=mb: o.scalar_tensor_tensor(
                out=gc[:, 0:1], in0=ma[:, 0:1], scalar=1e-8, in1=mb[:, 0:1],
                op0=OP.add, op1=OP.add),
              reads=[f"mx8a_v{par}", f"mx8b_v{par}"], writes=[f"den{par}"],
              tag=f"G_d{ti}")
            E("v", lambda o, gc=gc: o.reciprocal(out=gc[:, 2:3],
                                                 in_=gc[:, 0:1]),
              reads=[f"den{par}"], writes=[f"rden{par}"], tag=f"G_rd{ti}")
            E("v", lambda o, gc=gc, ma=ma: o.tensor_scalar(
                out=gc[:, 3:4], in0=ma[:, 0:1], scalar1=gc[:, 2:3],
                scalar2=None, op0=OP.mult),
              reads=[f"mx8a_v{par}", f"rden{par}"], writes=[f"gm{par}"],
              tag=f"G_g1{ti}")
            E("v", lambda o, o8=o8, fm=fm, gc=gc: o.tensor_scalar(
                out=ge_all[:, o8:o8+8], in0=fm[:], scalar1=gc[:, 3:4],
                scalar2=None, op0=OP.mult),
              reads=[f"fmask{par}", f"gm{par}"], writes=[f"ge{ti}"],
              tag=f"G_ge1{ti}")
            E("v", lambda o, gc=gc, mb=mb: o.tensor_scalar(
                out=gc[:, 3:4], in0=mb[:, 0:1], scalar1=gc[:, 2:3],
                scalar2=None, op0=OP.mult),
              reads=[f"mx8b_v{par}", f"rden{par}"], writes=[f"gm{par}"],
              tag=f"G_g2{ti}")
            E("v", lambda o, o8=o8, sm=sm, gc=gc: o.scalar_tensor_tensor(
                out=ge_all[:, o8:o8+8], in0=sm[:], scalar=gc[:, 3:4],
                in1=ge_all[:, o8:o8+8], op0=OP.mult, op1=OP.add),
              reads=[f"smask{par}", f"gm{par}", f"ge{ti}"],
              writes=[f"ge{ti}"], tag=f"G_ge2{ti}")
            if tt == 3:
                E("s", lambda o, ch=ch: o.dma_start(
                    out=ln2t_hbm_v[:, :, ch*SUB:(ch+1)*SUB],
                    in_=ln2st_v[:, :, :]),
                  reads=[f"ln2st{q}" for q in range(4)],
                  writes=[f"ln2thbm{ch}"], dma=True, tag=f"C_stln2{ch}")

        if 'C' in PH:
            pend = None
            for ch in range(NSUB):
                C_vu(ch)
                for tt in range(4):
                    ti = ch * 4 + tt
                    C_S1(ti)
                    if pend is not None:
                        C_S2(pend)
                    pend = ti
            C_S2(pend)

        es_abc.close()
        pl.barrier()

        y_sb = sb("y_sb", [128, NT * DM], BF16)
        wF = [sb(f"wF{i}", [128, NC_K * DM], BF16) for i in range(4)]
        slab = [sb(f"slab{i}", [128, NC_K * SUB], BF16) for i in range(2)]
        h1s = [sb(f"h1s{i}", [128, SUB], F32) for i in range(2)]
        hb = sb("hb", [128, NC_K * SUB], BF16)
        x1ld = [sb(f"x1ld{i}", [128, DM], F32) for i in range(2)]
        outt = [sb(f"outt{i}", [128, DM], F32) for i in range(2)]
        wF_v = [w[:].rearrange("p (c f) -> p c f", c=NC_K) for w in wF]
        slab_v = [s_[:].rearrange("p (c t) -> p c t", c=NC_K) for s_ in slab]
        hb_v = hb[:].rearrange("p (c t) -> p c t", c=NC_K)
        y_v = y_sb[:].rearrange("p (n f) -> p n f", n=NT)

        E("v", lambda o: o.memset(y_sb[:], 0.0), writes=["y_sb"], tag="D_zy")

        branches = [(ew1t_in[e], ew3t_in[e], ew2t_in[e], e) for e in range(NE)]
        branches.append((fc1t_in, fc2t_in, fc3t_in, None))
        if "D" not in PH:
            branches = branches[-1:] if "d" in PH else []

        wslot = {}

        def emit_branch_loads(b):
            w1, w3, w2, _e = branches[b]
            for j, wsrc in enumerate((w1, w3, w2)):
                slot = (3 * b + j) % 4
                wslot[(b, j)] = slot
                E("g", lambda o, slot=slot, s_=wsrc: o.dma_start(
                    out=wF[slot][:], in_=s_.rearrange("(c p) f -> p c f", p=128)),
                  writes=[f"slot{slot}"], dma=True, tag=f"ld_w{b}_{j}")

        if branches:
            emit_branch_loads(0)

        allhb = [f"hb{j}" for j in range(NC_K)]
        for b, (w1, w3, w2, e) in enumerate(branches):
            shared = e is None
            s1, s3, s2 = wslot[(b, 0)], wslot[(b, 1)], wslot[(b, 2)]
            for sb_i in range(NSUB):
                sl, slv = slab[sb_i % 2], slab_v[sb_i % 2]
                slk = f"slab{sb_i % 2}"
                E("y", lambda o, sb_i=sb_i, sl=sl: o.dma_start(
                    out=sl[:], in_=ln2t_hbm_v[:, :, sb_i*SUB:(sb_i+1)*SUB]),
                  writes=[slk], dma=True, tag=f"D_lds{b}_{sb_i}")
                for j in range(NC_K):
                    b1k, b1n = P[j % 2], f"P{j % 2}"
                    b3k, b3n = P[2 + j % 2], f"P{2 + j % 2}"
                    for c in range(NC_K):
                        E("t", lambda o, j=j, c=c, b1k=b1k, slv=slv, s1=s1: o.matmul(
                            b1k[:], wF_v[s1][:, c, j*128:(j+1)*128],
                            slv[:, c, :],
                            start=(c == 0), stop=(c == NC_K - 1)),
                          reads=[f"slot{s1}", slk], writes=[b1n],
                          tag=f"D_h1_{b}_{sb_i}_{j}_{c}")
                    for c in range(NC_K):
                        E("t", lambda o, j=j, c=c, b3k=b3k, slv=slv, s3=s3: o.matmul(
                            b3k[:], wF_v[s3][:, c, j*128:(j+1)*128],
                            slv[:, c, :],
                            start=(c == 0), stop=(c == NC_K - 1)),
                          reads=[f"slot{s3}", slk], writes=[b3n],
                          tag=f"D_h3_{b}_{sb_i}_{j}_{c}")
                    E("s", lambda o, j=j, b1k=b1k: o.activation(
                        out=h1s[j % 2][:], in_=b1k[:], func=AF.Silu),
                      reads=[b1n], writes=[f"h1s{j % 2}"],
                      tag=f"D_si{b}_{sb_i}_{j}")
                    E("v", lambda o, j=j, b3k=b3k: o.tensor_tensor(
                        out=hb_v[:, j, :], in0=h1s[j % 2][:], in1=b3k[:],
                        op=OP.mult),
                      reads=[f"h1s{j % 2}", b3n], writes=[f"hb{j}"],
                      tag=f"D_hb{b}_{sb_i}_{j}")
                for tt in range(4):
                    ti = sb_i * 4 + tt
                    if shared:
                        E("y", lambda o, ti=ti: o.dma_start(
                            out=x1ld[ti % 2][:],
                            in_=x1_hbm[ti*128:(ti+1)*128, :]),
                          writes=[f"x1ld{ti % 2}"],
                          dma=True, tag=f"D_ldx1{ti}")
                    for h in range(2):
                        bk, bn = P[4 + (2*tt + h) % 2], f"P{4 + (2*tt + h) % 2}"
                        for j in range(NC_K):
                            E("t", lambda o, j=j, tt=tt, h=h, bk=bk, s2=s2: o.matmul(
                                bk[:], hb_v[:, j, tt*128:(tt+1)*128],
                                wF_v[s2][:, j, h*512:(h+1)*512],
                                start=(j == 0), stop=(j == NC_K - 1)),
                              reads=allhb + [f"slot{s2}"], writes=[bn],
                              tag=f"D_y{b}_{sb_i}_{tt}_{h}_{j}")
                        if not shared:
                            E("v", lambda o, ti=ti, h=h, bk=bk, e=e: o.scalar_tensor_tensor(
                                out=y_v[:, ti, h*512:(h+1)*512], in0=bk[:],
                                scalar=ge_all[:, ti*8+e:ti*8+e+1],
                                in1=y_v[:, ti, h*512:(h+1)*512],
                                op0=OP.mult, op1=OP.add),
                              reads=[bn, f"ge{ti}", "y_sb"], writes=["y_sb"],
                              tag=f"D_ye{b}_{ti}_{h}")
                        else:
                            E("v", lambda o, ti=ti, h=h, bk=bk: o.scalar_tensor_tensor(
                                out=outt[ti % 2][:, h*512:(h+1)*512], in0=bk[:],
                                scalar=1.0, in1=y_v[:, ti, h*512:(h+1)*512],
                                op0=OP.mult, op1=OP.add),
                              reads=[bn, "y_sb"], writes=[f"outt{ti % 2}"],
                              tag=f"D_yz{ti}_{h}")
                            E("v", lambda o, ti=ti, h=h: o.tensor_tensor(
                                out=outt[ti % 2][:, h*512:(h+1)*512],
                                in0=outt[ti % 2][:, h*512:(h+1)*512],
                                in1=x1ld[ti % 2][:, h*512:(h+1)*512],
                                op=OP.add),
                              reads=[f"outt{ti % 2}", f"x1ld{ti % 2}"],
                              writes=[f"outt{ti % 2}"], tag=f"D_out{ti}_{h}")
                    if shared:
                        E("s", lambda o, ti=ti: o.dma_start(
                            out=out_hbm[ti*128:(ti+1)*128, :],
                            in_=outt[ti % 2][:]),
                          reads=[f"outt{ti % 2}"], writes=[f"outhbm{ti}"],
                          dma=True, tag=f"D_sto{ti}")
                if sb_i == 0 and b + 1 < len(branches):
                    emit_branch_loads(b + 1)

        pl.barrier()
        if not branches:
            E("v", lambda o: o.memset(outt[0][:], 0.0), writes=["outt0"],
              tag="dbg_z")
            for ti in range(NT):
                E("s", lambda o, ti=ti: o.dma_start(
                    out=out_hbm[ti*128:(ti+1)*128, :], in_=outt[0][:]),
                  reads=["outt0"], dma=True, tag=f"dbg_sto{ti}")
            pl.barrier()
        pl.finalize(nc, es, block)
    return nc


_NC_CACHE = {}


def build_cached(S=4096):
    if S not in _NC_CACHE:
        _NC_CACHE[S] = build_program(S)
    return _NC_CACHE[S]


def make_in_maps(L0_attn_x, L0_attn_Wq, L0_attn_Wk, L0_attn_Wv, L0_attn_Wr,
                 L0_ffn_gate_w, L0_ffn_gate_b, L0_ffn_ew1, L0_ffn_ew2,
                 L0_ffn_ew3, L0_ffn_fc1, L0_ffn_fc2, L0_ffn_fc3):
    x = np.ascontiguousarray(np.asarray(L0_attn_x, dtype=np.float32))
    f32 = lambda a: np.ascontiguousarray(np.asarray(a, dtype=np.float32))
    T2 = lambda a: np.ascontiguousarray(np.swapaxes(f32(a), -1, -2))
    gwt = np.zeros((DM, 8), np.float32)
    gwt[:, 0:6] = T2(L0_ffn_gate_w)
    gb = np.zeros((1, 8), np.float32)
    gb[0, 0:6] = f32(L0_ffn_gate_b)
    shared = {"wq": f32(L0_attn_Wq), "wk": f32(L0_attn_Wk),
              "wv": f32(L0_attn_Wv), "wr": f32(L0_attn_Wr),
              "gwt": gwt, "gb": gb,
              "ew1t": T2(L0_ffn_ew1), "ew3t": T2(L0_ffn_ew3),
              "ew2t": T2(L0_ffn_ew2),
              "fc1t": T2(L0_ffn_fc1), "fc2t": T2(L0_ffn_fc2),
              "fc3t": T2(L0_ffn_fc3),
              "ident": np.eye(128, dtype=ml_dtypes.bfloat16)}
    return [dict(shared, x=np.ascontiguousarray(x[b]))
            for b in range(x.shape[0])]


def kernel(**inputs):
    in_maps = make_in_maps(**inputs)
    nc = build_cached(4096)
    res = run_bass_kernel_spmd(nc, in_maps, core_ids=list(range(len(in_maps))))
    return np.stack([r["out"] for r in res.results], axis=0)


# version 11
# speedup vs baseline: 2.7613x; 1.0534x over previous
import os
import numpy as np
import ml_dtypes
from contextlib import ExitStack
import concourse.bass as bass
import concourse.mybir as mybir
from concourse.bass_utils import run_bass_kernel_spmd

F32 = mybir.dt.float32
BF16 = mybir.dt.bfloat16
AX = mybir.AxisListType.X
OP = mybir.AluOpType
AF = mybir.ActivationFunctionType

DM = 1024
NC_K = DM // 128
NE = 6
SUB = 512


class Plan:

    ENGINES = ("t", "v", "s", "y", "g")
    NSLOT = 8

    def __init__(self):
        self.steps = {e: [] for e in self.ENGINES}
        self.last_write = {}
        self.readers = {}
        self.dma_count = {e: 0 for e in self.ENGINES}
        self.pin_count = 0
        self.seq = 0

    def emit(self, eng, fn, reads=(), writes=(), dma=False, tag="", extra_deps=(),
             pin=False):
        idx = len(self.steps[eng])
        deps = set(extra_deps)
        for k in reads:
            w = self.last_write.get(k)
            if w is not None:
                deps.add(w)
        for k in writes:
            w = self.last_write.get(k)
            if w is not None:
                deps.add(w)
            last = {}
            for (de, di) in self.readers.get(k, ()):
                if self.steps[de][di]["dma"]:
                    deps.add((de, di))
                elif di > last.get(de, -1):
                    last[de] = di
            deps.update(last.items())
        deps = {(de, di) for (de, di) in deps if not (de == eng and di >= idx)}
        if eng == "t":
            deps = {(de, di) for (de, di) in deps if de != "t"}
        st = {"fn": fn, "deps": sorted(deps), "dma": dma, "tag": tag,
              "seq": self.seq}
        if dma:
            if pin:
                st["dslot"] = ("pin", self.pin_count)
                self.pin_count += 1
            else:
                st["dslot"] = self.dma_count[eng] % self.NSLOT
                self.dma_count[eng] += 1
        self.seq += 1
        self.steps[eng].append(st)
        ev = (eng, idx)
        for k in reads:
            self.readers.setdefault(k, []).append(ev)
        for k in writes:
            self.last_write[k] = ev
            self.readers[k] = []
        return ev

    def barrier(self):
        lasts = []
        for e in self.ENGINES:
            if not self.steps[e]:
                continue
            lasts.append((e, len(self.steps[e]) - 1))
            seen = set()
            for i in range(len(self.steps[e]) - 1, -1, -1):
                st = self.steps[e][i]
                if st["dma"] and st["dslot"] not in seen:
                    seen.add(st["dslot"])
                    lasts.append((e, i))

        def mknop():
            def f(obj):
                return obj.nop()
            return f
        for e in self.ENGINES:
            deps = [d for d in lasts if d[0] != e] +                    [d for d in lasts if d[0] == e and self.steps[e][d[1]]["dma"]]
            deps = [d for d in deps if not (d[0] == e and d[1] >= len(self.steps[e]))]
            self.steps[e].append({"fn": mknop(), "deps": sorted(set(deps)),
                                  "dma": False, "tag": "barrier",
                                  "seq": self.seq})
            self.seq += 1

    def queue_flush(self, eng):
        deps = []
        seen = set()
        for i in range(len(self.steps[eng]) - 1, -1, -1):
            st = self.steps[eng][i]
            if st["dma"] and st["dslot"] not in seen:
                seen.add(st["dslot"])
                deps.append((eng, i))

        def f(obj):
            return obj.nop()
        idx = len(self.steps[eng])
        self.steps[eng].append({"fn": f, "deps": sorted(deps), "dma": False,
                                "tag": f"flush_{eng}", "seq": self.seq})
        self.seq += 1
        return (eng, idx)

    def final_wait(self, eng, deps):
        def f(obj):
            return obj.nop()
        self.steps[eng].append({"fn": f, "deps": sorted(set(deps)),
                                "dma": False, "tag": "final", "seq": self.seq})
        self.seq += 1

    def finalize(self, nc, es, block):
        kseq = int(os.environ.get("KSEQ", "0"))
        if kseq:
            for e in self.ENGINES:
                self.steps[e] = [st for st in self.steps[e]
                                 if st["seq"] < kseq]
            self.last_write = {k: v for k, v in self.last_write.items()
                               if v[1] < len(self.steps[v[0]])}
            self.readers = {}
            self.barrier()
        consumers = {}
        for e in self.ENGINES:
            for i, st in enumerate(self.steps[e]):
                if st["tag"] in ("barrier", "final") or                         st["tag"].startswith("flush"):
                    continue
                for d in st["deps"]:
                    consumers.setdefault(d, []).append((e, i))
        def reduce_consumers(evs):
            keep, last = [], {}
            for (de, di) in evs:
                if self.steps[de][di]["dma"]:
                    keep.append((de, di))
                elif di > last.get(de, -1):
                    last[de] = di
            return keep + [(de, di) for de, di in last.items()]

        for e in self.ENGINES:
            prev_by_slot = {}
            for i, st in enumerate(self.steps[e]):
                if not st["dma"] or isinstance(st["dslot"], tuple):
                    continue
                sl = st["dslot"]
                pv = prev_by_slot.get(sl)
                if pv is not None:
                    add = [pv] + reduce_consumers(consumers.get(pv, []))
                    myseq = st["seq"]
                    extra = []
                    for (de, di) in add:
                        if de == e and di >= i:
                            continue
                        dseq = self.steps[de][di]["seq"]
                        assert dseq < myseq, (
                            "slot reuse forward dep", e, i, st["tag"],
                            de, di, self.steps[de][di]["tag"])
                        extra.append((de, di))
                    st["deps"] = sorted(set(st["deps"]) | set(extra))
                prev_by_slot[sl] = (e, i)
        need_mark = {e: set() for e in self.ENGINES}
        for e in self.ENGINES:
            for st in self.steps[e]:
                for (de, di) in st["deps"]:
                    need_mark[de].add(di)
        sems = {(e, p): es.enter_context(nc.semaphore(f"sem_{e}{p}"))
                for e in self.ENGINES for p in range(2)}
        dsems = {}
        for e in self.ENGINES:
            if self.dma_count[e]:
                n = min(self.NSLOT, self.dma_count[e])
                for sl in range(n):
                    dsems[(e, sl)] = es.enter_context(
                        nc.semaphore(f"dsem_{e}{sl}"))
            for i, st in enumerate(self.steps[e]):
                if st.get("dslot") is not None and isinstance(st["dslot"], tuple):
                    dsems[(e, st["dslot"])] = es.enter_context(
                        nc.semaphore(f"dsem_{e}p{st['dslot'][1]}"))
        ev_sem = {}
        slot_counts = {}
        for e in self.ENGINES:
            mcnt = [0, 0]
            nmark = 0
            for i, st in enumerate(self.steps[e]):
                if st["dma"]:
                    sl = st["dslot"]
                    c = slot_counts.get((e, sl), 0) + 16
                    slot_counts[(e, sl)] = c
                    st["inc_sem"] = dsems[(e, sl)]
                    st["inc"] = 16
                    ev_sem[(e, i)] = (st["inc_sem"], c)
                elif i in need_mark[e]:
                    p = nmark % 2
                    nmark += 1
                    mcnt[p] += 1
                    st["inc_sem"] = sems[(e, p)]
                    st["inc"] = 1
                    st["chain"] = (sems[(e, p)], mcnt[p] - 1)
                    ev_sem[(e, i)] = (sems[(e, p)], mcnt[p])
                else:
                    st["inc"] = 0
                    ev_sem[(e, i)] = None

        def emit_stream(eng, obj):
            water = {}
            for i, st in enumerate(self.steps[eng]):
                waits = {}
                for d in st["deps"]:
                    sv = ev_sem[d]
                    assert sv is not None, (eng, i, d, st["tag"])
                    sem, v = sv
                    if v > waits.get(id(sem), (None, -1))[1]:
                        waits[id(sem)] = (sem, v)
                ch = st.get("chain")
                if ch is not None and ch[1] > 0:
                    if ch[1] > waits.get(id(ch[0]), (None, -1))[1]:
                        waits[id(ch[0])] = ch
                for sem, v in waits.values():
                    if water.get(id(sem), -1) >= v:
                        continue
                    obj.wait_ge(sem, v)
                    water[id(sem)] = v
                ins = st["fn"](obj)
                if st["inc"]:
                    ins.then_inc(st["inc_sem"], st["inc"])

        @block.tensor
        def _(t):
            emit_stream("t", t)

        @block.vector
        def _(v):
            emit_stream("v", v)

        @block.scalar
        def _(s):
            emit_stream("s", s)

        @block.sync
        def _(y):
            emit_stream("y", y)

        @block.gpsimd
        def _(g):
            emit_stream("g", g)


def build_program(S=4096):
    PH = os.environ.get("KPHASES", "ABCD")
    KT = int(os.environ.get("KTILES", "0")) or 10**9
    NT = S // 128
    NSUB = S // SUB
    nc = bass.Bass("TRN2", target_bir_lowering=False)

    x_in = nc.dram_tensor("x", [S, DM], F32, kind="ExternalInput").ap()
    wq_in = nc.dram_tensor("wq", [DM, DM], F32, kind="ExternalInput").ap()
    wk_in = nc.dram_tensor("wk", [DM, DM], F32, kind="ExternalInput").ap()
    wv_in = nc.dram_tensor("wv", [DM, DM], F32, kind="ExternalInput").ap()
    wr_in = nc.dram_tensor("wr", [DM, DM], F32, kind="ExternalInput").ap()
    gwt_in = nc.dram_tensor("gwt", [DM, 8], F32, kind="ExternalInput").ap()
    gb_in = nc.dram_tensor("gb", [1, 8], F32, kind="ExternalInput").ap()
    ew1t_in = nc.dram_tensor("ew1t", [NE, DM, DM], F32, kind="ExternalInput").ap()
    ew3t_in = nc.dram_tensor("ew3t", [NE, DM, DM], F32, kind="ExternalInput").ap()
    ew2t_in = nc.dram_tensor("ew2t", [NE, DM, DM], F32, kind="ExternalInput").ap()
    fc1t_in = nc.dram_tensor("fc1t", [DM, DM], F32, kind="ExternalInput").ap()
    fc2t_in = nc.dram_tensor("fc2t", [DM, DM], F32, kind="ExternalInput").ap()
    fc3t_in = nc.dram_tensor("fc3t", [DM, DM], F32, kind="ExternalInput").ap()
    ident_in = nc.dram_tensor("ident", [128, 128], BF16, kind="ExternalInput").ap()
    out_hbm = nc.dram_tensor("out", [S, DM], F32, kind="ExternalOutput").ap()
    qx_hbm = nc.dram_tensor("qx_cache", [S, DM], F32, kind="Internal").ap()
    x1_hbm = nc.dram_tensor("x1_cache", [S, DM], F32, kind="Internal").ap()
    ln2t_hbm = nc.dram_tensor("ln2t_cache", [NC_K, 128, S], BF16, kind="Internal").ap()
    ln2t_hbm_v = ln2t_hbm.rearrange("c p t -> p c t")

    es = ExitStack()
    es_abc = ExitStack()
    with es:
        sb = lambda n, shp, dt: es.enter_context(nc.sbuf_tensor(n, shp, dt))
        sba = lambda n, shp, dt: es_abc.enter_context(nc.sbuf_tensor(n, shp, dt))
        psa = lambda n, shp, dt: es.enter_context(nc.psum_tensor(n, shp, dt))
        sb2 = lambda n, shp, dt: [sb(n + "0", shp, dt), sb(n + "1", shp, dt)]
        sba2 = lambda n, shp, dt: [sba(n + "0", shp, dt), sba(n + "1", shp, dt)]

        ident = sb("ident_sb", [128, 128], BF16)
        ones_col = sb("ones_col", [128, 1], BF16)
        ones_row = sb("ones_row", [1, 128], BF16)
        bias_b = sb("bias_b", [128, 8], F32)
        gw_bf = sb("gw_bf", [128, NC_K * 8], BF16)
        probs_all = sb("probs_all", [128, NT * 8], F32)
        ge_all = sb("ge_all", [128, NT * 8], F32)
        mx8a = sb2("mx8a", [128, 8], F32)
        mx8b = sb2("mx8b", [128, 8], F32)
        p2buf = sb2("p2buf", [128, 8], F32)
        p3buf = sb2("p3buf", [128, 8], F32)
        fmask = sb2("fmask", [128, 8], F32)
        smask = sb2("smask", [128, 8], F32)
        gcol = sb2("gcol", [128, 4], F32)
        ptmp = sb2("ptmp", [128, 8], F32)
        etmp = sb2("etmp", [128, 8], F32)
        stats = sb2("stats", [128, 12], F32)
        mv = sb2("mv", [128, 2], F32)
        rstd = sb2("rstd", [128, 2], F32)
        nbias = sb2("nbias", [128, 1], F32)
        mcol = sb2("mcol", [128, 4], F32)
        zcol = sb2("zcol", [128, 4], F32)
        gk_f = sb("gk_f", [128, 8], F32)
        ceps = sb("ceps", [128, 1], F32)
        gb_row = sb("gb_row", [1, 8], F32)
        gb_row_bf = sb("gb_row_bf", [1, 8], BF16)

        P = [psa(f"P{i}", [128, 512], F32) for i in range(4)] + [None, None]
        ptb = psa("ptb", [128, 1024], BF16)
        pg = psa("pg", [128, 8], F32)
        es_a = ExitStack()
        gqP = [es_a.enter_context(nc.psum_tensor(f"gqP{h}", [1, 512], F32))
               for h in range(2)]
        es_b = ExitStack()

        block = es.enter_context(nc.Block())

        pl = Plan()
        E = pl.emit

        ln1T = sba("ln1T", [128, NC_K * S], BF16)
        wA = [sba(f"wA{i}", [128, NC_K * DM], BF16) for i in range(4)]
        xb = [sba(f"xb{i}", [128, DM], F32) for i in range(3)]
        qxst = sba2("qxst", [128, DM], F32)
        qxld = sba2("qxld", [128, DM], F32)
        ln1_bf = sba2("ln1_bf", [128, DM], BF16)
        expbuf = sba2("expbuf", [128, DM], BF16)
        wbuf = sba2("wbuf", [128, DM], BF16)
        pbuf = sba2("pbuf", [128, DM], F32)
        x1t = sba2("x1t", [128, DM], F32)
        ln2_bf = sba2("ln2_bf", [128, DM], BF16)
        ln2st = sba("ln2st", [128, NC_K * SUB], BF16)
        gq_row_bf = sba("gq_row_bf", [1, DM], BF16)
        gq_bcast = sba("gq_bcast", [128, DM], BF16)
        gkf = sba("gkf", [128, 8], F32)
        rZf = sba2("rZf", [128, 1], F32)
        rZc = sba2("rZc", [128, 1], BF16)

        ln1T_v = ln1T[:].rearrange("p (c t) -> p c t", c=NC_K)
        wA_v = [w[:].rearrange("p (c f) -> p c f", c=NC_K) for w in wA]
        ln2st_v = ln2st[:].rearrange("p (c t) -> p c t", c=NC_K)
        gw_v = gw_bf[:].rearrange("p (c f) -> p c f", c=NC_K)
        ptb_v = ptb[:].rearrange("p (c t) -> p c t", c=NC_K)

        E("y", lambda o: o.dma_start(out=ident[:], in_=ident_in),
          writes=["ident"], dma=True, tag="ld_ident", pin=True)
        E("v", lambda o: o.memset(ones_col[:], 1.0), writes=["ones_col"])
        E("v", lambda o: o.memset(ones_row[:], 1.0), writes=["ones_row"])
        E("v", lambda o: o.memset(mx8a[0][:], -3e30), writes=["mx8a0"])
        E("v", lambda o: o.memset(mx8a[1][:], -3e30), writes=["mx8a1"])
        E("v", lambda o: o.memset(mx8b[0][:], -3e30), writes=["mx8b0"])
        E("v", lambda o: o.memset(mx8b[1][:], -3e30), writes=["mx8b1"])
        E("v", lambda o: o.memset(probs_all[:], -2e30), writes=["probs_all"])
        E("v", lambda o: o.memset(ceps[:], 1e-5), writes=["ceps"])
        E("g", lambda o: o.dma_start(
            out=gw_v[:, :, 0:8], in_=gwt_in.rearrange("(c p) f -> p c f", p=128)),
          writes=["gw_bf"], dma=True, tag="ld_gw", pin=True)
        E("y", lambda o: o.dma_start(out=gb_row[0:1, :], in_=gb_in),
          writes=["bias_row"], dma=True, tag="ld_gb", pin=True)
        E("v", lambda o: o.tensor_copy(out=gb_row_bf[:], in_=gb_row[:]),
          reads=["bias_row"], writes=["gb_row_bf"], tag="gb_cast")
        E("t", lambda o: o.matmul(pg[:, 0:8], ones_row[:], gb_row_bf[0:1, :],
                                  start=True, stop=True),
          reads=["gb_row_bf", "ones_row"], writes=["pg"], tag="gb_mm")
        E("v", lambda o: o.tensor_copy(out=bias_b[:], in_=pg[:, 0:8]),
          reads=["pg"], writes=["bias_b"], tag="gb_ev")

        def wload(bufkey, dst, src, tag):
            E("g", lambda o, d=dst, s_=src: o.dma_start(
                out=d, in_=s_.rearrange("(c p) f -> p c f", p=128)),
              writes=[bufkey], dma=True, tag=tag)

        wload("wA0", wA[0][:], wq_in, "ld_wq")
        wload("wA1", wA[1][:], wk_in, "ld_wk")
        wload("wA2", wA[2][:], wv_in, "ld_wv")
        wload("wA3", wA[3][:], wr_in, "ld_wr")

        def layernorm(par, src_buf, src_key, dst_buf, dst_key, tg):
            st_, mv_, rs_, nb_ = stats[par], mv[par], rstd[par], nbias[par]
            E("v", lambda o: o.bn_stats(out=st_[:, 0:6], in_=src_buf[:, 0:512]),
              reads=[src_key], writes=[f"stats{par}a"], tag=f"{tg}_bnsa")
            E("v", lambda o: o.bn_stats(out=st_[:, 6:12], in_=src_buf[:, 512:1024]),
              reads=[src_key], writes=[f"stats{par}b"], tag=f"{tg}_bnsb")
            E("v", lambda o: o.bn_aggr(out=mv_[:], in_=st_[:]),
              reads=[f"stats{par}a", f"stats{par}b"], writes=[f"mv{par}"],
              tag=f"{tg}_bna")
            E("s", lambda o: o.activation(out=rs_[:, 0:1], in_=mv_[:, 1:2],
                                          func=AF.Ln, bias=ceps[:], scale=1.0),
              reads=[f"mv{par}", "ceps"], writes=[f"rstd{par}0"], tag=f"{tg}_ln_")
            E("s", lambda o: o.activation(out=rs_[:, 1:2], in_=rs_[:, 0:1],
                                          func=AF.Exp, bias=0.0, scale=-0.5),
              reads=[f"rstd{par}0"], writes=[f"rstd{par}1"], tag=f"{tg}_rex")
            E("v", lambda o: o.tensor_scalar(out=nb_[:], in0=mv_[:, 0:1],
                                             scalar1=rs_[:, 1:2], scalar2=-1.0,
                                             op0=OP.mult, op1=OP.mult),
              reads=[f"mv{par}", f"rstd{par}1"], writes=[f"nbias{par}"],
              tag=f"{tg}_nb")
            E("s", lambda o: o.activation(out=dst_buf[:], in_=src_buf[:],
                                          func=AF.Identity, bias=nb_[:],
                                          scale=rs_[:, 1:2]),
              reads=[src_key, f"nbias{par}", f"rstd{par}1"], writes=[dst_key],
              tag=f"{tg}_ln")

        NTA = min(KT, NT) if 'A' in PH else 0

        def A_LOAD(i):
            par = i % 2
            xk = f"xb{i % 3}"
            E("y", lambda o, i=i: o.dma_start(
                out=xb[i % 3][:], in_=x_in[i*128:(i+1)*128, :]),
              writes=[xk], dma=True, tag=f"A_ldx{i}")
            layernorm(par, xb[i % 3], xk, ln1_bf[par], f"ln1_bf{par}", f"A{i}")

        def A_TR(i):
            par = i % 2
            lnk = f"ln1_bf{par}"
            for c in range(NC_K):
                E("t", lambda o, c=c, par=par: o.transpose(
                    ptb[:, c*128:(c+1)*128], ln1_bf[par][:, c*128:(c+1)*128],
                    ident[:]),
                  reads=[lnk, "ident"], writes=["ptb"], tag=f"A_tr{i}_{c}")
            E("s", lambda o, i=i: o.activation(
                out=ln1T_v[:, :, i*128:(i+1)*128], in_=ptb_v,
                func=AF.Identity, bias=0.0, scale=1.0),
              reads=["ptb"], writes=[f"ln1T{i}"], tag=f"A_ev{i}")

        def A_MM(i):
            par = i % 2
            pslot = par * 2
            for h in range(2):
                bnk, bk = P[pslot + h], f"P{pslot + h}"
                for c in range(NC_K):
                    E("t", lambda o, h=h, c=c, i=i, bnk=bnk: o.matmul(
                        bnk[:], ln1T_v[:, c, i*128:(i+1)*128],
                        wA_v[0][:, c, h*512:(h+1)*512],
                        start=(c == 0), stop=(c == NC_K - 1)),
                      reads=[f"ln1T{i}", "wA0"], writes=[bk],
                      tag=f"A_q{i}_{h}_{c}")

        def A_GQ(i):
            par = i % 2
            for h in range(2):
                E("t", lambda o, h=h, par=par, i=i: o.matmul(
                    gqP[h][0:1, :], rZc[par][:],
                    wbuf[par][:, h*512:(h+1)*512],
                    start=(i == 0), stop=(i == NTA - 1)),
                  reads=[f"rZc{par}", f"w{par}_{h}"], writes=[f"gqP{h}"],
                  tag=f"A_gq{i}_{h}")

        def A_POST(i):
            par = i % 2
            pslot = par * 2
            xk = f"xb{i % 3}"
            qk = f"qxst{par}"
            mc, zc = mcol[par], zcol[par]
            E("v", lambda o, b=P[pslot], mc=mc: o.reduce_max(
                out=mc[:, 0:1], in_=b[:], axis=AX),
              reads=[f"P{pslot}"], writes=[f"m0_{par}"], tag=f"A_m0{i}")
            E("v", lambda o, b=P[pslot+1], mc=mc: o.reduce_max(
                out=mc[:, 1:2], in_=b[:], axis=AX),
              reads=[f"P{pslot+1}"], writes=[f"m1_{par}"], tag=f"A_m1{i}")
            E("v", lambda o, mc=mc: o.tensor_scalar(
                out=mc[:, 3:4],
                in0=mc[:, 0:1], scalar1=mc[:, 1:2], scalar2=-1.0 / 32.0,
                op0=OP.max, op1=OP.mult),
              reads=[f"m0_{par}", f"m1_{par}"], writes=[f"negM{par}"],
              tag=f"A_nM{i}")
            for h in range(2):
                bnk, bk = P[pslot + h], f"P{pslot + h}"
                E("s", lambda o, h=h, bnk=bnk, par=par, mc=mc, zc=zc: o.activation(
                    out=expbuf[par][:, h*512:(h+1)*512], in_=bnk[:], func=AF.Exp,
                    bias=mc[:, 3:4], scale=1.0/32.0,
                    accum_out=zc[:, h:h+1]),
                  reads=[bk, f"negM{par}"], writes=[f"exp{par}_{h}", f"z{par}{h}"],
                  tag=f"A_e{i}_{h}")
            E("v", lambda o, zc=zc: o.tensor_tensor(
                out=zc[:, 2:3], in0=zc[:, 0:1], in1=zc[:, 1:2], op=OP.add),
              reads=[f"z{par}0", f"z{par}1"], writes=[f"Z{par}"], tag=f"A_Z{i}")
            E("v", lambda o, zc=zc, par=par: o.reciprocal(
                out=rZf[par][:], in_=zc[:, 2:3]),
              reads=[f"Z{par}"], writes=[f"rZf{par}"], tag=f"A_rZ{i}")
            E("v", lambda o, par=par: o.tensor_copy(
                out=rZc[par][:], in_=rZf[par][:]),
              reads=[f"rZf{par}"], writes=[f"rZc{par}"], tag=f"A_rc{i}")
            for h in range(2):
                bnk, bk = P[pslot + h], f"P{pslot + h}"
                E("v", lambda o, h=h, bnk=bnk, par=par: o.tensor_tensor(
                    out=wbuf[par][:, h*512:(h+1)*512], in0=bnk[:],
                    in1=expbuf[par][:, h*512:(h+1)*512], op=OP.mult),
                  reads=[bk, f"exp{par}_{h}"], writes=[f"w{par}_{h}"],
                  tag=f"A_w{i}_{h}")
                E("v", lambda o, h=h, bnk=bnk, par=par, i=i: o.tensor_tensor(
                    out=qxst[par][:, h*512:(h+1)*512], in0=bnk[:],
                    in1=xb[i % 3][:, h*512:(h+1)*512], op=OP.add),
                  reads=[bk, xk], writes=[qk], tag=f"A_qx{i}_{h}")
            E("s", lambda o, i=i, par=par: o.dma_start(
                out=qx_hbm[i*128:(i+1)*128, :], in_=qxst[par][:]),
              reads=[qk], writes=[f"qxhbm{i}"], dma=True, tag=f"A_stq{i}")

        if NTA:
            A_LOAD(0)
            if NTA > 1:
                A_LOAD(1)
            A_TR(0)
            if NTA > 1:
                A_TR(1)
            for i in range(NTA):
                A_MM(i)
                if i > 0:
                    A_GQ(i - 1)
                if i + 2 < NTA:
                    A_LOAD(i + 2)
                A_POST(i)
                if i + 2 < NTA:
                    A_TR(i + 2)
            A_GQ(NTA - 1)

        fold_evs = []
        for h in range(2):
            ev = E("v", lambda o, h=h: o.tensor_copy(
                out=gq_row_bf[0:1, h*512:(h+1)*512], in_=gqP[h][0:1, :]),
              reads=[f"gqP{h}"], writes=["gq_row_bf"], tag=f"gq_ev{h}")
            fold_evs.append(ev)
        for h in range(2):
            E("t", lambda o, h=h: o.matmul(P[h][:], ones_row[:],
                                           gq_row_bf[0:1, h*512:(h+1)*512],
                                           start=True, stop=True),
              reads=["gq_row_bf", "ones_row"], writes=[f"P{h}"], tag=f"gqb_mm{h}")
            E("v", lambda o, h=h: o.tensor_copy(
                out=gq_bcast[:, h*512:(h+1)*512], in_=P[h][:]),
              reads=[f"P{h}"], writes=["gqb"], tag=f"gqb_ev{h}")
        es_a.close()
        gkP = es_b.enter_context(nc.psum_tensor("gkP", [128, 8], F32))
        qx_flush = pl.queue_flush("s")

        def B_MM(i):
            par = i % 2
            pslot = par * 2
            for h in range(2):
                bnk, bk = P[pslot + h], f"P{pslot + h}"
                for c in range(NC_K):
                    E("t", lambda o, h=h, c=c, i=i, bnk=bnk: o.matmul(
                        bnk[:], ln1T_v[:, c, i*128:(i+1)*128],
                        wA_v[1][:, c, h*512:(h+1)*512],
                        start=(c == 0), stop=(c == NC_K - 1)),
                      reads=[f"ln1T{i}", "wA1"], writes=[bk],
                      tag=f"B_k{i}_{h}_{c}")

        def B_GK(i, extra=()):
            par = i % 2
            for c in range(NC_K):
                E("t", lambda o, c=c, par=par, i=i: o.matmul(
                    gkP[:, c:c+1], wbuf[par][:, c*128:(c+1)*128],
                    rZc[par][:],
                    start=(i == 0 and c == 0),
                    stop=(i == NT - 1 and c == NC_K - 1)),
                  reads=[f"w{par}_{c // 4}", f"rZc{par}"], writes=["gkP"],
                  extra_deps=(extra if i == 0 else ()), tag=f"B_gk{i}_{c}")

        def B_WVT(c_):
            for j_ in range(NC_K):
                E("t", lambda o, c_=c_, j_=j_: o.transpose(
                    ptb[:, j_*128:(j_+1)*128],
                    wA[2][:][:, c_*DM + j_*128:c_*DM + (j_+1)*128],
                    ident[:]),
                  reads=["wA2", "ident"], writes=["ptb"], tag=f"B_wt{c_}_{j_}")
            E("v", lambda o, c_=c_: o.tensor_copy(
                out=wA_v[0][:, :, c_*128:(c_+1)*128], in_=ptb_v),
              reads=["ptb"], writes=["wA0", "wvT"], tag=f"B_wc{c_}")

        def B_POST(i):
            par = i % 2
            pslot = par * 2
            pb = pbuf[par]
            mc, zc = mcol[par], zcol[par]
            for h in range(2):
                bnk, bk = P[pslot + h], f"P{pslot + h}"
                E("v", lambda o, h=h, bnk=bnk, pb=pb: o.tensor_tensor(
                    out=pb[:, h*512:(h+1)*512], in0=bnk[:],
                    in1=gq_bcast[:, h*512:(h+1)*512], op=OP.mult),
                  reads=[bk, "gqb"], writes=[f"pb{par}_{h}"],
                  tag=f"B_p{i}_{h}")
            E("v", lambda o, pb=pb, mc=mc: o.reduce_max(
                out=mc[:, 2:3], in_=pb[:], axis=AX),
              reads=[f"pb{par}_0", f"pb{par}_1"], writes=[f"M{par}"],
              tag=f"B_M{i}")
            E("v", lambda o, mc=mc: o.tensor_scalar(
                out=mc[:, 3:4], in0=mc[:, 2:3], scalar1=-1.0/32.0, scalar2=None,
                op0=OP.mult),
              reads=[f"M{par}"], writes=[f"negM{par}"], tag=f"B_nM{i}")
            E("s", lambda o, pb=pb, par=par, mc=mc, zc=zc: o.activation(
                out=expbuf[par][:], in_=pb[:], func=AF.Exp,
                bias=mc[:, 3:4], scale=1.0/32.0, accum_out=zc[:, 2:3]),
              reads=[f"pb{par}_0", f"pb{par}_1", f"negM{par}"],
              writes=[f"exp{par}_0", f"exp{par}_1", f"Z{par}"], tag=f"B_e{i}")
            E("v", lambda o, zc=zc, par=par: o.reciprocal(
                out=rZf[par][:], in_=zc[:, 2:3]),
              reads=[f"Z{par}"], writes=[f"rZf{par}"], tag=f"B_rZ{i}")
            E("v", lambda o, par=par: o.tensor_copy(
                out=rZc[par][:], in_=rZf[par][:]),
              reads=[f"rZf{par}"], writes=[f"rZc{par}"], tag=f"B_rc{i}")
            E("v", lambda o, pb=pb, par=par: o.tensor_tensor(
                out=wbuf[par][:], in0=pb[:], in1=expbuf[par][:], op=OP.mult),
              reads=[f"pb{par}_0", f"pb{par}_1", f"exp{par}_0", f"exp{par}_1"],
              writes=[f"w{par}_0", f"w{par}_1"], tag=f"B_w{i}")

        KB = os.environ.get("KB", "gwp")
        if 'B' in PH:
            for i in range(NT):
                B_MM(i)
                if i > 0 and 'g' in KB:
                    B_GK(i - 1, extra=fold_evs)
                if 'w' in KB and i % 4 == 3:
                    B_WVT(i // 4)
                if 'p' in KB:
                    B_POST(i)
            if 'g' in KB:
                B_GK(NT - 1)

        E("v", lambda o: o.tensor_copy(out=gkf[:], in_=gkP[:, 0:8]),
          reads=["gkP"], writes=["gkf"], tag="gk_ev")
        for c in range(NC_K):
            eng = "s" if c % 2 == 0 else "v"
            if eng == "s":
                E("s", lambda o, c=c: o.activation(
                    out=wA_v[2][:, c, :], in_=wA_v[3][:, c, :],
                    func=AF.Identity, bias=0.0, scale=gkf[:, c:c+1]),
                  reads=["wA3", "gkf"], writes=["wA2", "wp"], tag=f"wp_s{c}")
            else:
                E("v", lambda o, c=c: o.tensor_scalar(
                    out=wA_v[2][:, c, :], in0=wA_v[3][:, c, :],
                    scalar1=gkf[:, c:c+1], scalar2=None, op0=OP.mult),
                  reads=["wA3", "gkf"], writes=["wA2", "wp"], tag=f"wp_v{c}")
        for ab in range(NC_K):
            for hb in range(2):
                slot = (2 * ab + hb) % 4
                bnk, bk = P[slot], f"P{slot}"
                for cd in range(NC_K):
                    E("t", lambda o, ab=ab, hb=hb, cd=cd, bnk=bnk: o.matmul(
                        bnk[:], wA_v[0][:, cd, ab*128:(ab+1)*128],
                        wA_v[2][:, cd, hb*512:(hb+1)*512],
                        start=(cd == 0), stop=(cd == NC_K - 1)),
                      reads=["wvT", "wp"], writes=[bk],
                      tag=f"wvr_{ab}_{hb}_{cd}")
                E("v", lambda o, ab=ab, hb=hb, bnk=bnk: o.tensor_copy(
                    out=wA_v[3][:, ab, hb*512:(hb+1)*512], in_=bnk[:]),
                  reads=[bk], writes=["wA3", "wvr"], tag=f"wvre_{ab}_{hb}")

        def C_LD(i):
            par = i % 2
            E("y", lambda o, i=i, par=par: o.dma_start(
                out=qxld[par][:], in_=qx_hbm[i*128:(i+1)*128, :]),
              writes=[f"qxld{par}"], dma=True, extra_deps=[qx_flush],
              tag=f"C_ldqx{i}")

        def C_MM(i):
            par = i % 2
            pslot = par * 2
            for h in range(2):
                bnk, bk = P[pslot + h], f"P{pslot + h}"
                for c in range(NC_K):
                    E("t", lambda o, h=h, c=c, i=i, bnk=bnk: o.matmul(
                        bnk[:], ln1T_v[:, c, i*128:(i+1)*128],
                        wA_v[3][:, c, h*512:(h+1)*512],
                        start=(c == 0), stop=(c == NC_K - 1)),
                      reads=[f"ln1T{i}", "wvr"], writes=[bk],
                      tag=f"C_r{i}_{h}_{c}")

        def C_TAIL1(i):
            par = i % 2
            pslot = par * 2
            for h in range(2):
                bnk, bk = P[pslot + h], f"P{pslot + h}"
                E("v", lambda o, h=h, par=par, bnk=bnk: o.scalar_tensor_tensor(
                    out=x1t[par][:, h*512:(h+1)*512], in0=bnk[:],
                    scalar=1.0, in1=qxld[par][:, h*512:(h+1)*512],
                    op0=OP.mult, op1=OP.add),
                  reads=[bk, f"qxld{par}"], writes=[f"x1t{par}"],
                  tag=f"C_x1{i}_{h}")
            E("s", lambda o, i=i, par=par: o.dma_start(
                out=x1_hbm[i*128:(i+1)*128, :], in_=x1t[par][:]),
              reads=[f"x1t{par}"], writes=[f"x1hbm{i}"], dma=True,
              tag=f"C_stx1{i}")
            layernorm(par, x1t[par], f"x1t{par}", ln2_bf[par], f"ln2_bf{par}",
                      f"C{i}")

        def C_TAIL2(i):
            tt = i % 4
            par = i % 2
            for c in range(NC_K):
                E("t", lambda o, c=c, par=par: o.transpose(
                    ptb[:, c*128:(c+1)*128],
                    ln2_bf[par][:, c*128:(c+1)*128], ident[:]),
                  reads=[f"ln2_bf{par}", "ident"], writes=["ptb"],
                  tag=f"C_tr{i}_{c}")
            E("s", lambda o, tt=tt: o.activation(
                out=ln2st_v[:, :, tt*128:(tt+1)*128], in_=ptb_v,
                func=AF.Identity, bias=0.0, scale=1.0),
              reads=["ptb"], writes=[f"ln2st{tt}"], tag=f"C_ev{i}")
            for c in range(NC_K):
                E("t", lambda o, c=c, tt=tt: o.matmul(
                    pg[:, 0:6], ln2st_v[:, c, tt*128:(tt+1)*128],
                    gw_v[:, c, 0:6], start=(c == 0), stop=(c == NC_K - 1)),
                  reads=[f"ln2st{tt}", "gw_bf"], writes=["pg"],
                  tag=f"C_gmm{i}_{c}")
            E("v", lambda o, par=par: o.tensor_tensor(
                out=ptmp[par][:, 0:6], in0=pg[:, 0:6],
                in1=bias_b[:, 0:6], op=OP.add),
              reads=["pg", "bias_b"], writes=[f"ptmp{par}"], tag=f"C_gb{i}")
            E("s", lambda o, par=par: o.activation(
                out=etmp[par][:, 0:6], in_=ptmp[par][:, 0:6],
                func=AF.Exp, bias=0.0, scale=-1.0),
              reads=[f"ptmp{par}"], writes=[f"etmp{par}"], tag=f"C_sg1{i}")
            E("v", lambda o, par=par: o.tensor_scalar(
                out=etmp[par][:, 0:6], in0=etmp[par][:, 0:6], scalar1=1.0,
                scalar2=None, op0=OP.add),
              reads=[f"etmp{par}"], writes=[f"etmp{par}"], tag=f"C_sg2{i}")
            E("v", lambda o, ti=i, par=par: o.reciprocal(
                out=probs_all[:, ti*8:ti*8+6], in_=etmp[par][:, 0:6]),
              reads=[f"etmp{par}"], writes=[f"probs{i}"], tag=f"C_sg3{i}")
            ti = i
            o8 = ti * 8
            Pt = probs_all[:, o8:o8+8]
            ma, mb = mx8a[par], mx8b[par]
            p2_, p3_ = p2buf[par], p3buf[par]
            fm, sm, gc = fmask[par], smask[par], gcol[par]
            E("v", lambda o, Pt=Pt, ma=ma: o.reduce_max(
                out=ma[:, 0:1], in_=Pt, axis=AX),
              reads=[f"probs{ti}", "probs_all"], writes=[f"mx8a_v{par}"],
              tag=f"G_m1{ti}")
            E("v", lambda o, Pt=Pt, ma=ma, p2_=p2_: o.match_replace(
                out=p2_[:], in_to_replace=ma[:], in_values=Pt,
                imm_value=-2e30),
              reads=[f"probs{ti}", "probs_all", f"mx8a{par}",
                     f"mx8a_v{par}"],
              writes=[f"p2buf{par}"], tag=f"G_mr1{ti}")
            E("v", lambda o, mb=mb, p2_=p2_: o.reduce_max(
                out=mb[:, 0:1], in_=p2_[:], axis=AX),
              reads=[f"p2buf{par}"], writes=[f"mx8b_v{par}"],
              tag=f"G_m2{ti}")
            E("v", lambda o, Pt=Pt, fm=fm, p2_=p2_: o.tensor_tensor(
                out=fm[:], in0=Pt, in1=p2_[:], op=OP.not_equal),
              reads=[f"probs{ti}", "probs_all", f"p2buf{par}"],
              writes=[f"fmask{par}"], tag=f"G_f{ti}")
            E("v", lambda o, mb=mb, p2_=p2_, p3_=p3_: o.match_replace(
                out=p3_[:], in_to_replace=mb[:], in_values=p2_[:],
                imm_value=-2e30),
              reads=[f"p2buf{par}", f"mx8b{par}", f"mx8b_v{par}"],
              writes=[f"p3buf{par}"], tag=f"G_mr2{ti}")
            E("v", lambda o, sm=sm, p2_=p2_, p3_=p3_: o.tensor_tensor(
                out=sm[:], in0=p2_[:], in1=p3_[:], op=OP.not_equal),
              reads=[f"p2buf{par}", f"p3buf{par}"], writes=[f"smask{par}"],
              tag=f"G_s{ti}")
            E("v", lambda o, gc=gc, ma=ma, mb=mb: o.scalar_tensor_tensor(
                out=gc[:, 0:1], in0=ma[:, 0:1], scalar=1e-8, in1=mb[:, 0:1],
                op0=OP.add, op1=OP.add),
              reads=[f"mx8a_v{par}", f"mx8b_v{par}"], writes=[f"den{par}"],
              tag=f"G_d{ti}")
            E("v", lambda o, gc=gc: o.reciprocal(out=gc[:, 2:3],
                                                 in_=gc[:, 0:1]),
              reads=[f"den{par}"], writes=[f"rden{par}"], tag=f"G_rd{ti}")
            E("v", lambda o, gc=gc, ma=ma: o.tensor_scalar(
                out=gc[:, 3:4], in0=ma[:, 0:1], scalar1=gc[:, 2:3],
                scalar2=None, op0=OP.mult),
              reads=[f"mx8a_v{par}", f"rden{par}"], writes=[f"gm{par}"],
              tag=f"G_g1{ti}")
            E("v", lambda o, o8=o8, fm=fm, gc=gc: o.tensor_scalar(
                out=ge_all[:, o8:o8+8], in0=fm[:], scalar1=gc[:, 3:4],
                scalar2=None, op0=OP.mult),
              reads=[f"fmask{par}", f"gm{par}"], writes=[f"ge{ti}"],
              tag=f"G_ge1{ti}")
            E("v", lambda o, gc=gc, mb=mb: o.tensor_scalar(
                out=gc[:, 3:4], in0=mb[:, 0:1], scalar1=gc[:, 2:3],
                scalar2=None, op0=OP.mult),
              reads=[f"mx8b_v{par}", f"rden{par}"], writes=[f"gm{par}"],
              tag=f"G_g2{ti}")
            E("v", lambda o, o8=o8, sm=sm, gc=gc: o.scalar_tensor_tensor(
                out=ge_all[:, o8:o8+8], in0=sm[:], scalar=gc[:, 3:4],
                in1=ge_all[:, o8:o8+8], op0=OP.mult, op1=OP.add),
              reads=[f"smask{par}", f"gm{par}", f"ge{ti}"],
              writes=[f"ge{ti}"], tag=f"G_ge2{ti}")
            if tt == 3:
                ch = i // 4
                E("s", lambda o, ch=ch: o.dma_start(
                    out=ln2t_hbm_v[:, :, ch*SUB:(ch+1)*SUB],
                    in_=ln2st_v[:, :, :]),
                  reads=[f"ln2st{q}" for q in range(4)],
                  writes=[f"ln2thbm{ch}"], dma=True, tag=f"C_stln2{ch}")

        if 'C' in PH:
            for i in range(NT):
                C_LD(i)
                C_MM(i)
                if i > 0:
                    C_TAIL1(i - 1)
                if i > 1:
                    C_TAIL2(i - 2)
            C_TAIL1(NT - 1)
            C_TAIL2(NT - 2)
            C_TAIL2(NT - 1)

        es_abc.close()
        es_b.close()
        P[4] = psa("P4", [128, 512], F32)
        P[5] = psa("P5", [128, 512], F32)
        pl.barrier()

        y_sb = sb("y_sb", [128, NT * DM], BF16)
        wF = [sb(f"wF{i}", [128, NC_K * DM], BF16) for i in range(4)]
        slab = [sb(f"slab{i}", [128, NC_K * SUB], BF16) for i in range(2)]
        h1s = [sb(f"h1s{i}", [128, SUB], F32) for i in range(2)]
        hb = sb("hb", [128, NC_K * SUB], BF16)
        x1ld = [sb(f"x1ld{i}", [128, DM], F32) for i in range(2)]
        outt = [sb(f"outt{i}", [128, DM], F32) for i in range(2)]
        wF_v = [w[:].rearrange("p (c f) -> p c f", c=NC_K) for w in wF]
        slab_v = [s_[:].rearrange("p (c t) -> p c t", c=NC_K) for s_ in slab]
        hb_v = hb[:].rearrange("p (c t) -> p c t", c=NC_K)
        y_v = y_sb[:].rearrange("p (n f) -> p n f", n=NT)

        E("v", lambda o: o.memset(y_sb[:], 0.0), writes=["y_sb"], tag="D_zy")

        branches = [(ew1t_in[e], ew3t_in[e], ew2t_in[e], e) for e in range(NE)]
        branches.append((fc1t_in, fc2t_in, fc3t_in, None))
        if "D" not in PH:
            branches = branches[-1:] if "d" in PH else []

        wslot = {}

        def emit_branch_loads(b):
            w1, w3, w2, _e = branches[b]
            for j, wsrc in enumerate((w1, w3, w2)):
                slot = (3 * b + j) % 4
                wslot[(b, j)] = slot
                E("g", lambda o, slot=slot, s_=wsrc: o.dma_start(
                    out=wF[slot][:], in_=s_.rearrange("(c p) f -> p c f", p=128)),
                  writes=[f"slot{slot}"], dma=True, tag=f"ld_w{b}_{j}")

        if branches:
            emit_branch_loads(0)

        allhb = [f"hb{j}" for j in range(NC_K)]
        for b, (w1, w3, w2, e) in enumerate(branches):
            shared = e is None
            s1, s3, s2 = wslot[(b, 0)], wslot[(b, 1)], wslot[(b, 2)]
            for sb_i in range(NSUB):
                sl, slv = slab[sb_i % 2], slab_v[sb_i % 2]
                slk = f"slab{sb_i % 2}"
                E("y", lambda o, sb_i=sb_i, sl=sl: o.dma_start(
                    out=sl[:], in_=ln2t_hbm_v[:, :, sb_i*SUB:(sb_i+1)*SUB]),
                  writes=[slk], dma=True, tag=f"D_lds{b}_{sb_i}")
                for j in range(NC_K):
                    b1k, b1n = P[j % 2], f"P{j % 2}"
                    b3k, b3n = P[2 + j % 2], f"P{2 + j % 2}"
                    for c in range(NC_K):
                        E("t", lambda o, j=j, c=c, b1k=b1k, slv=slv, s1=s1: o.matmul(
                            b1k[:], wF_v[s1][:, c, j*128:(j+1)*128],
                            slv[:, c, :],
                            start=(c == 0), stop=(c == NC_K - 1)),
                          reads=[f"slot{s1}", slk], writes=[b1n],
                          tag=f"D_h1_{b}_{sb_i}_{j}_{c}")
                    for c in range(NC_K):
                        E("t", lambda o, j=j, c=c, b3k=b3k, slv=slv, s3=s3: o.matmul(
                            b3k[:], wF_v[s3][:, c, j*128:(j+1)*128],
                            slv[:, c, :],
                            start=(c == 0), stop=(c == NC_K - 1)),
                          reads=[f"slot{s3}", slk], writes=[b3n],
                          tag=f"D_h3_{b}_{sb_i}_{j}_{c}")
                    E("s", lambda o, j=j, b1k=b1k: o.activation(
                        out=h1s[j % 2][:], in_=b1k[:], func=AF.Silu),
                      reads=[b1n], writes=[f"h1s{j % 2}"],
                      tag=f"D_si{b}_{sb_i}_{j}")
                    E("v", lambda o, j=j, b3k=b3k: o.tensor_tensor(
                        out=hb_v[:, j, :], in0=h1s[j % 2][:], in1=b3k[:],
                        op=OP.mult),
                      reads=[f"h1s{j % 2}", b3n], writes=[f"hb{j}"],
                      tag=f"D_hb{b}_{sb_i}_{j}")
                for tt in range(4):
                    ti = sb_i * 4 + tt
                    if shared:
                        E("y", lambda o, ti=ti: o.dma_start(
                            out=x1ld[ti % 2][:],
                            in_=x1_hbm[ti*128:(ti+1)*128, :]),
                          writes=[f"x1ld{ti % 2}"],
                          dma=True, tag=f"D_ldx1{ti}")
                    for h in range(2):
                        bk, bn = P[4 + (2*tt + h) % 2], f"P{4 + (2*tt + h) % 2}"
                        for j in range(NC_K):
                            E("t", lambda o, j=j, tt=tt, h=h, bk=bk, s2=s2: o.matmul(
                                bk[:], hb_v[:, j, tt*128:(tt+1)*128],
                                wF_v[s2][:, j, h*512:(h+1)*512],
                                start=(j == 0), stop=(j == NC_K - 1)),
                              reads=allhb + [f"slot{s2}"], writes=[bn],
                              tag=f"D_y{b}_{sb_i}_{tt}_{h}_{j}")
                        if not shared:
                            E("v", lambda o, ti=ti, h=h, bk=bk, e=e: o.scalar_tensor_tensor(
                                out=y_v[:, ti, h*512:(h+1)*512], in0=bk[:],
                                scalar=ge_all[:, ti*8+e:ti*8+e+1],
                                in1=y_v[:, ti, h*512:(h+1)*512],
                                op0=OP.mult, op1=OP.add),
                              reads=[bn, f"ge{ti}", "y_sb"], writes=["y_sb"],
                              tag=f"D_ye{b}_{ti}_{h}")
                        else:
                            E("v", lambda o, ti=ti, h=h, bk=bk: o.scalar_tensor_tensor(
                                out=outt[ti % 2][:, h*512:(h+1)*512], in0=bk[:],
                                scalar=1.0, in1=y_v[:, ti, h*512:(h+1)*512],
                                op0=OP.mult, op1=OP.add),
                              reads=[bn, "y_sb"], writes=[f"outt{ti % 2}"],
                              tag=f"D_yz{ti}_{h}")
                            E("v", lambda o, ti=ti, h=h: o.tensor_tensor(
                                out=outt[ti % 2][:, h*512:(h+1)*512],
                                in0=outt[ti % 2][:, h*512:(h+1)*512],
                                in1=x1ld[ti % 2][:, h*512:(h+1)*512],
                                op=OP.add),
                              reads=[f"outt{ti % 2}", f"x1ld{ti % 2}"],
                              writes=[f"outt{ti % 2}"], tag=f"D_out{ti}_{h}")
                    if shared:
                        E("s", lambda o, ti=ti: o.dma_start(
                            out=out_hbm[ti*128:(ti+1)*128, :],
                            in_=outt[ti % 2][:]),
                          reads=[f"outt{ti % 2}"], writes=[f"outhbm{ti}"],
                          dma=True, tag=f"D_sto{ti}")
                if sb_i == 0 and b + 1 < len(branches):
                    emit_branch_loads(b + 1)

        pl.barrier()
        if not branches:
            E("v", lambda o: o.memset(outt[0][:], 0.0), writes=["outt0"],
              tag="dbg_z")
            for ti in range(NT):
                E("s", lambda o, ti=ti: o.dma_start(
                    out=out_hbm[ti*128:(ti+1)*128, :], in_=outt[0][:]),
                  reads=["outt0"], dma=True, tag=f"dbg_sto{ti}")
            pl.barrier()
        pl.finalize(nc, es, block)
    return nc


_NC_CACHE = {}


def build_cached(S=4096):
    if S not in _NC_CACHE:
        _NC_CACHE[S] = build_program(S)
    return _NC_CACHE[S]


def make_in_maps(L0_attn_x, L0_attn_Wq, L0_attn_Wk, L0_attn_Wv, L0_attn_Wr,
                 L0_ffn_gate_w, L0_ffn_gate_b, L0_ffn_ew1, L0_ffn_ew2,
                 L0_ffn_ew3, L0_ffn_fc1, L0_ffn_fc2, L0_ffn_fc3):
    x = np.ascontiguousarray(np.asarray(L0_attn_x, dtype=np.float32))
    f32 = lambda a: np.ascontiguousarray(np.asarray(a, dtype=np.float32))
    T2 = lambda a: np.ascontiguousarray(np.swapaxes(f32(a), -1, -2))
    gwt = np.zeros((DM, 8), np.float32)
    gwt[:, 0:6] = T2(L0_ffn_gate_w)
    gb = np.zeros((1, 8), np.float32)
    gb[0, 0:6] = f32(L0_ffn_gate_b)
    shared = {"wq": f32(L0_attn_Wq), "wk": f32(L0_attn_Wk),
              "wv": f32(L0_attn_Wv), "wr": f32(L0_attn_Wr),
              "gwt": gwt, "gb": gb,
              "ew1t": T2(L0_ffn_ew1), "ew3t": T2(L0_ffn_ew3),
              "ew2t": T2(L0_ffn_ew2),
              "fc1t": T2(L0_ffn_fc1), "fc2t": T2(L0_ffn_fc2),
              "fc3t": T2(L0_ffn_fc3),
              "ident": np.eye(128, dtype=ml_dtypes.bfloat16)}
    return [dict(shared, x=np.ascontiguousarray(x[b]))
            for b in range(x.shape[0])]


def kernel(**inputs):
    in_maps = make_in_maps(**inputs)
    nc = build_cached(4096)
    res = run_bass_kernel_spmd(nc, in_maps, core_ids=list(range(len(in_maps))))
    return np.stack([r["out"] for r in res.results], axis=0)


# version 12
# speedup vs baseline: 2.8993x; 1.0500x over previous
import os
import numpy as np
import ml_dtypes
from contextlib import ExitStack
import concourse.bass as bass
import concourse.mybir as mybir
from concourse.bass_utils import run_bass_kernel_spmd

F32 = mybir.dt.float32
BF16 = mybir.dt.bfloat16
AX = mybir.AxisListType.X
OP = mybir.AluOpType
AF = mybir.ActivationFunctionType

DM = 1024
NC_K = DM // 128
NE = 6
SUB = 512


class Plan:

    ENGINES = ("t", "v", "s", "y", "g")
    NSLOT = 8

    def __init__(self):
        self.steps = {e: [] for e in self.ENGINES}
        self.last_write = {}
        self.readers = {}
        self.dma_count = {e: 0 for e in self.ENGINES}
        self.pin_count = 0
        self.seq = 0

    def emit(self, eng, fn, reads=(), writes=(), dma=False, tag="", extra_deps=(),
             pin=False):
        idx = len(self.steps[eng])
        deps = set(extra_deps)
        for k in reads:
            w = self.last_write.get(k)
            if w is not None:
                deps.add(w)
        for k in writes:
            w = self.last_write.get(k)
            if w is not None:
                deps.add(w)
            last = {}
            for (de, di) in self.readers.get(k, ()):
                if self.steps[de][di]["dma"]:
                    deps.add((de, di))
                elif di > last.get(de, -1):
                    last[de] = di
            deps.update(last.items())
        deps = {(de, di) for (de, di) in deps if not (de == eng and di >= idx)}
        if eng == "t":
            deps = {(de, di) for (de, di) in deps if de != "t"}
        st = {"fn": fn, "deps": sorted(deps), "dma": dma, "tag": tag,
              "seq": self.seq}
        if dma:
            if pin:
                st["dslot"] = ("pin", self.pin_count)
                self.pin_count += 1
            else:
                st["dslot"] = self.dma_count[eng] % self.NSLOT
                self.dma_count[eng] += 1
        self.seq += 1
        self.steps[eng].append(st)
        ev = (eng, idx)
        for k in reads:
            self.readers.setdefault(k, []).append(ev)
        for k in writes:
            self.last_write[k] = ev
            self.readers[k] = []
        return ev

    def barrier(self):
        lasts = []
        for e in self.ENGINES:
            if not self.steps[e]:
                continue
            lasts.append((e, len(self.steps[e]) - 1))
            seen = set()
            for i in range(len(self.steps[e]) - 1, -1, -1):
                st = self.steps[e][i]
                if st["dma"] and st["dslot"] not in seen:
                    seen.add(st["dslot"])
                    lasts.append((e, i))

        def mknop():
            def f(obj):
                return obj.nop()
            return f
        for e in self.ENGINES:
            deps = [d for d in lasts if d[0] != e] +                    [d for d in lasts if d[0] == e and self.steps[e][d[1]]["dma"]]
            deps = [d for d in deps if not (d[0] == e and d[1] >= len(self.steps[e]))]
            self.steps[e].append({"fn": mknop(), "deps": sorted(set(deps)),
                                  "dma": False, "tag": "barrier",
                                  "seq": self.seq})
            self.seq += 1

    def queue_flush(self, eng):
        deps = []
        seen = set()
        for i in range(len(self.steps[eng]) - 1, -1, -1):
            st = self.steps[eng][i]
            if st["dma"] and st["dslot"] not in seen:
                seen.add(st["dslot"])
                deps.append((eng, i))

        def f(obj):
            return obj.nop()
        idx = len(self.steps[eng])
        self.steps[eng].append({"fn": f, "deps": sorted(deps), "dma": False,
                                "tag": f"flush_{eng}", "seq": self.seq})
        self.seq += 1
        return (eng, idx)

    def final_wait(self, eng, deps):
        def f(obj):
            return obj.nop()
        self.steps[eng].append({"fn": f, "deps": sorted(set(deps)),
                                "dma": False, "tag": "final", "seq": self.seq})
        self.seq += 1

    def finalize(self, nc, es, block):
        kseq = int(os.environ.get("KSEQ", "0"))
        if kseq:
            for e in self.ENGINES:
                self.steps[e] = [st for st in self.steps[e]
                                 if st["seq"] < kseq]
            self.last_write = {k: v for k, v in self.last_write.items()
                               if v[1] < len(self.steps[v[0]])}
            self.readers = {}
            self.barrier()
        consumers = {}
        for e in self.ENGINES:
            for i, st in enumerate(self.steps[e]):
                if st["tag"] in ("barrier", "final") or                         st["tag"].startswith("flush"):
                    continue
                for d in st["deps"]:
                    consumers.setdefault(d, []).append((e, i))
        def reduce_consumers(evs):
            keep, last = [], {}
            for (de, di) in evs:
                if self.steps[de][di]["dma"]:
                    keep.append((de, di))
                elif di > last.get(de, -1):
                    last[de] = di
            return keep + [(de, di) for de, di in last.items()]

        for e in self.ENGINES:
            prev_by_slot = {}
            for i, st in enumerate(self.steps[e]):
                if not st["dma"] or isinstance(st["dslot"], tuple):
                    continue
                sl = st["dslot"]
                pv = prev_by_slot.get(sl)
                if pv is not None:
                    add = [pv] + reduce_consumers(consumers.get(pv, []))
                    myseq = st["seq"]
                    extra = []
                    for (de, di) in add:
                        if de == e and di >= i:
                            continue
                        dseq = self.steps[de][di]["seq"]
                        assert dseq < myseq, (
                            "slot reuse forward dep", e, i, st["tag"],
                            de, di, self.steps[de][di]["tag"])
                        extra.append((de, di))
                    st["deps"] = sorted(set(st["deps"]) | set(extra))
                prev_by_slot[sl] = (e, i)
        need_mark = {e: set() for e in self.ENGINES}
        for e in self.ENGINES:
            for st in self.steps[e]:
                for (de, di) in st["deps"]:
                    need_mark[de].add(di)
        sems = {(e, p): es.enter_context(nc.semaphore(f"sem_{e}{p}"))
                for e in self.ENGINES for p in range(2)}
        dsems = {}
        for e in self.ENGINES:
            if self.dma_count[e]:
                n = min(self.NSLOT, self.dma_count[e])
                for sl in range(n):
                    dsems[(e, sl)] = es.enter_context(
                        nc.semaphore(f"dsem_{e}{sl}"))
            for i, st in enumerate(self.steps[e]):
                if st.get("dslot") is not None and isinstance(st["dslot"], tuple):
                    dsems[(e, st["dslot"])] = es.enter_context(
                        nc.semaphore(f"dsem_{e}p{st['dslot'][1]}"))
        ev_sem = {}
        slot_counts = {}
        for e in self.ENGINES:
            mcnt = [0, 0]
            nmark = 0
            for i, st in enumerate(self.steps[e]):
                if st["dma"]:
                    sl = st["dslot"]
                    c = slot_counts.get((e, sl), 0) + 16
                    slot_counts[(e, sl)] = c
                    st["inc_sem"] = dsems[(e, sl)]
                    st["inc"] = 16
                    ev_sem[(e, i)] = (st["inc_sem"], c)
                elif i in need_mark[e]:
                    p = nmark % 2
                    nmark += 1
                    mcnt[p] += 1
                    st["inc_sem"] = sems[(e, p)]
                    st["inc"] = 1
                    st["chain"] = (sems[(e, p)], mcnt[p] - 1)
                    ev_sem[(e, i)] = (sems[(e, p)], mcnt[p])
                else:
                    st["inc"] = 0
                    ev_sem[(e, i)] = None

        def emit_stream(eng, obj):
            water = {}
            for i, st in enumerate(self.steps[eng]):
                waits = {}
                for d in st["deps"]:
                    sv = ev_sem[d]
                    assert sv is not None, (eng, i, d, st["tag"])
                    sem, v = sv
                    if v > waits.get(id(sem), (None, -1))[1]:
                        waits[id(sem)] = (sem, v)
                ch = st.get("chain")
                if ch is not None and ch[1] > 0:
                    if ch[1] > waits.get(id(ch[0]), (None, -1))[1]:
                        waits[id(ch[0])] = ch
                for sem, v in waits.values():
                    if water.get(id(sem), -1) >= v:
                        continue
                    obj.wait_ge(sem, v)
                    water[id(sem)] = v
                ins = st["fn"](obj)
                if st["inc"]:
                    ins.then_inc(st["inc_sem"], st["inc"])

        @block.tensor
        def _(t):
            emit_stream("t", t)

        @block.vector
        def _(v):
            emit_stream("v", v)

        @block.scalar
        def _(s):
            emit_stream("s", s)

        @block.sync
        def _(y):
            emit_stream("y", y)

        @block.gpsimd
        def _(g):
            emit_stream("g", g)


def build_program(S=4096):
    PH = os.environ.get("KPHASES", "ABCD")
    KT = int(os.environ.get("KTILES", "0")) or 10**9
    NT = S // 128
    NSUB = S // SUB
    nc = bass.Bass("TRN2", target_bir_lowering=False)

    x_in = nc.dram_tensor("x", [S, DM], F32, kind="ExternalInput").ap()
    wq_in = nc.dram_tensor("wq", [DM, DM], F32, kind="ExternalInput").ap()
    wk_in = nc.dram_tensor("wk", [DM, DM], F32, kind="ExternalInput").ap()
    wv_in = nc.dram_tensor("wv", [DM, DM], F32, kind="ExternalInput").ap()
    wr_in = nc.dram_tensor("wr", [DM, DM], F32, kind="ExternalInput").ap()
    gwt_in = nc.dram_tensor("gwt", [DM, 8], F32, kind="ExternalInput").ap()
    gb_in = nc.dram_tensor("gb", [1, 8], F32, kind="ExternalInput").ap()
    ew1t_in = nc.dram_tensor("ew1t", [NE, DM, DM], F32, kind="ExternalInput").ap()
    ew3t_in = nc.dram_tensor("ew3t", [NE, DM, DM], F32, kind="ExternalInput").ap()
    ew2t_in = nc.dram_tensor("ew2t", [NE, DM, DM], F32, kind="ExternalInput").ap()
    fc1t_in = nc.dram_tensor("fc1t", [DM, DM], F32, kind="ExternalInput").ap()
    fc2t_in = nc.dram_tensor("fc2t", [DM, DM], F32, kind="ExternalInput").ap()
    fc3t_in = nc.dram_tensor("fc3t", [DM, DM], F32, kind="ExternalInput").ap()
    ident_in = nc.dram_tensor("ident", [128, 128], BF16, kind="ExternalInput").ap()
    out_hbm = nc.dram_tensor("out", [S, DM], F32, kind="ExternalOutput").ap()
    qx_hbm = nc.dram_tensor("qx_cache", [S, DM], F32, kind="Internal").ap()
    x1_hbm = nc.dram_tensor("x1_cache", [S, DM], F32, kind="Internal").ap()
    ln2t_hbm = nc.dram_tensor("ln2t_cache", [NC_K, 128, S], BF16, kind="Internal").ap()
    ln2t_hbm_v = ln2t_hbm.rearrange("c p t -> p c t")

    es = ExitStack()
    es_abc = ExitStack()
    with es:
        sb = lambda n, shp, dt: es.enter_context(nc.sbuf_tensor(n, shp, dt))
        sba = lambda n, shp, dt: es_abc.enter_context(nc.sbuf_tensor(n, shp, dt))
        psa = lambda n, shp, dt: es.enter_context(nc.psum_tensor(n, shp, dt))
        sb2 = lambda n, shp, dt: [sb(n + "0", shp, dt), sb(n + "1", shp, dt)]
        sba2 = lambda n, shp, dt: [sba(n + "0", shp, dt), sba(n + "1", shp, dt)]

        ident = sb("ident_sb", [128, 128], BF16)
        ones_col = sb("ones_col", [128, 1], BF16)
        ones_row = sb("ones_row", [1, 128], BF16)
        bias_b = sb("bias_b", [128, 8], F32)
        gw_bf = sb("gw_bf", [128, NC_K * 8], BF16)
        probs_all = sb("probs_all", [128, NT * 8], F32)
        ge_all = sb("ge_all", [128, NT * 8], F32)
        mx8a = sb2("mx8a", [128, 8], F32)
        mx8b = sb2("mx8b", [128, 8], F32)
        p2buf = sb2("p2buf", [128, 8], F32)
        p3buf = sb2("p3buf", [128, 8], F32)
        fmask = sb2("fmask", [128, 8], F32)
        smask = sb2("smask", [128, 8], F32)
        gcol = sb2("gcol", [128, 4], F32)
        ptmp = sb2("ptmp", [128, 8], F32)
        etmp = sb2("etmp", [128, 8], F32)
        stats = sb2("stats", [128, 12], F32)
        mv = sb2("mv", [128, 2], F32)
        rstd = sb2("rstd", [128, 2], F32)
        nbias = sb2("nbias", [128, 1], F32)
        mcol = sb2("mcol", [128, 4], F32)
        zcol = sb2("zcol", [128, 4], F32)
        gk_f = sb("gk_f", [128, 8], F32)
        ceps = sb("ceps", [128, 1], F32)
        gb_row = sb("gb_row", [1, 8], F32)
        gb_row_bf = sb("gb_row_bf", [1, 8], BF16)

        P = [psa(f"P{i}", [128, 512], F32) for i in range(6)]
        es_a = ExitStack()
        gqP = es_a.enter_context(nc.psum_tensor("gqP", [2, 512], F32))
        es_b = ExitStack()

        block = es.enter_context(nc.Block())

        pl = Plan()
        E = pl.emit

        ln1T = sba("ln1T", [128, NC_K * S], BF16)
        wA = [None] * 4
        wA[0] = sba("wA0", [128, NC_K * DM], BF16)
        wA[2] = sba("wA2", [128, NC_K * DM], BF16)
        wA[3] = sba("wA3", [128, NC_K * DM], BF16)
        xb = [sba(f"xb{i}", [128, DM], F32) for i in range(5)]
        x1t = sba2("x1t", [128, DM], F32)
        ln1_bf = [sba(f"ln1_bf{i}", [128, DM], BF16) for i in range(3)]
        expbuf = sba2("expbuf", [128, DM], BF16)
        wbuf = sba2("wbuf", [128, DM], BF16)
        gq_row_bf = sba("gq_row_bf", [1, DM], BF16)
        gq_bcast = sba("gq_bcast", [128, DM], BF16)
        gkf = sba("gkf", [128, 8], F32)
        tmp2 = sba("tmp2", [2, 512], BF16)
        rZf = sba2("rZf", [128, 1], F32)
        rZc2 = sba2("rZc2", [128, 3], BF16)
        wA[1] = es_b.enter_context(nc.sbuf_tensor("wA1", [128, NC_K * DM], BF16))
        pbuf = [es_b.enter_context(nc.sbuf_tensor(f"pbuf{i}", [128, DM], F32))
                for i in range(2)]

        ln1T_v = ln1T[:].rearrange("p (c t) -> p c t", c=NC_K)
        wA_v = [w[:].rearrange("p (c f) -> p c f", c=NC_K) for w in wA]
        gw_v = gw_bf[:].rearrange("p (c f) -> p c f", c=NC_K)
        E("y", lambda o: o.dma_start(out=ident[:], in_=ident_in),
          writes=["ident"], dma=True, tag="ld_ident", pin=True)
        E("v", lambda o: o.memset(ones_col[:], 1.0), writes=["ones_col"])
        E("v", lambda o: o.memset(ones_row[:], 1.0), writes=["ones_row"])
        E("v", lambda o: o.memset(mx8a[0][:], -3e30), writes=["mx8a0"])
        E("v", lambda o: o.memset(mx8a[1][:], -3e30), writes=["mx8a1"])
        E("v", lambda o: o.memset(mx8b[0][:], -3e30), writes=["mx8b0"])
        E("v", lambda o: o.memset(mx8b[1][:], -3e30), writes=["mx8b1"])
        E("v", lambda o: o.memset(probs_all[:], -2e30), writes=["probs_all"])
        E("v", lambda o: o.memset(ceps[:], 1e-5), writes=["ceps"])
        E("v", lambda o: o.memset(rZc2[0][:], 0.0), writes=["rZc20i"])
        E("v", lambda o: o.memset(rZc2[1][:], 0.0), writes=["rZc21i"])
        E("g", lambda o: o.dma_start(
            out=gw_v[:, :, 0:8], in_=gwt_in.rearrange("(c p) f -> p c f", p=128)),
          writes=["gw_bf"], dma=True, tag="ld_gw", pin=True)
        E("y", lambda o: o.dma_start(out=gb_row[0:1, :], in_=gb_in),
          writes=["bias_row"], dma=True, tag="ld_gb", pin=True)
        E("v", lambda o: o.tensor_copy(out=gb_row_bf[:], in_=gb_row[:]),
          reads=["bias_row"], writes=["gb_row_bf"], tag="gb_cast")
        E("t", lambda o: o.matmul(P[0][:, 0:8], ones_row[:], gb_row_bf[0:1, :],
                                  start=True, stop=True),
          reads=["gb_row_bf", "ones_row"], writes=["P0"], tag="gb_mm")
        E("v", lambda o: o.tensor_copy(out=bias_b[:], in_=P[0][:, 0:8]),
          reads=["P0"], writes=["bias_b"], tag="gb_ev")

        def wload(bufkey, dst, src, tag):
            E("g", lambda o, d=dst, s_=src: o.dma_start(
                out=d, in_=s_.rearrange("(c p) f -> p c f", p=128)),
              writes=[bufkey], dma=True, tag=tag)

        wload("wA0", wA[0][:], wq_in, "ld_wq")
        wload("wA1", wA[1][:], wk_in, "ld_wk")
        wload("wA2", wA[2][:], wv_in, "ld_wv")
        wload("wA3", wA[3][:], wr_in, "ld_wr")

        def layernorm(par, src_buf, src_key, dst_buf, dst_key, tg):
            st_, mv_, rs_, nb_ = stats[par], mv[par], rstd[par], nbias[par]
            E("v", lambda o: o.bn_stats(out=st_[:, 0:6], in_=src_buf[:, 0:512]),
              reads=[src_key], writes=[f"stats{par}a"], tag=f"{tg}_bnsa")
            E("v", lambda o: o.bn_stats(out=st_[:, 6:12], in_=src_buf[:, 512:1024]),
              reads=[src_key], writes=[f"stats{par}b"], tag=f"{tg}_bnsb")
            E("v", lambda o: o.bn_aggr(out=mv_[:], in_=st_[:]),
              reads=[f"stats{par}a", f"stats{par}b"], writes=[f"mv{par}"],
              tag=f"{tg}_bna")
            E("s", lambda o: o.activation(out=rs_[:, 0:1], in_=mv_[:, 1:2],
                                          func=AF.Ln, bias=ceps[:], scale=1.0),
              reads=[f"mv{par}", "ceps"], writes=[f"rstd{par}0"], tag=f"{tg}_ln_")
            E("s", lambda o: o.activation(out=rs_[:, 1:2], in_=rs_[:, 0:1],
                                          func=AF.Exp, bias=0.0, scale=-0.5),
              reads=[f"rstd{par}0"], writes=[f"rstd{par}1"], tag=f"{tg}_rex")
            E("v", lambda o: o.tensor_scalar(out=nb_[:], in0=mv_[:, 0:1],
                                             scalar1=rs_[:, 1:2], scalar2=-1.0,
                                             op0=OP.mult, op1=OP.mult),
              reads=[f"mv{par}", f"rstd{par}1"], writes=[f"nbias{par}"],
              tag=f"{tg}_nb")
            E("s", lambda o: o.activation(out=dst_buf[:], in_=src_buf[:],
                                          func=AF.Identity, bias=nb_[:],
                                          scale=rs_[:, 1:2]),
              reads=[src_key, f"nbias{par}", f"rstd{par}1"], writes=[dst_key],
              tag=f"{tg}_ln")

        NTA = min(KT, NT) if 'A' in PH else 0

        def A_LDX(i):
            E("y", lambda o, i=i: o.dma_start(
                out=xb[i % 5][:], in_=x_in[i*128:(i+1)*128, :]),
              writes=[f"xb{i % 5}"], dma=True, tag=f"A_ldx{i}")

        def A_LN(i):
            layernorm(i % 2, xb[i % 5], f"xb{i % 5}", ln1_bf[i % 3],
                      f"ln1_bf{i % 3}", f"A{i}")

        def A_TRD(i):
            E("y", lambda o, i=i: o.dma_start(
                out=ln1T_v[:, :, i*128:(i+1)*128], in_=ln1_bf[i % 3][:],
                transpose=True),
              reads=[f"ln1_bf{i % 3}"], writes=[f"ln1T{i}"], dma=True,
              tag=f"A_trd{i}")

        def MM16(i, wkey, wv, ph, lnread=True, extra=()):
            pslot = (i % 3) * 2
            rd = [f"ln1T{i}", wkey] if lnread else [wkey]
            for h in range(2):
                bnk, bk = P[pslot + h], f"P{pslot + h}"
                for c in range(NC_K):
                    E("t", lambda o, h=h, c=c, i=i, bnk=bnk, wv=wv: o.matmul(
                        bnk[:], ln1T_v[:, c, i*128:(i+1)*128],
                        wv[:, c, h*512:(h+1)*512],
                        start=(c == 0), stop=(c == NC_K - 1)),
                      reads=rd, writes=[bk],
                      extra_deps=extra, tag=f"{ph}_mm{i}_{h}_{c}")

        def A_P1(i):
            par, pslot = i % 2, (i % 3) * 2
            mc = mcol[par]
            E("v", lambda o, b=P[pslot], mc=mc: o.reduce_max(
                out=mc[:, 0:1], in_=b[:], axis=AX),
              reads=[f"P{pslot}"], writes=[f"m0_{par}"], tag=f"A_m0{i}")
            E("v", lambda o, b=P[pslot+1], mc=mc: o.reduce_max(
                out=mc[:, 1:2], in_=b[:], axis=AX),
              reads=[f"P{pslot+1}"], writes=[f"m1_{par}"], tag=f"A_m1{i}")
            E("v", lambda o, mc=mc: o.tensor_scalar(
                out=mc[:, 3:4], in0=mc[:, 0:1], scalar1=mc[:, 1:2],
                scalar2=-1.0 / 32.0, op0=OP.max, op1=OP.mult),
              reads=[f"m0_{par}", f"m1_{par}"], writes=[f"negM{par}"],
              tag=f"A_nM{i}")
            for h in range(2):
                E("v", lambda o, h=h, b=P[pslot+h], par=par, i=i: o.tensor_tensor(
                    out=x1t[par][:, h*512:(h+1)*512], in0=b[:],
                    in1=xb[i % 5][:, h*512:(h+1)*512], op=OP.add),
                  reads=[f"P{pslot+h}", f"xb{i % 5}"], writes=[f"x1t{par}"],
                  tag=f"A_qx{i}_{h}")
            E("s", lambda o, i=i, par=par: o.dma_start(
                out=qx_hbm[i*128:(i+1)*128, :], in_=x1t[par][:]),
              reads=[f"x1t{par}"], writes=[f"qxhbm{i}"], dma=True,
              tag=f"A_stq{i}")

        def A_EXP(i):
            par, pslot = i % 2, (i % 3) * 2
            mc, zc = mcol[par], zcol[par]
            for h in range(2):
                E("s", lambda o, h=h, b=P[pslot+h], par=par, mc=mc, zc=zc:
                  o.activation(
                    out=expbuf[par][:, h*512:(h+1)*512], in_=b[:], func=AF.Exp,
                    bias=mc[:, 3:4], scale=1.0/32.0, accum_out=zc[:, h:h+1]),
                  reads=[f"P{pslot+h}", f"negM{par}"],
                  writes=[f"exp{par}_{h}", f"z{par}{h}"], tag=f"A_e{i}_{h}")

        def A_P2(i):
            par, pslot = i % 2, (i % 3) * 2
            zc = zcol[par]
            E("v", lambda o, zc=zc: o.tensor_tensor(
                out=zc[:, 2:3], in0=zc[:, 0:1], in1=zc[:, 1:2], op=OP.add),
              reads=[f"z{par}0", f"z{par}1"], writes=[f"Z{par}"], tag=f"A_Z{i}")
            E("v", lambda o, zc=zc, par=par: o.reciprocal(
                out=rZf[par][:], in_=zc[:, 2:3]),
              reads=[f"Z{par}"], writes=[f"rZf{par}"], tag=f"A_rZ{i}")
            E("v", lambda o, par=par: o.tensor_copy(
                out=rZc2[par][:, 0:1], in_=rZf[par][:]),
              reads=[f"rZf{par}"], writes=[f"rZc{par}"], tag=f"A_rc{i}")
            E("v", lambda o, par=par: o.tensor_copy(
                out=rZc2[par][:, 2:3], in_=rZf[par][:]),
              reads=[f"rZf{par}"], writes=[f"rZc2{par}"], tag=f"A_rc2{i}")
            for h in range(2):
                E("v", lambda o, h=h, b=P[pslot+h], par=par: o.tensor_tensor(
                    out=wbuf[par][:, h*512:(h+1)*512], in0=b[:],
                    in1=expbuf[par][:, h*512:(h+1)*512], op=OP.mult),
                  reads=[f"P{pslot+h}", f"exp{par}_{h}"], writes=[f"w{par}_{h}"],
                  tag=f"A_w{i}_{h}")

        def A_GQ(i):
            par = i % 2
            for h in range(2):
                E("t", lambda o, h=h, par=par, i=i: o.matmul(
                    gqP[0:2, :], rZc2[par][:, h:h+2],
                    wbuf[par][:, h*512:(h+1)*512],
                    start=(i == 0 and h == 0),
                    stop=(i == NTA - 1 and h == 1)),
                  reads=[f"rZc{par}", f"rZc2{par}", f"w{par}_{h}"],
                  writes=["gqP"], tag=f"A_gq{i}_{h}")

        if NTA:
            for j in range(min(3, NTA)):
                A_LDX(j)
            for j in range(min(2, NTA)):
                A_LN(j)
            A_TRD(0)
            for i in range(NTA):
                if i + 2 < NTA:
                    A_LN(i + 2)
                A_MM_ = MM16(i, "wA0", wA_v[0], "A")
                if i >= 3:
                    A_GQ(i - 3)
                if i >= 1:
                    A_P1(i - 1)
                    A_EXP(i - 1)
                if i >= 2:
                    A_P2(i - 2)
                if i + 3 < NTA:
                    A_LDX(i + 3)
                if i + 1 < NTA:
                    A_TRD(i + 1)
            A_P1(NTA - 1)
            A_EXP(NTA - 1)
            A_P2(NTA - 2)
            A_P2(NTA - 1)
            for j in range(max(0, NTA - 3), NTA):
                A_GQ(j)

        fold_ev = E("v", lambda o: o.tensor_copy(out=tmp2[:], in_=gqP[0:2, :]),
                    reads=["gqP"], writes=["tmp2"], tag="gq_ev")
        E("y", lambda o: o.dma_start(out=gq_row_bf[0:1, 0:512],
                                     in_=tmp2[0:1, :]),
          reads=["tmp2"], writes=["gq_row_bf"], dma=True, tag="gq_d0")
        E("y", lambda o: o.dma_start(out=gq_row_bf[0:1, 512:1024],
                                     in_=tmp2[1:2, :]),
          reads=["tmp2"], writes=["gq_row_bf"], dma=True, tag="gq_d1")
        for h in range(2):
            E("t", lambda o, h=h: o.matmul(P[h][:], ones_row[:],
                                           gq_row_bf[0:1, h*512:(h+1)*512],
                                           start=True, stop=True),
              reads=["gq_row_bf", "ones_row"], writes=[f"P{h}"],
              tag=f"gqb_mm{h}")
            E("v", lambda o, h=h: o.tensor_copy(
                out=gq_bcast[:, h*512:(h+1)*512], in_=P[h][:]),
              reads=[f"P{h}"], writes=["gqb"], tag=f"gqb_ev{h}")
        es_a.close()
        gkP = es_b.enter_context(nc.psum_tensor("gkP", [128, 8], F32))
        qx_flush = pl.queue_flush("s")
        ln1T_done = pl.queue_flush("y")

        def B_GK(i, extra=()):
            par = i % 2
            for c in range(NC_K):
                E("t", lambda o, c=c, par=par, i=i: o.matmul(
                    gkP[:, c:c+1], wbuf[par][:, c*128:(c+1)*128],
                    rZc2[par][:, 0:1],
                    start=(i == 0 and c == 0),
                    stop=(i == NT - 1 and c == NC_K - 1)),
                  reads=[f"w{par}_{c // 4}", f"rZc{par}"], writes=["gkP"],
                  extra_deps=(extra if i == 0 else ()), tag=f"B_gk{i}_{c}")

        def B_P1(i):
            par, pslot = i % 2, (i % 3) * 2
            pb, mc = pbuf[par], mcol[par]
            for h in range(2):
                E("v", lambda o, h=h, b=P[pslot+h], pb=pb: o.tensor_tensor(
                    out=pb[:, h*512:(h+1)*512], in0=b[:],
                    in1=gq_bcast[:, h*512:(h+1)*512], op=OP.mult),
                  reads=[f"P{pslot+h}", "gqb"], writes=[f"pb{par}_{h}"],
                  tag=f"B_p{i}_{h}")
            E("v", lambda o, pb=pb, mc=mc: o.reduce_max(
                out=mc[:, 2:3], in_=pb[:], axis=AX),
              reads=[f"pb{par}_0", f"pb{par}_1"], writes=[f"M{par}"],
              tag=f"B_M{i}")
            E("v", lambda o, mc=mc: o.tensor_scalar(
                out=mc[:, 3:4], in0=mc[:, 2:3], scalar1=-1.0/32.0,
                scalar2=None, op0=OP.mult),
              reads=[f"M{par}"], writes=[f"negM{par}"], tag=f"B_nM{i}")

        def B_EXP(i):
            par = i % 2
            pb, mc, zc = pbuf[par], mcol[par], zcol[par]
            E("s", lambda o, pb=pb, par=par, mc=mc, zc=zc: o.activation(
                out=expbuf[par][:], in_=pb[:], func=AF.Exp,
                bias=mc[:, 3:4], scale=1.0/32.0, accum_out=zc[:, 2:3]),
              reads=[f"pb{par}_0", f"pb{par}_1", f"negM{par}"],
              writes=[f"exp{par}_0", f"exp{par}_1", f"Z{par}"], tag=f"B_e{i}")

        def B_P2(i):
            par = i % 2
            pb, zc = pbuf[par], zcol[par]
            E("v", lambda o, zc=zc, par=par: o.reciprocal(
                out=rZf[par][:], in_=zc[:, 2:3]),
              reads=[f"Z{par}"], writes=[f"rZf{par}"], tag=f"B_rZ{i}")
            E("v", lambda o, par=par: o.tensor_copy(
                out=rZc2[par][:, 0:1], in_=rZf[par][:]),
              reads=[f"rZf{par}"], writes=[f"rZc{par}"], tag=f"B_rc{i}")
            E("v", lambda o, pb=pb, par=par: o.tensor_tensor(
                out=wbuf[par][:], in0=pb[:], in1=expbuf[par][:], op=OP.mult),
              reads=[f"pb{par}_0", f"pb{par}_1", f"exp{par}_0", f"exp{par}_1"],
              writes=[f"w{par}_0", f"w{par}_1"], tag=f"B_w{i}")

        def B_WVTD(c_):
            E("y", lambda o, c_=c_: o.dma_start(
                out=wA_v[0][:, :, c_*128:(c_+1)*128],
                in_=wA[2][:][:, c_*DM:(c_+1)*DM], transpose=True),
              reads=["wA2"], writes=["wA0", "wvT"], dma=True,
              tag=f"B_wvtd{c_}")

        KB = os.environ.get("KB", "gwp")
        if 'B' in PH:
            for i in range(NT):
                MM16(i, "wA1", wA_v[1], "B", lnread=False,
                     extra=[ln1T_done])
                if i >= 3 and 'g' in KB:
                    B_GK(i - 3, extra=[fold_ev])
                if i >= 1 and 'p' in KB:
                    B_P1(i - 1)
                    B_EXP(i - 1)
                if i >= 2 and 'p' in KB:
                    B_P2(i - 2)
                if i % 4 == 0 and 'w' in KB:
                    B_WVTD(i // 4)
            if 'p' in KB:
                B_P1(NT - 1)
                B_EXP(NT - 1)
                B_P2(NT - 2)
                B_P2(NT - 1)
            if 'g' in KB:
                for j in range(NT - 3, NT):
                    B_GK(j)

        E("v", lambda o: o.tensor_copy(out=gkf[:], in_=gkP[:, 0:8]),
          reads=["gkP"], writes=["gkf"], tag="gk_ev")
        for c in range(NC_K):
            if c % 2 == 0:
                E("s", lambda o, c=c: o.activation(
                    out=wA_v[2][:, c, :], in_=wA_v[3][:, c, :],
                    func=AF.Identity, bias=0.0, scale=gkf[:, c:c+1]),
                  reads=["wA3", "gkf"], writes=["wA2", "wp"], tag=f"wp_s{c}")
            else:
                E("v", lambda o, c=c: o.tensor_scalar(
                    out=wA_v[2][:, c, :], in0=wA_v[3][:, c, :],
                    scalar1=gkf[:, c:c+1], scalar2=None, op0=OP.mult),
                  reads=["wA3", "gkf"], writes=["wA2", "wp"], tag=f"wp_v{c}")
        for ab in range(NC_K):
            for hb in range(2):
                slot = (2 * ab + hb) % 6
                bnk, bk = P[slot], f"P{slot}"
                for cd in range(NC_K):
                    E("t", lambda o, ab=ab, hb=hb, cd=cd, bnk=bnk: o.matmul(
                        bnk[:], wA_v[0][:, cd, ab*128:(ab+1)*128],
                        wA_v[2][:, cd, hb*512:(hb+1)*512],
                        start=(cd == 0), stop=(cd == NC_K - 1)),
                      reads=["wvT", "wp"], writes=[bk],
                      tag=f"wvr_{ab}_{hb}_{cd}")
                E("v", lambda o, ab=ab, hb=hb, bnk=bnk: o.tensor_copy(
                    out=wA_v[3][:, ab, hb*512:(hb+1)*512], in_=bnk[:]),
                  reads=[bk], writes=["wA3", "wvr"], tag=f"wvre_{ab}_{hb}")

        es_b.close()
        pg = es_abc.enter_context(nc.psum_tensor("pg", [128, 8], F32))
        qxld = [es_abc.enter_context(nc.sbuf_tensor(f"qxld{i}", [128, DM], F32))
                for i in range(2)]
        ln2_bf = [es_abc.enter_context(
            nc.sbuf_tensor(f"ln2_bf{i}", [128, DM], BF16)) for i in range(2)]
        ln2st = es_abc.enter_context(
            nc.sbuf_tensor("ln2st", [128, NC_K * 8 * 128], BF16))
        ln2st_v = ln2st[:].rearrange("p (c t) -> p c t", c=NC_K)
        pl.barrier()

        def C_LD(i):
            E("y", lambda o, i=i: o.dma_start(
                out=qxld[i % 2][:], in_=qx_hbm[i*128:(i+1)*128, :]),
              writes=[f"qxld{i % 2}"], dma=True, extra_deps=[qx_flush],
              tag=f"C_ldqx{i}")

        def C_T1(i):
            par, pslot = i % 2, (i % 3) * 2
            for h in range(2):
                E("v", lambda o, h=h, par=par, b=P[pslot+h], i=i: o.tensor_tensor(
                    out=x1t[par][:, h*512:(h+1)*512], in0=b[:],
                    in1=qxld[i % 2][:, h*512:(h+1)*512], op=OP.add),
                  reads=[f"P{pslot+h}", f"qxld{i % 2}"], writes=[f"x1t{par}"],
                  tag=f"C_x1{i}_{h}")
            E("s", lambda o, i=i, par=par: o.dma_start(
                out=x1_hbm[i*128:(i+1)*128, :], in_=x1t[par][:]),
              reads=[f"x1t{par}"], writes=[f"x1hbm{i}"], dma=True,
              tag=f"C_stx1{i}")
            layernorm(par, x1t[par], f"x1t{par}", ln2_bf[par], f"ln2_bf{par}",
                      f"C{i}")

        def C_TRD(i):
            E("y", lambda o, i=i: o.dma_start(
                out=ln2st_v[:, :, (i % 8)*128:(i % 8 + 1)*128],
                in_=ln2_bf[i % 2][:], transpose=True),
              reads=[f"ln2_bf{i % 2}"], writes=[f"ln2st{i % 8}"], dma=True,
              tag=f"C_trd{i}")

        def C_T2(i):
            t8 = i % 8
            par = i % 2
            for c in range(NC_K):
                E("t", lambda o, c=c, t8=t8: o.matmul(
                    pg[:, 0:6], ln2st_v[:, c, t8*128:(t8+1)*128],
                    gw_v[:, c, 0:6], start=(c == 0), stop=(c == NC_K - 1)),
                  reads=[f"ln2st{t8}", "gw_bf"], writes=["pg"],
                  tag=f"C_gmm{i}_{c}")
            E("v", lambda o, par=par: o.tensor_tensor(
                out=ptmp[par][:, 0:6], in0=pg[:, 0:6],
                in1=bias_b[:, 0:6], op=OP.add),
              reads=["pg", "bias_b"], writes=[f"ptmp{par}"], tag=f"C_gb{i}")
            E("s", lambda o, par=par: o.activation(
                out=etmp[par][:, 0:6], in_=ptmp[par][:, 0:6],
                func=AF.Exp, bias=0.0, scale=-1.0),
              reads=[f"ptmp{par}"], writes=[f"etmp{par}"], tag=f"C_sg1{i}")
            E("v", lambda o, par=par: o.tensor_scalar(
                out=etmp[par][:, 0:6], in0=etmp[par][:, 0:6], scalar1=1.0,
                scalar2=None, op0=OP.add),
              reads=[f"etmp{par}"], writes=[f"etmp{par}"], tag=f"C_sg2{i}")
            E("v", lambda o, ti=i, par=par: o.reciprocal(
                out=probs_all[:, ti*8:ti*8+6], in_=etmp[par][:, 0:6]),
              reads=[f"etmp{par}"], writes=[f"probs{i}"], tag=f"C_sg3{i}")
            ti = i
            o8 = ti * 8
            Pt = probs_all[:, o8:o8+8]
            ma, mb = mx8a[par], mx8b[par]
            p2_, p3_ = p2buf[par], p3buf[par]
            fm, sm, gc = fmask[par], smask[par], gcol[par]
            E("v", lambda o, Pt=Pt, ma=ma: o.reduce_max(
                out=ma[:, 0:1], in_=Pt, axis=AX),
              reads=[f"probs{ti}", "probs_all"], writes=[f"mx8a_v{par}"],
              tag=f"G_m1{ti}")
            E("v", lambda o, Pt=Pt, ma=ma, p2_=p2_: o.match_replace(
                out=p2_[:], in_to_replace=ma[:], in_values=Pt,
                imm_value=-2e30),
              reads=[f"probs{ti}", "probs_all", f"mx8a{par}",
                     f"mx8a_v{par}"],
              writes=[f"p2buf{par}"], tag=f"G_mr1{ti}")
            E("v", lambda o, mb=mb, p2_=p2_: o.reduce_max(
                out=mb[:, 0:1], in_=p2_[:], axis=AX),
              reads=[f"p2buf{par}"], writes=[f"mx8b_v{par}"],
              tag=f"G_m2{ti}")
            E("v", lambda o, Pt=Pt, fm=fm, p2_=p2_: o.tensor_tensor(
                out=fm[:], in0=Pt, in1=p2_[:], op=OP.not_equal),
              reads=[f"probs{ti}", "probs_all", f"p2buf{par}"],
              writes=[f"fmask{par}"], tag=f"G_f{ti}")
            E("v", lambda o, mb=mb, p2_=p2_, p3_=p3_: o.match_replace(
                out=p3_[:], in_to_replace=mb[:], in_values=p2_[:],
                imm_value=-2e30),
              reads=[f"p2buf{par}", f"mx8b{par}", f"mx8b_v{par}"],
              writes=[f"p3buf{par}"], tag=f"G_mr2{ti}")
            E("v", lambda o, sm=sm, p2_=p2_, p3_=p3_: o.tensor_tensor(
                out=sm[:], in0=p2_[:], in1=p3_[:], op=OP.not_equal),
              reads=[f"p2buf{par}", f"p3buf{par}"], writes=[f"smask{par}"],
              tag=f"G_s{ti}")
            E("v", lambda o, gc=gc, ma=ma, mb=mb: o.scalar_tensor_tensor(
                out=gc[:, 0:1], in0=ma[:, 0:1], scalar=1e-8, in1=mb[:, 0:1],
                op0=OP.add, op1=OP.add),
              reads=[f"mx8a_v{par}", f"mx8b_v{par}"], writes=[f"den{par}"],
              tag=f"G_d{ti}")
            E("v", lambda o, gc=gc: o.reciprocal(out=gc[:, 2:3],
                                                 in_=gc[:, 0:1]),
              reads=[f"den{par}"], writes=[f"rden{par}"], tag=f"G_rd{ti}")
            E("v", lambda o, gc=gc, ma=ma: o.tensor_scalar(
                out=gc[:, 3:4], in0=ma[:, 0:1], scalar1=gc[:, 2:3],
                scalar2=None, op0=OP.mult),
              reads=[f"mx8a_v{par}", f"rden{par}"], writes=[f"gm{par}"],
              tag=f"G_g1{ti}")
            E("v", lambda o, o8=o8, fm=fm, gc=gc: o.tensor_scalar(
                out=ge_all[:, o8:o8+8], in0=fm[:], scalar1=gc[:, 3:4],
                scalar2=None, op0=OP.mult),
              reads=[f"fmask{par}", f"gm{par}"], writes=[f"ge{ti}"],
              tag=f"G_ge1{ti}")
            E("v", lambda o, gc=gc, mb=mb: o.tensor_scalar(
                out=gc[:, 3:4], in0=mb[:, 0:1], scalar1=gc[:, 2:3],
                scalar2=None, op0=OP.mult),
              reads=[f"mx8b_v{par}", f"rden{par}"], writes=[f"gm{par}"],
              tag=f"G_g2{ti}")
            E("v", lambda o, o8=o8, sm=sm, gc=gc: o.scalar_tensor_tensor(
                out=ge_all[:, o8:o8+8], in0=sm[:], scalar=gc[:, 3:4],
                in1=ge_all[:, o8:o8+8], op0=OP.mult, op1=OP.add),
              reads=[f"smask{par}", f"gm{par}", f"ge{ti}"],
              writes=[f"ge{ti}"], tag=f"G_ge2{ti}")
            if i % 4 == 3:
                ch = i // 4
                E("g", lambda o, ch=ch: o.dma_start(
                    out=ln2t_hbm_v[:, :, ch*SUB:(ch+1)*SUB],
                    in_=ln2st_v[:, :, (ch % 2)*SUB:(ch % 2 + 1)*SUB]),
                  reads=[f"ln2st{q}" for q in range((ch % 2)*4,
                                                    (ch % 2)*4 + 4)],
                  writes=[f"ln2thbm{ch}"], dma=True, tag=f"C_stln2{ch}")

        if 'C' in PH:
            C_LD(0)
            for i in range(NT):
                MM16(i, "wvr", wA_v[3], "C", lnread=False,
                     extra=[ln1T_done])
                if i >= 3:
                    C_T2(i - 3)
                if i >= 2:
                    C_TRD(i - 2)
                if i >= 1:
                    C_T1(i - 1)
                if i + 1 < NT:
                    C_LD(i + 1)
            C_TRD(NT - 2)
            C_T1(NT - 1)
            C_TRD(NT - 1)
            for j in range(NT - 3, NT):
                C_T2(j)
        es_abc.close()
        pl.barrier()

        y_sb = sb("y_sb", [128, NT * DM], BF16)
        wF = [sb(f"wF{i}", [128, NC_K * DM], BF16) for i in range(4)]
        slab = [sb(f"slab{i}", [128, NC_K * SUB], BF16) for i in range(2)]
        h1s = [sb(f"h1s{i}", [128, SUB], F32) for i in range(2)]
        hb = sb("hb", [128, NC_K * SUB], BF16)
        x1ld = [sb(f"x1ld{i}", [128, DM], F32) for i in range(2)]
        outt = [sb(f"outt{i}", [128, DM], F32) for i in range(2)]
        wF_v = [w[:].rearrange("p (c f) -> p c f", c=NC_K) for w in wF]
        slab_v = [s_[:].rearrange("p (c t) -> p c t", c=NC_K) for s_ in slab]
        hb_v = hb[:].rearrange("p (c t) -> p c t", c=NC_K)
        y_v = y_sb[:].rearrange("p (n f) -> p n f", n=NT)

        E("v", lambda o: o.memset(y_sb[:], 0.0), writes=["y_sb"], tag="D_zy")

        branches = [(ew1t_in[e], ew3t_in[e], ew2t_in[e], e) for e in range(NE)]
        branches.append((fc1t_in, fc2t_in, fc3t_in, None))
        if "D" not in PH:
            branches = branches[-1:] if "d" in PH else []

        wslot = {}

        def emit_branch_loads(b):
            w1, w3, w2, _e = branches[b]
            for j, wsrc in enumerate((w1, w3, w2)):
                slot = (3 * b + j) % 4
                wslot[(b, j)] = slot
                E("g", lambda o, slot=slot, s_=wsrc: o.dma_start(
                    out=wF[slot][:], in_=s_.rearrange("(c p) f -> p c f", p=128)),
                  writes=[f"slot{slot}"], dma=True, tag=f"ld_w{b}_{j}")

        if branches:
            emit_branch_loads(0)

        allhb = [f"hb{j}" for j in range(NC_K)]
        for b, (w1, w3, w2, e) in enumerate(branches):
            shared = e is None
            s1, s3, s2 = wslot[(b, 0)], wslot[(b, 1)], wslot[(b, 2)]
            for sb_i in range(NSUB):
                sl, slv = slab[sb_i % 2], slab_v[sb_i % 2]
                slk = f"slab{sb_i % 2}"
                E("y", lambda o, sb_i=sb_i, sl=sl: o.dma_start(
                    out=sl[:], in_=ln2t_hbm_v[:, :, sb_i*SUB:(sb_i+1)*SUB]),
                  writes=[slk], dma=True, tag=f"D_lds{b}_{sb_i}")
                for j in range(NC_K):
                    b1k, b1n = P[j % 2], f"P{j % 2}"
                    b3k, b3n = P[2 + j % 2], f"P{2 + j % 2}"
                    for c in range(NC_K):
                        E("t", lambda o, j=j, c=c, b1k=b1k, slv=slv, s1=s1: o.matmul(
                            b1k[:], wF_v[s1][:, c, j*128:(j+1)*128],
                            slv[:, c, :],
                            start=(c == 0), stop=(c == NC_K - 1)),
                          reads=[f"slot{s1}", slk], writes=[b1n],
                          tag=f"D_h1_{b}_{sb_i}_{j}_{c}")
                    for c in range(NC_K):
                        E("t", lambda o, j=j, c=c, b3k=b3k, slv=slv, s3=s3: o.matmul(
                            b3k[:], wF_v[s3][:, c, j*128:(j+1)*128],
                            slv[:, c, :],
                            start=(c == 0), stop=(c == NC_K - 1)),
                          reads=[f"slot{s3}", slk], writes=[b3n],
                          tag=f"D_h3_{b}_{sb_i}_{j}_{c}")
                    E("s", lambda o, j=j, b1k=b1k: o.activation(
                        out=h1s[j % 2][:], in_=b1k[:], func=AF.Silu),
                      reads=[b1n], writes=[f"h1s{j % 2}"],
                      tag=f"D_si{b}_{sb_i}_{j}")
                    E("v", lambda o, j=j, b3k=b3k: o.tensor_tensor(
                        out=hb_v[:, j, :], in0=h1s[j % 2][:], in1=b3k[:],
                        op=OP.mult),
                      reads=[f"h1s{j % 2}", b3n], writes=[f"hb{j}"],
                      tag=f"D_hb{b}_{sb_i}_{j}")
                for tt in range(4):
                    ti = sb_i * 4 + tt
                    if shared:
                        E("y", lambda o, ti=ti: o.dma_start(
                            out=x1ld[ti % 2][:],
                            in_=x1_hbm[ti*128:(ti+1)*128, :]),
                          writes=[f"x1ld{ti % 2}"],
                          dma=True, tag=f"D_ldx1{ti}")
                    for h in range(2):
                        bk, bn = P[4 + (2*tt + h) % 2], f"P{4 + (2*tt + h) % 2}"
                        for j in range(NC_K):
                            E("t", lambda o, j=j, tt=tt, h=h, bk=bk, s2=s2: o.matmul(
                                bk[:], hb_v[:, j, tt*128:(tt+1)*128],
                                wF_v[s2][:, j, h*512:(h+1)*512],
                                start=(j == 0), stop=(j == NC_K - 1)),
                              reads=allhb + [f"slot{s2}"], writes=[bn],
                              tag=f"D_y{b}_{sb_i}_{tt}_{h}_{j}")
                        if not shared:
                            E("v", lambda o, ti=ti, h=h, bk=bk, e=e: o.scalar_tensor_tensor(
                                out=y_v[:, ti, h*512:(h+1)*512], in0=bk[:],
                                scalar=ge_all[:, ti*8+e:ti*8+e+1],
                                in1=y_v[:, ti, h*512:(h+1)*512],
                                op0=OP.mult, op1=OP.add),
                              reads=[bn, f"ge{ti}", "y_sb"], writes=["y_sb"],
                              tag=f"D_ye{b}_{ti}_{h}")
                        else:
                            E("v", lambda o, ti=ti, h=h, bk=bk: o.scalar_tensor_tensor(
                                out=outt[ti % 2][:, h*512:(h+1)*512], in0=bk[:],
                                scalar=1.0, in1=y_v[:, ti, h*512:(h+1)*512],
                                op0=OP.mult, op1=OP.add),
                              reads=[bn, "y_sb"], writes=[f"outt{ti % 2}"],
                              tag=f"D_yz{ti}_{h}")
                            E("v", lambda o, ti=ti, h=h: o.tensor_tensor(
                                out=outt[ti % 2][:, h*512:(h+1)*512],
                                in0=outt[ti % 2][:, h*512:(h+1)*512],
                                in1=x1ld[ti % 2][:, h*512:(h+1)*512],
                                op=OP.add),
                              reads=[f"outt{ti % 2}", f"x1ld{ti % 2}"],
                              writes=[f"outt{ti % 2}"], tag=f"D_out{ti}_{h}")
                    if shared:
                        E("s", lambda o, ti=ti: o.dma_start(
                            out=out_hbm[ti*128:(ti+1)*128, :],
                            in_=outt[ti % 2][:]),
                          reads=[f"outt{ti % 2}"], writes=[f"outhbm{ti}"],
                          dma=True, tag=f"D_sto{ti}")
                if sb_i == 0 and b + 1 < len(branches):
                    emit_branch_loads(b + 1)

        pl.barrier()
        if not branches:
            E("v", lambda o: o.memset(outt[0][:], 0.0), writes=["outt0"],
              tag="dbg_z")
            for ti in range(NT):
                E("s", lambda o, ti=ti: o.dma_start(
                    out=out_hbm[ti*128:(ti+1)*128, :], in_=outt[0][:]),
                  reads=["outt0"], dma=True, tag=f"dbg_sto{ti}")
            pl.barrier()
        pl.finalize(nc, es, block)
    return nc


_NC_CACHE = {}


def build_cached(S=4096):
    if S not in _NC_CACHE:
        _NC_CACHE[S] = build_program(S)
    return _NC_CACHE[S]


def make_in_maps(L0_attn_x, L0_attn_Wq, L0_attn_Wk, L0_attn_Wv, L0_attn_Wr,
                 L0_ffn_gate_w, L0_ffn_gate_b, L0_ffn_ew1, L0_ffn_ew2,
                 L0_ffn_ew3, L0_ffn_fc1, L0_ffn_fc2, L0_ffn_fc3):
    x = np.ascontiguousarray(np.asarray(L0_attn_x, dtype=np.float32))
    f32 = lambda a: np.ascontiguousarray(np.asarray(a, dtype=np.float32))
    T2 = lambda a: np.ascontiguousarray(np.swapaxes(f32(a), -1, -2))
    gwt = np.zeros((DM, 8), np.float32)
    gwt[:, 0:6] = T2(L0_ffn_gate_w)
    gb = np.zeros((1, 8), np.float32)
    gb[0, 0:6] = f32(L0_ffn_gate_b)
    shared = {"wq": f32(L0_attn_Wq), "wk": f32(L0_attn_Wk),
              "wv": f32(L0_attn_Wv), "wr": f32(L0_attn_Wr),
              "gwt": gwt, "gb": gb,
              "ew1t": T2(L0_ffn_ew1), "ew3t": T2(L0_ffn_ew3),
              "ew2t": T2(L0_ffn_ew2),
              "fc1t": T2(L0_ffn_fc1), "fc2t": T2(L0_ffn_fc2),
              "fc3t": T2(L0_ffn_fc3),
              "ident": np.eye(128, dtype=ml_dtypes.bfloat16)}
    return [dict(shared, x=np.ascontiguousarray(x[b]))
            for b in range(x.shape[0])]


def kernel(**inputs):
    in_maps = make_in_maps(**inputs)
    nc = build_cached(4096)
    res = run_bass_kernel_spmd(nc, in_maps, core_ids=list(range(len(in_maps))))
    return np.stack([r["out"] for r in res.results], axis=0)


# version 13 (final)
# speedup vs baseline: 2.9449x; 1.0157x over previous
import os
import numpy as np
import ml_dtypes
from contextlib import ExitStack
import concourse.bass as bass
import concourse.mybir as mybir
from concourse.bass_utils import run_bass_kernel_spmd

F32 = mybir.dt.float32
BF16 = mybir.dt.bfloat16
AX = mybir.AxisListType.X
OP = mybir.AluOpType
AF = mybir.ActivationFunctionType

DM = 1024
NC_K = DM // 128
NE = 6
SUB = 512


class Plan:

    ENGINES = ("t", "v", "s", "y", "g")
    NSLOT = 8

    def __init__(self):
        self.steps = {e: [] for e in self.ENGINES}
        self.last_write = {}
        self.readers = {}
        self.dma_count = {e: 0 for e in self.ENGINES}
        self.pin_count = 0
        self.seq = 0

    def emit(self, eng, fn, reads=(), writes=(), dma=False, tag="", extra_deps=(),
             pin=False):
        idx = len(self.steps[eng])
        deps = set(extra_deps)
        for k in reads:
            w = self.last_write.get(k)
            if w is not None:
                deps.add(w)
        for k in writes:
            w = self.last_write.get(k)
            if w is not None:
                deps.add(w)
            last = {}
            for (de, di) in self.readers.get(k, ()):
                if self.steps[de][di]["dma"]:
                    deps.add((de, di))
                elif di > last.get(de, -1):
                    last[de] = di
            deps.update(last.items())
        deps = {(de, di) for (de, di) in deps if not (de == eng and di >= idx)}
        if eng == "t":
            deps = {(de, di) for (de, di) in deps if de != "t"}
        st = {"fn": fn, "deps": sorted(deps), "dma": dma, "tag": tag,
              "seq": self.seq}
        if dma:
            if pin:
                st["dslot"] = ("pin", self.pin_count)
                self.pin_count += 1
            else:
                st["dslot"] = self.dma_count[eng] % self.NSLOT
                self.dma_count[eng] += 1
        self.seq += 1
        self.steps[eng].append(st)
        ev = (eng, idx)
        for k in reads:
            self.readers.setdefault(k, []).append(ev)
        for k in writes:
            self.last_write[k] = ev
            self.readers[k] = []
        return ev

    def barrier(self):
        lasts = []
        for e in self.ENGINES:
            if not self.steps[e]:
                continue
            lasts.append((e, len(self.steps[e]) - 1))
            seen = set()
            for i in range(len(self.steps[e]) - 1, -1, -1):
                st = self.steps[e][i]
                if st["dma"] and st["dslot"] not in seen:
                    seen.add(st["dslot"])
                    lasts.append((e, i))

        def mknop():
            def f(obj):
                return obj.nop()
            return f
        for e in self.ENGINES:
            deps = [d for d in lasts if d[0] != e] +                    [d for d in lasts if d[0] == e and self.steps[e][d[1]]["dma"]]
            deps = [d for d in deps if not (d[0] == e and d[1] >= len(self.steps[e]))]
            self.steps[e].append({"fn": mknop(), "deps": sorted(set(deps)),
                                  "dma": False, "tag": "barrier",
                                  "seq": self.seq})
            self.seq += 1

    def queue_flush(self, eng):
        deps = []
        seen = set()
        for i in range(len(self.steps[eng]) - 1, -1, -1):
            st = self.steps[eng][i]
            if st["dma"] and st["dslot"] not in seen:
                seen.add(st["dslot"])
                deps.append((eng, i))

        def f(obj):
            return obj.nop()
        idx = len(self.steps[eng])
        self.steps[eng].append({"fn": f, "deps": sorted(deps), "dma": False,
                                "tag": f"flush_{eng}", "seq": self.seq})
        self.seq += 1
        return (eng, idx)

    def final_wait(self, eng, deps):
        def f(obj):
            return obj.nop()
        self.steps[eng].append({"fn": f, "deps": sorted(set(deps)),
                                "dma": False, "tag": "final", "seq": self.seq})
        self.seq += 1

    def finalize(self, nc, es, block):
        kseq = int(os.environ.get("KSEQ", "0"))
        if kseq:
            for e in self.ENGINES:
                self.steps[e] = [st for st in self.steps[e]
                                 if st["seq"] < kseq]
            self.last_write = {k: v for k, v in self.last_write.items()
                               if v[1] < len(self.steps[v[0]])}
            self.readers = {}
            self.barrier()
        consumers = {}
        for e in self.ENGINES:
            for i, st in enumerate(self.steps[e]):
                if st["tag"] in ("barrier", "final") or                         st["tag"].startswith("flush"):
                    continue
                for d in st["deps"]:
                    consumers.setdefault(d, []).append((e, i))
        def reduce_consumers(evs):
            keep, last = [], {}
            for (de, di) in evs:
                if self.steps[de][di]["dma"]:
                    keep.append((de, di))
                elif di > last.get(de, -1):
                    last[de] = di
            return keep + [(de, di) for de, di in last.items()]

        for e in self.ENGINES:
            prev_by_slot = {}
            for i, st in enumerate(self.steps[e]):
                if not st["dma"] or isinstance(st["dslot"], tuple):
                    continue
                sl = st["dslot"]
                pv = prev_by_slot.get(sl)
                if pv is not None:
                    add = [pv] + reduce_consumers(consumers.get(pv, []))
                    myseq = st["seq"]
                    extra = []
                    for (de, di) in add:
                        if de == e and di >= i:
                            continue
                        dseq = self.steps[de][di]["seq"]
                        assert dseq < myseq, (
                            "slot reuse forward dep", e, i, st["tag"],
                            de, di, self.steps[de][di]["tag"])
                        extra.append((de, di))
                    st["deps"] = sorted(set(st["deps"]) | set(extra))
                prev_by_slot[sl] = (e, i)
        need_mark = {e: set() for e in self.ENGINES}
        for e in self.ENGINES:
            for st in self.steps[e]:
                for (de, di) in st["deps"]:
                    need_mark[de].add(di)
        sems = {(e, p): es.enter_context(nc.semaphore(f"sem_{e}{p}"))
                for e in self.ENGINES for p in range(2)}
        dsems = {}
        for e in self.ENGINES:
            if self.dma_count[e]:
                n = min(self.NSLOT, self.dma_count[e])
                for sl in range(n):
                    dsems[(e, sl)] = es.enter_context(
                        nc.semaphore(f"dsem_{e}{sl}"))
            for i, st in enumerate(self.steps[e]):
                if st.get("dslot") is not None and isinstance(st["dslot"], tuple):
                    dsems[(e, st["dslot"])] = es.enter_context(
                        nc.semaphore(f"dsem_{e}p{st['dslot'][1]}"))
        ev_sem = {}
        slot_counts = {}
        for e in self.ENGINES:
            mcnt = [0, 0]
            nmark = 0
            for i, st in enumerate(self.steps[e]):
                if st["dma"]:
                    sl = st["dslot"]
                    c = slot_counts.get((e, sl), 0) + 16
                    slot_counts[(e, sl)] = c
                    st["inc_sem"] = dsems[(e, sl)]
                    st["inc"] = 16
                    ev_sem[(e, i)] = (st["inc_sem"], c)
                elif i in need_mark[e]:
                    p = nmark % 2
                    nmark += 1
                    mcnt[p] += 1
                    st["inc_sem"] = sems[(e, p)]
                    st["inc"] = 1
                    st["chain"] = (sems[(e, p)], mcnt[p] - 1)
                    ev_sem[(e, i)] = (sems[(e, p)], mcnt[p])
                else:
                    st["inc"] = 0
                    ev_sem[(e, i)] = None

        def emit_stream(eng, obj):
            water = {}
            for i, st in enumerate(self.steps[eng]):
                waits = {}
                for d in st["deps"]:
                    sv = ev_sem[d]
                    assert sv is not None, (eng, i, d, st["tag"])
                    sem, v = sv
                    if v > waits.get(id(sem), (None, -1))[1]:
                        waits[id(sem)] = (sem, v)
                ch = st.get("chain")
                if ch is not None and ch[1] > 0:
                    if ch[1] > waits.get(id(ch[0]), (None, -1))[1]:
                        waits[id(ch[0])] = ch
                for sem, v in waits.values():
                    if water.get(id(sem), -1) >= v:
                        continue
                    obj.wait_ge(sem, v)
                    water[id(sem)] = v
                ins = st["fn"](obj)
                if st["inc"]:
                    ins.then_inc(st["inc_sem"], st["inc"])

        @block.tensor
        def _(t):
            emit_stream("t", t)

        @block.vector
        def _(v):
            emit_stream("v", v)

        @block.scalar
        def _(s):
            emit_stream("s", s)

        @block.sync
        def _(y):
            emit_stream("y", y)

        @block.gpsimd
        def _(g):
            emit_stream("g", g)


def build_program(S=4096):
    PH = os.environ.get("KPHASES", "ABCD")
    KT = int(os.environ.get("KTILES", "0")) or 10**9
    NT = S // 128
    NSUB = S // SUB
    nc = bass.Bass("TRN2", target_bir_lowering=False)

    x_in = nc.dram_tensor("x", [S, DM], F32, kind="ExternalInput").ap()
    wq_in = nc.dram_tensor("wq", [DM, DM], F32, kind="ExternalInput").ap()
    wk_in = nc.dram_tensor("wk", [DM, DM], F32, kind="ExternalInput").ap()
    wv_in = nc.dram_tensor("wv", [DM, DM], F32, kind="ExternalInput").ap()
    wr_in = nc.dram_tensor("wr", [DM, DM], F32, kind="ExternalInput").ap()
    gwt_in = nc.dram_tensor("gwt", [DM, 8], F32, kind="ExternalInput").ap()
    gb_in = nc.dram_tensor("gb", [1, 8], F32, kind="ExternalInput").ap()
    ew1t_in = nc.dram_tensor("ew1t", [NE, DM, DM], F32, kind="ExternalInput").ap()
    ew3t_in = nc.dram_tensor("ew3t", [NE, DM, DM], F32, kind="ExternalInput").ap()
    ew2t_in = nc.dram_tensor("ew2t", [NE, DM, DM], F32, kind="ExternalInput").ap()
    fc1t_in = nc.dram_tensor("fc1t", [DM, DM], F32, kind="ExternalInput").ap()
    fc2t_in = nc.dram_tensor("fc2t", [DM, DM], F32, kind="ExternalInput").ap()
    fc3t_in = nc.dram_tensor("fc3t", [DM, DM], F32, kind="ExternalInput").ap()
    ident_in = nc.dram_tensor("ident", [128, 128], BF16, kind="ExternalInput").ap()
    out_hbm = nc.dram_tensor("out", [S, DM], F32, kind="ExternalOutput").ap()
    qx_hbm = nc.dram_tensor("qx_cache", [S, DM], F32, kind="Internal").ap()
    x1_hbm = nc.dram_tensor("x1_cache", [S, DM], F32, kind="Internal").ap()
    ln2t_hbm = nc.dram_tensor("ln2t_cache", [NC_K, 128, S], BF16, kind="Internal").ap()
    ln2t_hbm_v = ln2t_hbm.rearrange("c p t -> p c t")

    es = ExitStack()
    es_abc = ExitStack()
    with es:
        sb = lambda n, shp, dt: es.enter_context(nc.sbuf_tensor(n, shp, dt))
        sba = lambda n, shp, dt: es_abc.enter_context(nc.sbuf_tensor(n, shp, dt))
        psa = lambda n, shp, dt: es.enter_context(nc.psum_tensor(n, shp, dt))
        sb2 = lambda n, shp, dt: [sb(n + "0", shp, dt), sb(n + "1", shp, dt)]
        sba2 = lambda n, shp, dt: [sba(n + "0", shp, dt), sba(n + "1", shp, dt)]

        ident = sb("ident_sb", [128, 128], BF16)
        ones_col = sb("ones_col", [128, 1], BF16)
        ones_row = sb("ones_row", [1, 128], BF16)
        bias_b = sb("bias_b", [128, 8], F32)
        gw_bf = sb("gw_bf", [128, NC_K * 8], BF16)
        probs_all = sb("probs_all", [128, NT * 8], F32)
        ge_all = sb("ge_all", [128, NT * 8], F32)
        mx8a = sb2("mx8a", [128, 8], F32)
        mx8b = sb2("mx8b", [128, 8], F32)
        p2buf = sb2("p2buf", [128, 8], F32)
        p3buf = sb2("p3buf", [128, 8], F32)
        fmask = sb2("fmask", [128, 8], F32)
        smask = sb2("smask", [128, 8], F32)
        gcol = sb2("gcol", [128, 4], F32)
        ptmp = sb2("ptmp", [128, 8], F32)
        etmp = sb2("etmp", [128, 8], F32)
        stats = sb2("stats", [128, 12], F32)
        mv = sb2("mv", [128, 2], F32)
        rstd = sb2("rstd", [128, 2], F32)
        nbias = sb2("nbias", [128, 1], F32)
        mcol = sb2("mcol", [128, 4], F32)
        zcol = sb2("zcol", [128, 4], F32)
        gk_f = sb("gk_f", [128, 8], F32)
        ceps = sb("ceps", [128, 1], F32)
        gb_row = sb("gb_row", [1, 8], F32)
        gb_row_bf = sb("gb_row_bf", [1, 8], BF16)

        P = [psa(f"P{i}", [128, 512], F32) for i in range(6)]
        es_a = ExitStack()
        gqP = es_a.enter_context(nc.psum_tensor("gqP", [2, 512], F32))
        es_b = ExitStack()

        block = es.enter_context(nc.Block())

        pl = Plan()
        E = pl.emit

        ln1T = sba("ln1T", [128, NC_K * S], BF16)
        wA = [None] * 4
        wA[0] = sba("wA0", [128, NC_K * DM], BF16)
        wA[2] = sba("wA2", [128, NC_K * DM], BF16)
        wA[3] = sba("wA3", [128, NC_K * DM], BF16)
        xb = [sba(f"xb{i}", [128, DM], F32) for i in range(5)]
        x1t = sba2("x1t", [128, DM], F32)
        ln1_bf = [sba(f"ln1_bf{i}", [128, DM], BF16) for i in range(3)]
        expbuf = sba2("expbuf", [128, DM], BF16)
        wbuf = sba2("wbuf", [128, DM], BF16)
        gq_row_bf = sba("gq_row_bf", [1, DM], BF16)
        gq_bcast = sba("gq_bcast", [128, DM], BF16)
        gkf = sba("gkf", [128, 8], F32)
        tmp2 = sba("tmp2", [2, 512], BF16)
        rZf = sba2("rZf", [128, 1], F32)
        rZc2 = sba2("rZc2", [128, 3], BF16)
        wA[1] = es_b.enter_context(nc.sbuf_tensor("wA1", [128, NC_K * DM], BF16))
        pbuf = [es_b.enter_context(nc.sbuf_tensor(f"pbuf{i}", [128, DM], F32))
                for i in range(2)]

        ln1T_v = ln1T[:].rearrange("p (c t) -> p c t", c=NC_K)
        wA_v = [w[:].rearrange("p (c f) -> p c f", c=NC_K) for w in wA]
        gw_v = gw_bf[:].rearrange("p (c f) -> p c f", c=NC_K)
        E("y", lambda o: o.dma_start(out=ident[:], in_=ident_in),
          writes=["ident"], dma=True, tag="ld_ident", pin=True)
        E("v", lambda o: o.memset(ones_col[:], 1.0), writes=["ones_col"])
        E("v", lambda o: o.memset(ones_row[:], 1.0), writes=["ones_row"])
        E("v", lambda o: o.memset(mx8a[0][:], -3e30), writes=["mx8a0"])
        E("v", lambda o: o.memset(mx8a[1][:], -3e30), writes=["mx8a1"])
        E("v", lambda o: o.memset(mx8b[0][:], -3e30), writes=["mx8b0"])
        E("v", lambda o: o.memset(mx8b[1][:], -3e30), writes=["mx8b1"])
        E("v", lambda o: o.memset(probs_all[:], -2e30), writes=["probs_all"])
        E("v", lambda o: o.memset(ceps[:], 1e-5), writes=["ceps"])
        E("v", lambda o: o.memset(rZc2[0][:], 0.0), writes=["rZc20i"])
        E("v", lambda o: o.memset(rZc2[1][:], 0.0), writes=["rZc21i"])
        E("g", lambda o: o.dma_start(
            out=gw_v[:, :, 0:8], in_=gwt_in.rearrange("(c p) f -> p c f", p=128)),
          writes=["gw_bf"], dma=True, tag="ld_gw", pin=True)
        E("y", lambda o: o.dma_start(out=gb_row[0:1, :], in_=gb_in),
          writes=["bias_row"], dma=True, tag="ld_gb", pin=True)
        E("v", lambda o: o.tensor_copy(out=gb_row_bf[:], in_=gb_row[:]),
          reads=["bias_row"], writes=["gb_row_bf"], tag="gb_cast")
        E("t", lambda o: o.matmul(P[0][:, 0:8], ones_row[:], gb_row_bf[0:1, :],
                                  start=True, stop=True),
          reads=["gb_row_bf", "ones_row"], writes=["P0"], tag="gb_mm")
        E("v", lambda o: o.tensor_copy(out=bias_b[:], in_=P[0][:, 0:8]),
          reads=["P0"], writes=["bias_b"], tag="gb_ev")

        def wload(bufkey, dst, src, tag):
            E("g", lambda o, d=dst, s_=src: o.dma_start(
                out=d, in_=s_.rearrange("(c p) f -> p c f", p=128)),
              writes=[bufkey], dma=True, tag=tag, pin=True)

        wload("wA0", wA[0][:], wq_in, "ld_wq")
        wload("wA1", wA[1][:], wk_in, "ld_wk")
        wload("wA2", wA[2][:], wv_in, "ld_wv")
        wload("wA3", wA[3][:], wr_in, "ld_wr")

        def layernorm(par, src_buf, src_key, dst_buf, dst_key, tg):
            st_, mv_, rs_, nb_ = stats[par], mv[par], rstd[par], nbias[par]
            E("v", lambda o: o.bn_stats(out=st_[:, 0:6], in_=src_buf[:, 0:512]),
              reads=[src_key], writes=[f"stats{par}a"], tag=f"{tg}_bnsa")
            E("v", lambda o: o.bn_stats(out=st_[:, 6:12], in_=src_buf[:, 512:1024]),
              reads=[src_key], writes=[f"stats{par}b"], tag=f"{tg}_bnsb")
            E("v", lambda o: o.bn_aggr(out=mv_[:], in_=st_[:]),
              reads=[f"stats{par}a", f"stats{par}b"], writes=[f"mv{par}"],
              tag=f"{tg}_bna")
            E("s", lambda o: o.activation(out=rs_[:, 0:1], in_=mv_[:, 1:2],
                                          func=AF.Ln, bias=ceps[:], scale=1.0),
              reads=[f"mv{par}", "ceps"], writes=[f"rstd{par}0"], tag=f"{tg}_ln_")
            E("s", lambda o: o.activation(out=rs_[:, 1:2], in_=rs_[:, 0:1],
                                          func=AF.Exp, bias=0.0, scale=-0.5),
              reads=[f"rstd{par}0"], writes=[f"rstd{par}1"], tag=f"{tg}_rex")
            E("v", lambda o: o.tensor_scalar(out=nb_[:], in0=mv_[:, 0:1],
                                             scalar1=rs_[:, 1:2], scalar2=-1.0,
                                             op0=OP.mult, op1=OP.mult),
              reads=[f"mv{par}", f"rstd{par}1"], writes=[f"nbias{par}"],
              tag=f"{tg}_nb")
            E("s", lambda o: o.activation(out=dst_buf[:], in_=src_buf[:],
                                          func=AF.Identity, bias=nb_[:],
                                          scale=rs_[:, 1:2]),
              reads=[src_key, f"nbias{par}", f"rstd{par}1"], writes=[dst_key],
              tag=f"{tg}_ln")

        NTA = min(KT, NT) if 'A' in PH else 0

        def A_LDX(i):
            E("g", lambda o, i=i: o.dma_start(
                out=xb[i % 5][:], in_=x_in[i*128:(i+1)*128, :]),
              writes=[f"xb{i % 5}"], dma=True, tag=f"A_ldx{i}")

        def A_LN(i):
            layernorm(i % 2, xb[i % 5], f"xb{i % 5}", ln1_bf[i % 3],
                      f"ln1_bf{i % 3}", f"A{i}")

        def A_TRD(i):
            for q, eng in ((0, "y"), (1, "s")):
                E(eng, lambda o, i=i, q=q: o.dma_start(
                    out=ln1T_v[:, q*4:(q+1)*4, i*128:(i+1)*128],
                    in_=ln1_bf[i % 3][:, q*512:(q+1)*512], transpose=True),
                  reads=[f"ln1_bf{i % 3}"], writes=[f"ln1T{i}h{q}"], dma=True,
                  tag=f"A_trd{i}_{q}")

        def MM16(i, wkey, wv, ph, lnread=True, extra=()):
            pslot = (i % 3) * 2
            rd = [f"ln1T{i}h0", f"ln1T{i}h1", wkey] if lnread else [wkey]
            for h in range(2):
                bnk, bk = P[pslot + h], f"P{pslot + h}"
                for c in range(NC_K):
                    E("t", lambda o, h=h, c=c, i=i, bnk=bnk, wv=wv: o.matmul(
                        bnk[:], ln1T_v[:, c, i*128:(i+1)*128],
                        wv[:, c, h*512:(h+1)*512],
                        start=(c == 0), stop=(c == NC_K - 1)),
                      reads=rd, writes=[bk],
                      extra_deps=extra, tag=f"{ph}_mm{i}_{h}_{c}")

        def A_P1(i):
            par, pslot = i % 2, (i % 3) * 2
            mc = mcol[par]
            E("v", lambda o, b=P[pslot], mc=mc: o.reduce_max(
                out=mc[:, 0:1], in_=b[:], axis=AX),
              reads=[f"P{pslot}"], writes=[f"m0_{par}"], tag=f"A_m0{i}")
            E("v", lambda o, b=P[pslot+1], mc=mc: o.reduce_max(
                out=mc[:, 1:2], in_=b[:], axis=AX),
              reads=[f"P{pslot+1}"], writes=[f"m1_{par}"], tag=f"A_m1{i}")
            E("v", lambda o, mc=mc: o.tensor_scalar(
                out=mc[:, 3:4], in0=mc[:, 0:1], scalar1=mc[:, 1:2],
                scalar2=-1.0 / 32.0, op0=OP.max, op1=OP.mult),
              reads=[f"m0_{par}", f"m1_{par}"], writes=[f"negM{par}"],
              tag=f"A_nM{i}")
            for h in range(2):
                E("v", lambda o, h=h, b=P[pslot+h], par=par, i=i: o.tensor_tensor(
                    out=x1t[par][:, h*512:(h+1)*512], in0=b[:],
                    in1=xb[i % 5][:, h*512:(h+1)*512], op=OP.add),
                  reads=[f"P{pslot+h}", f"xb{i % 5}"], writes=[f"x1t{par}"],
                  tag=f"A_qx{i}_{h}")
            E("s", lambda o, i=i, par=par: o.dma_start(
                out=qx_hbm[i*128:(i+1)*128, :], in_=x1t[par][:]),
              reads=[f"x1t{par}"], writes=[f"qxhbm{i}"], dma=True,
              tag=f"A_stq{i}")

        def A_EXP(i):
            par, pslot = i % 2, (i % 3) * 2
            mc, zc = mcol[par], zcol[par]
            for h in range(2):
                E("s", lambda o, h=h, b=P[pslot+h], par=par, mc=mc, zc=zc:
                  o.activation(
                    out=expbuf[par][:, h*512:(h+1)*512], in_=b[:], func=AF.Exp,
                    bias=mc[:, 3:4], scale=1.0/32.0, accum_out=zc[:, h:h+1]),
                  reads=[f"P{pslot+h}", f"negM{par}"],
                  writes=[f"exp{par}_{h}", f"z{par}{h}"], tag=f"A_e{i}_{h}")

        def A_P2(i):
            par, pslot = i % 2, (i % 3) * 2
            zc = zcol[par]
            E("v", lambda o, zc=zc: o.tensor_tensor(
                out=zc[:, 2:3], in0=zc[:, 0:1], in1=zc[:, 1:2], op=OP.add),
              reads=[f"z{par}0", f"z{par}1"], writes=[f"Z{par}"], tag=f"A_Z{i}")
            E("v", lambda o, zc=zc, par=par: o.reciprocal(
                out=rZf[par][:], in_=zc[:, 2:3]),
              reads=[f"Z{par}"], writes=[f"rZf{par}"], tag=f"A_rZ{i}")
            E("v", lambda o, par=par: o.tensor_copy(
                out=rZc2[par][:, 0:1], in_=rZf[par][:]),
              reads=[f"rZf{par}"], writes=[f"rZc{par}"], tag=f"A_rc{i}")
            E("v", lambda o, par=par: o.tensor_copy(
                out=rZc2[par][:, 2:3], in_=rZf[par][:]),
              reads=[f"rZf{par}"], writes=[f"rZc2{par}"], tag=f"A_rc2{i}")
            for h in range(2):
                E("v", lambda o, h=h, b=P[pslot+h], par=par: o.tensor_tensor(
                    out=wbuf[par][:, h*512:(h+1)*512], in0=b[:],
                    in1=expbuf[par][:, h*512:(h+1)*512], op=OP.mult),
                  reads=[f"P{pslot+h}", f"exp{par}_{h}"], writes=[f"w{par}_{h}"],
                  tag=f"A_w{i}_{h}")

        def A_GQ(i):
            par = i % 2
            for h in range(2):
                E("t", lambda o, h=h, par=par, i=i: o.matmul(
                    gqP[0:2, :], rZc2[par][:, h:h+2],
                    wbuf[par][:, h*512:(h+1)*512],
                    start=(i == 0 and h == 0),
                    stop=(i == NTA - 1 and h == 1)),
                  reads=[f"rZc{par}", f"rZc2{par}", f"w{par}_{h}"],
                  writes=["gqP"], tag=f"A_gq{i}_{h}")

        if NTA:
            for j in range(min(3, NTA)):
                A_LDX(j)
            for j in range(min(2, NTA)):
                A_LN(j)
            A_TRD(0)
            for i in range(NTA):
                if i + 2 < NTA:
                    A_LN(i + 2)
                A_MM_ = MM16(i, "wA0", wA_v[0], "A")
                if i >= 3:
                    A_GQ(i - 3)
                if i >= 1:
                    A_P1(i - 1)
                    A_EXP(i - 1)
                if i >= 2:
                    A_P2(i - 2)
                if i + 3 < NTA:
                    A_LDX(i + 3)
                if i + 1 < NTA:
                    A_TRD(i + 1)
            A_P1(NTA - 1)
            A_EXP(NTA - 1)
            A_P2(NTA - 2)
            A_P2(NTA - 1)
            for j in range(max(0, NTA - 3), NTA):
                A_GQ(j)

        fold_ev = E("v", lambda o: o.tensor_copy(out=tmp2[:], in_=gqP[0:2, :]),
                    reads=["gqP"], writes=["tmp2"], tag="gq_ev")
        E("y", lambda o: o.dma_start(out=gq_row_bf[0:1, 0:512],
                                     in_=tmp2[0:1, :]),
          reads=["tmp2"], writes=["gq_row_bf"], dma=True, tag="gq_d0")
        E("y", lambda o: o.dma_start(out=gq_row_bf[0:1, 512:1024],
                                     in_=tmp2[1:2, :]),
          reads=["tmp2"], writes=["gq_row_bf"], dma=True, tag="gq_d1")
        for h in range(2):
            E("t", lambda o, h=h: o.matmul(P[h][:], ones_row[:],
                                           gq_row_bf[0:1, h*512:(h+1)*512],
                                           start=True, stop=True),
              reads=["gq_row_bf", "ones_row"], writes=[f"P{h}"],
              tag=f"gqb_mm{h}")
            E("v", lambda o, h=h: o.tensor_copy(
                out=gq_bcast[:, h*512:(h+1)*512], in_=P[h][:]),
              reads=[f"P{h}"], writes=["gqb"], tag=f"gqb_ev{h}")
        es_a.close()
        gkP = es_b.enter_context(nc.psum_tensor("gkP", [128, 8], F32))
        qx_flush = pl.queue_flush("s")
        ln1T_done = pl.queue_flush("y")

        def B_GK(i, extra=()):
            par = i % 2
            for c in range(NC_K):
                E("t", lambda o, c=c, par=par, i=i: o.matmul(
                    gkP[:, c:c+1], wbuf[par][:, c*128:(c+1)*128],
                    rZc2[par][:, 0:1],
                    start=(i == 0 and c == 0),
                    stop=(i == NT - 1 and c == NC_K - 1)),
                  reads=[f"w{par}_{c // 4}", f"rZc{par}"], writes=["gkP"],
                  extra_deps=(extra if i == 0 else ()), tag=f"B_gk{i}_{c}")

        def B_P1(i):
            par, pslot = i % 2, (i % 3) * 2
            pb, mc = pbuf[par], mcol[par]
            for h in range(2):
                E("v", lambda o, h=h, b=P[pslot+h], pb=pb: o.tensor_tensor(
                    out=pb[:, h*512:(h+1)*512], in0=b[:],
                    in1=gq_bcast[:, h*512:(h+1)*512], op=OP.mult),
                  reads=[f"P{pslot+h}", "gqb"], writes=[f"pb{par}_{h}"],
                  tag=f"B_p{i}_{h}")
            E("v", lambda o, pb=pb, mc=mc: o.reduce_max(
                out=mc[:, 2:3], in_=pb[:], axis=AX),
              reads=[f"pb{par}_0", f"pb{par}_1"], writes=[f"M{par}"],
              tag=f"B_M{i}")
            E("v", lambda o, mc=mc: o.tensor_scalar(
                out=mc[:, 3:4], in0=mc[:, 2:3], scalar1=-1.0/32.0,
                scalar2=None, op0=OP.mult),
              reads=[f"M{par}"], writes=[f"negM{par}"], tag=f"B_nM{i}")

        def B_EXP(i):
            par = i % 2
            pb, mc, zc = pbuf[par], mcol[par], zcol[par]
            E("s", lambda o, pb=pb, par=par, mc=mc, zc=zc: o.activation(
                out=expbuf[par][:], in_=pb[:], func=AF.Exp,
                bias=mc[:, 3:4], scale=1.0/32.0, accum_out=zc[:, 2:3]),
              reads=[f"pb{par}_0", f"pb{par}_1", f"negM{par}"],
              writes=[f"exp{par}_0", f"exp{par}_1", f"Z{par}"], tag=f"B_e{i}")

        def B_P2(i):
            par = i % 2
            pb, zc = pbuf[par], zcol[par]
            E("v", lambda o, zc=zc, par=par: o.reciprocal(
                out=rZf[par][:], in_=zc[:, 2:3]),
              reads=[f"Z{par}"], writes=[f"rZf{par}"], tag=f"B_rZ{i}")
            E("v", lambda o, par=par: o.tensor_copy(
                out=rZc2[par][:, 0:1], in_=rZf[par][:]),
              reads=[f"rZf{par}"], writes=[f"rZc{par}"], tag=f"B_rc{i}")
            E("v", lambda o, pb=pb, par=par: o.tensor_tensor(
                out=wbuf[par][:], in0=pb[:], in1=expbuf[par][:], op=OP.mult),
              reads=[f"pb{par}_0", f"pb{par}_1", f"exp{par}_0", f"exp{par}_1"],
              writes=[f"w{par}_0", f"w{par}_1"], tag=f"B_w{i}")

        def B_WVTD(c_):
            E("y", lambda o, c_=c_: o.dma_start(
                out=wA_v[0][:, :, c_*128:(c_+1)*128],
                in_=wA[2][:][:, c_*DM:(c_+1)*DM], transpose=True),
              reads=["wA2"], writes=["wA0", "wvT"], dma=True,
              tag=f"B_wvtd{c_}")

        KB = os.environ.get("KB", "gwp")
        if 'B' in PH:
            for i in range(NT):
                MM16(i, "wA1", wA_v[1], "B", lnread=False,
                     extra=[ln1T_done])
                if i >= 3 and 'g' in KB:
                    B_GK(i - 3, extra=[fold_ev])
                if i >= 1 and 'p' in KB:
                    B_P1(i - 1)
                    B_EXP(i - 1)
                if i >= 2 and 'p' in KB:
                    B_P2(i - 2)
                if i % 4 == 0 and 'w' in KB:
                    B_WVTD(i // 4)
            if 'p' in KB:
                B_P1(NT - 1)
                B_EXP(NT - 1)
                B_P2(NT - 2)
                B_P2(NT - 1)
            if 'g' in KB:
                for j in range(NT - 3, NT):
                    B_GK(j)

        E("v", lambda o: o.tensor_copy(out=gkf[:], in_=gkP[:, 0:8]),
          reads=["gkP"], writes=["gkf"], tag="gk_ev")
        for c in range(NC_K):
            if c % 2 == 0:
                E("s", lambda o, c=c: o.activation(
                    out=wA_v[2][:, c, :], in_=wA_v[3][:, c, :],
                    func=AF.Identity, bias=0.0, scale=gkf[:, c:c+1]),
                  reads=["wA3", "gkf"], writes=["wA2", "wp"], tag=f"wp_s{c}")
            else:
                E("v", lambda o, c=c: o.tensor_scalar(
                    out=wA_v[2][:, c, :], in0=wA_v[3][:, c, :],
                    scalar1=gkf[:, c:c+1], scalar2=None, op0=OP.mult),
                  reads=["wA3", "gkf"], writes=["wA2", "wp"], tag=f"wp_v{c}")
        for ab in range(NC_K):
            for hb in range(2):
                slot = (2 * ab + hb) % 6
                bnk, bk = P[slot], f"P{slot}"
                for cd in range(NC_K):
                    E("t", lambda o, ab=ab, hb=hb, cd=cd, bnk=bnk: o.matmul(
                        bnk[:], wA_v[0][:, cd, ab*128:(ab+1)*128],
                        wA_v[2][:, cd, hb*512:(hb+1)*512],
                        start=(cd == 0), stop=(cd == NC_K - 1)),
                      reads=["wvT", "wp"], writes=[bk],
                      tag=f"wvr_{ab}_{hb}_{cd}")
                E("v", lambda o, ab=ab, hb=hb, bnk=bnk: o.tensor_copy(
                    out=wA_v[3][:, ab, hb*512:(hb+1)*512], in_=bnk[:]),
                  reads=[bk], writes=["wA3", "wvr"], tag=f"wvre_{ab}_{hb}")

        es_b.close()
        pg = es_abc.enter_context(nc.psum_tensor("pg", [128, 8], F32))
        qxld = [es_abc.enter_context(nc.sbuf_tensor(f"qxld{i}", [128, DM], F32))
                for i in range(2)]
        ln2_bf = [es_abc.enter_context(
            nc.sbuf_tensor(f"ln2_bf{i}", [128, DM], BF16)) for i in range(2)]
        ln2st = es_abc.enter_context(
            nc.sbuf_tensor("ln2st", [128, NC_K * 8 * 128], BF16))
        ln2st_v = ln2st[:].rearrange("p (c t) -> p c t", c=NC_K)
        pl.barrier()

        def C_LD(i):
            E("g", lambda o, i=i: o.dma_start(
                out=qxld[i % 2][:], in_=qx_hbm[i*128:(i+1)*128, :]),
              writes=[f"qxld{i % 2}"], dma=True, extra_deps=[qx_flush],
              tag=f"C_ldqx{i}")

        def C_T1(i):
            par, pslot = i % 2, (i % 3) * 2
            for h in range(2):
                E("v", lambda o, h=h, par=par, b=P[pslot+h], i=i: o.tensor_tensor(
                    out=x1t[par][:, h*512:(h+1)*512], in0=b[:],
                    in1=qxld[i % 2][:, h*512:(h+1)*512], op=OP.add),
                  reads=[f"P{pslot+h}", f"qxld{i % 2}"], writes=[f"x1t{par}"],
                  tag=f"C_x1{i}_{h}")
            E("s", lambda o, i=i, par=par: o.dma_start(
                out=x1_hbm[i*128:(i+1)*128, :], in_=x1t[par][:]),
              reads=[f"x1t{par}"], writes=[f"x1hbm{i}"], dma=True,
              tag=f"C_stx1{i}")
            layernorm(par, x1t[par], f"x1t{par}", ln2_bf[par], f"ln2_bf{par}",
                      f"C{i}")

        def C_TRD(i):
            for q, eng in ((0, "y"), (1, "s")):
                E(eng, lambda o, i=i, q=q: o.dma_start(
                    out=ln2st_v[:, q*4:(q+1)*4, (i % 8)*128:(i % 8 + 1)*128],
                    in_=ln2_bf[i % 2][:, q*512:(q+1)*512], transpose=True),
                  reads=[f"ln2_bf{i % 2}"], writes=[f"ln2st{i % 8}h{q}"],
                  dma=True, tag=f"C_trd{i}_{q}")

        def C_T2(i):
            t8 = i % 8
            par = i % 2
            for c in range(NC_K):
                E("t", lambda o, c=c, t8=t8: o.matmul(
                    pg[:, 0:6], ln2st_v[:, c, t8*128:(t8+1)*128],
                    gw_v[:, c, 0:6], start=(c == 0), stop=(c == NC_K - 1)),
                  reads=[f"ln2st{t8}h0", f"ln2st{t8}h1", "gw_bf"],
                  writes=["pg"], tag=f"C_gmm{i}_{c}")
            E("v", lambda o, par=par: o.tensor_tensor(
                out=ptmp[par][:, 0:6], in0=pg[:, 0:6],
                in1=bias_b[:, 0:6], op=OP.add),
              reads=["pg", "bias_b"], writes=[f"ptmp{par}"], tag=f"C_gb{i}")
            E("s", lambda o, par=par: o.activation(
                out=etmp[par][:, 0:6], in_=ptmp[par][:, 0:6],
                func=AF.Exp, bias=0.0, scale=-1.0),
              reads=[f"ptmp{par}"], writes=[f"etmp{par}"], tag=f"C_sg1{i}")
            E("v", lambda o, par=par: o.tensor_scalar(
                out=etmp[par][:, 0:6], in0=etmp[par][:, 0:6], scalar1=1.0,
                scalar2=None, op0=OP.add),
              reads=[f"etmp{par}"], writes=[f"etmp{par}"], tag=f"C_sg2{i}")
            E("v", lambda o, ti=i, par=par: o.reciprocal(
                out=probs_all[:, ti*8:ti*8+6], in_=etmp[par][:, 0:6]),
              reads=[f"etmp{par}"], writes=[f"probs{i}"], tag=f"C_sg3{i}")
            ti = i
            o8 = ti * 8
            Pt = probs_all[:, o8:o8+8]
            ma, mb = mx8a[par], mx8b[par]
            p2_, p3_ = p2buf[par], p3buf[par]
            fm, sm, gc = fmask[par], smask[par], gcol[par]
            E("v", lambda o, Pt=Pt, ma=ma: o.reduce_max(
                out=ma[:, 0:1], in_=Pt, axis=AX),
              reads=[f"probs{ti}", "probs_all"], writes=[f"mx8a_v{par}"],
              tag=f"G_m1{ti}")
            E("v", lambda o, Pt=Pt, ma=ma, p2_=p2_: o.match_replace(
                out=p2_[:], in_to_replace=ma[:], in_values=Pt,
                imm_value=-2e30),
              reads=[f"probs{ti}", "probs_all", f"mx8a{par}",
                     f"mx8a_v{par}"],
              writes=[f"p2buf{par}"], tag=f"G_mr1{ti}")
            E("v", lambda o, mb=mb, p2_=p2_: o.reduce_max(
                out=mb[:, 0:1], in_=p2_[:], axis=AX),
              reads=[f"p2buf{par}"], writes=[f"mx8b_v{par}"],
              tag=f"G_m2{ti}")
            E("v", lambda o, Pt=Pt, fm=fm, p2_=p2_: o.tensor_tensor(
                out=fm[:], in0=Pt, in1=p2_[:], op=OP.not_equal),
              reads=[f"probs{ti}", "probs_all", f"p2buf{par}"],
              writes=[f"fmask{par}"], tag=f"G_f{ti}")
            E("v", lambda o, mb=mb, p2_=p2_, p3_=p3_: o.match_replace(
                out=p3_[:], in_to_replace=mb[:], in_values=p2_[:],
                imm_value=-2e30),
              reads=[f"p2buf{par}", f"mx8b{par}", f"mx8b_v{par}"],
              writes=[f"p3buf{par}"], tag=f"G_mr2{ti}")
            E("v", lambda o, sm=sm, p2_=p2_, p3_=p3_: o.tensor_tensor(
                out=sm[:], in0=p2_[:], in1=p3_[:], op=OP.not_equal),
              reads=[f"p2buf{par}", f"p3buf{par}"], writes=[f"smask{par}"],
              tag=f"G_s{ti}")
            E("v", lambda o, gc=gc, ma=ma, mb=mb: o.scalar_tensor_tensor(
                out=gc[:, 0:1], in0=ma[:, 0:1], scalar=1e-8, in1=mb[:, 0:1],
                op0=OP.add, op1=OP.add),
              reads=[f"mx8a_v{par}", f"mx8b_v{par}"], writes=[f"den{par}"],
              tag=f"G_d{ti}")
            E("v", lambda o, gc=gc: o.reciprocal(out=gc[:, 2:3],
                                                 in_=gc[:, 0:1]),
              reads=[f"den{par}"], writes=[f"rden{par}"], tag=f"G_rd{ti}")
            E("v", lambda o, gc=gc, ma=ma: o.tensor_scalar(
                out=gc[:, 3:4], in0=ma[:, 0:1], scalar1=gc[:, 2:3],
                scalar2=None, op0=OP.mult),
              reads=[f"mx8a_v{par}", f"rden{par}"], writes=[f"gm{par}"],
              tag=f"G_g1{ti}")
            E("v", lambda o, o8=o8, fm=fm, gc=gc: o.tensor_scalar(
                out=ge_all[:, o8:o8+8], in0=fm[:], scalar1=gc[:, 3:4],
                scalar2=None, op0=OP.mult),
              reads=[f"fmask{par}", f"gm{par}"], writes=[f"ge{ti}"],
              tag=f"G_ge1{ti}")
            E("v", lambda o, gc=gc, mb=mb: o.tensor_scalar(
                out=gc[:, 3:4], in0=mb[:, 0:1], scalar1=gc[:, 2:3],
                scalar2=None, op0=OP.mult),
              reads=[f"mx8b_v{par}", f"rden{par}"], writes=[f"gm{par}"],
              tag=f"G_g2{ti}")
            E("v", lambda o, o8=o8, sm=sm, gc=gc: o.scalar_tensor_tensor(
                out=ge_all[:, o8:o8+8], in0=sm[:], scalar=gc[:, 3:4],
                in1=ge_all[:, o8:o8+8], op0=OP.mult, op1=OP.add),
              reads=[f"smask{par}", f"gm{par}", f"ge{ti}"],
              writes=[f"ge{ti}"], tag=f"G_ge2{ti}")
            if i % 4 == 3:
                ch = i // 4
                E("g", lambda o, ch=ch: o.dma_start(
                    out=ln2t_hbm_v[:, :, ch*SUB:(ch+1)*SUB],
                    in_=ln2st_v[:, :, (ch % 2)*SUB:(ch % 2 + 1)*SUB]),
                  reads=[f"ln2st{q}h{hh}"
                         for q in range((ch % 2)*4, (ch % 2)*4 + 4)
                         for hh in range(2)],
                  writes=[f"ln2thbm{ch}"], dma=True, tag=f"C_stln2{ch}",
                  pin=True)

        if 'C' in PH:
            C_LD(0)
            for i in range(NT):
                MM16(i, "wvr", wA_v[3], "C", lnread=False,
                     extra=[ln1T_done])
                if i >= 3:
                    C_T2(i - 3)
                if i >= 2:
                    C_TRD(i - 2)
                if i >= 1:
                    C_T1(i - 1)
                if i + 1 < NT:
                    C_LD(i + 1)
            C_TRD(NT - 2)
            C_T1(NT - 1)
            C_TRD(NT - 1)
            for j in range(NT - 3, NT):
                C_T2(j)
        es_abc.close()
        pl.barrier()

        y_sb = sb("y_sb", [128, NT * DM], BF16)
        wF = [sb(f"wF{i}", [128, NC_K * DM], BF16) for i in range(4)]
        slab = [sb(f"slab{i}", [128, NC_K * SUB], BF16) for i in range(2)]
        h1s = [sb(f"h1s{i}", [128, SUB], F32) for i in range(2)]
        hb = sb("hb", [128, NC_K * SUB], BF16)
        x1ld = [sb(f"x1ld{i}", [128, DM], F32) for i in range(2)]
        outt = [sb(f"outt{i}", [128, DM], F32) for i in range(2)]
        wF_v = [w[:].rearrange("p (c f) -> p c f", c=NC_K) for w in wF]
        slab_v = [s_[:].rearrange("p (c t) -> p c t", c=NC_K) for s_ in slab]
        hb_v = hb[:].rearrange("p (c t) -> p c t", c=NC_K)
        y_v = y_sb[:].rearrange("p (n f) -> p n f", n=NT)

        E("v", lambda o: o.memset(y_sb[:], 0.0), writes=["y_sb"], tag="D_zy")

        branches = [(ew1t_in[e], ew3t_in[e], ew2t_in[e], e) for e in range(NE)]
        branches.append((fc1t_in, fc2t_in, fc3t_in, None))
        if "D" not in PH:
            branches = branches[-1:] if "d" in PH else []

        wslot = {}

        def emit_branch_loads(b):
            w1, w3, w2, _e = branches[b]
            for j, wsrc in enumerate((w1, w3, w2)):
                slot = (3 * b + j) % 4
                wslot[(b, j)] = slot
                E("g", lambda o, slot=slot, s_=wsrc: o.dma_start(
                    out=wF[slot][:], in_=s_.rearrange("(c p) f -> p c f", p=128)),
                  writes=[f"slot{slot}"], dma=True, tag=f"ld_w{b}_{j}")

        if branches:
            emit_branch_loads(0)

        allhb = [f"hb{j}" for j in range(NC_K)]
        for b, (w1, w3, w2, e) in enumerate(branches):
            shared = e is None
            s1, s3, s2 = wslot[(b, 0)], wslot[(b, 1)], wslot[(b, 2)]
            for sb_i in range(NSUB):
                sl, slv = slab[sb_i % 2], slab_v[sb_i % 2]
                slk = f"slab{sb_i % 2}"
                E("y", lambda o, sb_i=sb_i, sl=sl: o.dma_start(
                    out=sl[:], in_=ln2t_hbm_v[:, :, sb_i*SUB:(sb_i+1)*SUB]),
                  writes=[slk], dma=True, tag=f"D_lds{b}_{sb_i}")
                for j in range(NC_K):
                    b1k, b1n = P[j % 2], f"P{j % 2}"
                    b3k, b3n = P[2 + j % 2], f"P{2 + j % 2}"
                    for c in range(NC_K):
                        E("t", lambda o, j=j, c=c, b1k=b1k, slv=slv, s1=s1: o.matmul(
                            b1k[:], wF_v[s1][:, c, j*128:(j+1)*128],
                            slv[:, c, :],
                            start=(c == 0), stop=(c == NC_K - 1)),
                          reads=[f"slot{s1}", slk], writes=[b1n],
                          tag=f"D_h1_{b}_{sb_i}_{j}_{c}")
                    for c in range(NC_K):
                        E("t", lambda o, j=j, c=c, b3k=b3k, slv=slv, s3=s3: o.matmul(
                            b3k[:], wF_v[s3][:, c, j*128:(j+1)*128],
                            slv[:, c, :],
                            start=(c == 0), stop=(c == NC_K - 1)),
                          reads=[f"slot{s3}", slk], writes=[b3n],
                          tag=f"D_h3_{b}_{sb_i}_{j}_{c}")
                    E("s", lambda o, j=j, b1k=b1k: o.activation(
                        out=h1s[j % 2][:], in_=b1k[:], func=AF.Silu),
                      reads=[b1n], writes=[f"h1s{j % 2}"],
                      tag=f"D_si{b}_{sb_i}_{j}")
                    E("v", lambda o, j=j, b3k=b3k: o.tensor_tensor(
                        out=hb_v[:, j, :], in0=h1s[j % 2][:], in1=b3k[:],
                        op=OP.mult),
                      reads=[f"h1s{j % 2}", b3n], writes=[f"hb{j}"],
                      tag=f"D_hb{b}_{sb_i}_{j}")
                for tt in range(4):
                    ti = sb_i * 4 + tt
                    if shared:
                        E("y", lambda o, ti=ti: o.dma_start(
                            out=x1ld[ti % 2][:],
                            in_=x1_hbm[ti*128:(ti+1)*128, :]),
                          writes=[f"x1ld{ti % 2}"],
                          dma=True, tag=f"D_ldx1{ti}")
                    for h in range(2):
                        bk, bn = P[4 + (2*tt + h) % 2], f"P{4 + (2*tt + h) % 2}"
                        for j in range(NC_K):
                            E("t", lambda o, j=j, tt=tt, h=h, bk=bk, s2=s2: o.matmul(
                                bk[:], hb_v[:, j, tt*128:(tt+1)*128],
                                wF_v[s2][:, j, h*512:(h+1)*512],
                                start=(j == 0), stop=(j == NC_K - 1)),
                              reads=allhb + [f"slot{s2}"], writes=[bn],
                              tag=f"D_y{b}_{sb_i}_{tt}_{h}_{j}")
                        if not shared:
                            E("v", lambda o, ti=ti, h=h, bk=bk, e=e: o.scalar_tensor_tensor(
                                out=y_v[:, ti, h*512:(h+1)*512], in0=bk[:],
                                scalar=ge_all[:, ti*8+e:ti*8+e+1],
                                in1=y_v[:, ti, h*512:(h+1)*512],
                                op0=OP.mult, op1=OP.add),
                              reads=[bn, f"ge{ti}", "y_sb"], writes=["y_sb"],
                              tag=f"D_ye{b}_{ti}_{h}")
                        else:
                            E("v", lambda o, ti=ti, h=h, bk=bk: o.scalar_tensor_tensor(
                                out=outt[ti % 2][:, h*512:(h+1)*512], in0=bk[:],
                                scalar=1.0, in1=y_v[:, ti, h*512:(h+1)*512],
                                op0=OP.mult, op1=OP.add),
                              reads=[bn, "y_sb"], writes=[f"outt{ti % 2}"],
                              tag=f"D_yz{ti}_{h}")
                            E("v", lambda o, ti=ti, h=h: o.tensor_tensor(
                                out=outt[ti % 2][:, h*512:(h+1)*512],
                                in0=outt[ti % 2][:, h*512:(h+1)*512],
                                in1=x1ld[ti % 2][:, h*512:(h+1)*512],
                                op=OP.add),
                              reads=[f"outt{ti % 2}", f"x1ld{ti % 2}"],
                              writes=[f"outt{ti % 2}"], tag=f"D_out{ti}_{h}")
                    if shared:
                        E("s", lambda o, ti=ti: o.dma_start(
                            out=out_hbm[ti*128:(ti+1)*128, :],
                            in_=outt[ti % 2][:]),
                          reads=[f"outt{ti % 2}"], writes=[f"outhbm{ti}"],
                          dma=True, tag=f"D_sto{ti}")
                if sb_i == 0 and b + 1 < len(branches):
                    emit_branch_loads(b + 1)

        pl.barrier()
        if not branches:
            E("v", lambda o: o.memset(outt[0][:], 0.0), writes=["outt0"],
              tag="dbg_z")
            for ti in range(NT):
                E("s", lambda o, ti=ti: o.dma_start(
                    out=out_hbm[ti*128:(ti+1)*128, :], in_=outt[0][:]),
                  reads=["outt0"], dma=True, tag=f"dbg_sto{ti}")
            pl.barrier()
        pl.finalize(nc, es, block)
    return nc


_NC_CACHE = {}


def build_cached(S=4096):
    if S not in _NC_CACHE:
        _NC_CACHE[S] = build_program(S)
    return _NC_CACHE[S]


def make_in_maps(L0_attn_x, L0_attn_Wq, L0_attn_Wk, L0_attn_Wv, L0_attn_Wr,
                 L0_ffn_gate_w, L0_ffn_gate_b, L0_ffn_ew1, L0_ffn_ew2,
                 L0_ffn_ew3, L0_ffn_fc1, L0_ffn_fc2, L0_ffn_fc3):
    x = np.ascontiguousarray(np.asarray(L0_attn_x, dtype=np.float32))
    f32 = lambda a: np.ascontiguousarray(np.asarray(a, dtype=np.float32))
    T2 = lambda a: np.ascontiguousarray(np.swapaxes(f32(a), -1, -2))
    gwt = np.zeros((DM, 8), np.float32)
    gwt[:, 0:6] = T2(L0_ffn_gate_w)
    gb = np.zeros((1, 8), np.float32)
    gb[0, 0:6] = f32(L0_ffn_gate_b)
    shared = {"wq": f32(L0_attn_Wq), "wk": f32(L0_attn_Wk),
              "wv": f32(L0_attn_Wv), "wr": f32(L0_attn_Wr),
              "gwt": gwt, "gb": gb,
              "ew1t": T2(L0_ffn_ew1), "ew3t": T2(L0_ffn_ew3),
              "ew2t": T2(L0_ffn_ew2),
              "fc1t": T2(L0_ffn_fc1), "fc2t": T2(L0_ffn_fc2),
              "fc3t": T2(L0_ffn_fc3),
              "ident": np.eye(128, dtype=ml_dtypes.bfloat16)}
    return [dict(shared, x=np.ascontiguousarray(x[b]))
            for b in range(x.shape[0])]


def kernel(**inputs):
    in_maps = make_in_maps(**inputs)
    nc = build_cached(4096)
    res = run_bass_kernel_spmd(nc, in_maps, core_ids=list(range(len(in_maps))))
    return np.stack([r["out"] for r in res.results], axis=0)
